# version 15
# baseline (speedup 1.0000x reference)
import math
import numpy as np
import ml_dtypes
import concourse.bass as bass
import concourse.mybir as mybir
from concourse.bass_utils import run_bass_kernel_spmd
from concourse.alu_op_type import AluOpType as OP

F32 = mybir.dt.float32
F32R = mybir.dt.float32r
BF16 = mybir.dt.bfloat16
AF = mybir.ActivationFunctionType
AXX = mybir.AxisListType.X

D = 2048
S = 2048
TI = 1024
DC = 16
NSYN = 8192
KSIM = 204
KFFN = 819
LN_EPS = 1e-5
MAGIC = 12582912.0
NSEC = 6
IC = 256
NIC = TI // IC
TOKB = 256
NTOKB = TI // TOKB
Z0_SIM = 1.2872
Z0_FFN = 1.2816
ISQ2 = 1.0 / math.sqrt(2.0)
BIGM = 1.0e7


class Emit:
    ENGS = ("sync", "vector", "scalar", "tensor", "gpsimd")

    def __init__(self):
        self.q = {e: [] for e in self.ENGS}
        self.cnt = {}
        self.handles = {}

    def sem(self, name):
        self.cnt.setdefault(name, 0)

    def op(self, eng, fn, inc=None, waits=()):
        waits = tuple((s, v) for (s, v) in waits if v > 0)

        def cl(e, fn=fn, inc=inc, waits=waits):
            for (s, v) in waits:
                e.wait_ge(self.handles[s], v)
            inst = fn(e)
            if inc is not None and inst is not None:
                inst.then_inc(self.handles[inc[0]], inc[1])

        self.q[eng].append(cl)
        if inc is not None:
            self.cnt[inc[0]] += inc[1]

    def wait(self, eng, sem, val):
        if val > 0:
            def cl(e, sem=sem, val=val):
                e.wait_ge(self.handles[sem], val)
            self.q[eng].append(cl)

    def c(self, name):
        return self.cnt.get(name, 0)


def build_nc():
    nc = bass.Bass("TRN2", target_bir_lowering=False,
                   detect_race_conditions=False)
    E = Emit()
    SEMS = ("dx", "dg", "sv", "ss", "st", "sg", "dout", "dt", "ds")
    for s_ in SEMS:
        E.sem(s_)

    da = {}
    def din(name, shape, dt=F32):
        da[name] = nc.dram_tensor(name, list(shape), dt, kind="ExternalInput").ap()

    din("x_fm", (D, S))
    din("w1T", (D, 128)); din("b1c", (128, 1))
    din("w2T", (128, 64)); din("b2st", (128, 1)); din("w3blk", (128, 2))
    din("b3s", (1, 1))
    din("wqkT", (D, 128)); din("bqk_row", (1, 128))
    din("wvT", (D, 64)); din("bv_row", (1, 64))
    din("woutaT", (64, D)); din("bout_pp", (128, DC))
    din("wsynT", (D, NSYN)); din("bsyn_row", (1, NSYN))
    din("woutfT", (NSYN, D)); din("boutf_pp", (128, DC))
    din("gammaA_pp", (128, DC)); din("betaA_pp", (128, DC))
    din("gammaF_pp", (128, DC)); din("betaF_pp", (128, DC))
    din("c_pp", (128, DC))
    din("wmod", (D, 6 * D))
    din("vauginit", (128, 128), BF16)
    din("idt_bf", (128, 128), BF16)
    din("idt_f32", (128, 128))
    din("iota16_in", (128, 16))
    din("ones_bf_row", (1, 512), BF16)
    din("ones_f32_col", (128, 1))
    din("sc12", (128, 1))
    out_tm = nc.dram_tensor("out_tm", [TI, D], F32, kind="ExternalOutput").ap()

    d_rows = nc.dram_tensor("d_rows", [16, S], F32, kind="Internal").ap()
    d_mod = nc.dram_tensor("d_mod", [1, 6 * D], F32, kind="Internal").ap()
    d_scal = nc.dram_tensor("d_scal", [1, 64], F32, kind="Internal").ap()
    d_cs = nc.dram_tensor("d_cs", [1, 32], F32, kind="Internal").ap()
    d_col = nc.dram_tensor("d_col", [1, 128], F32, kind="Internal").ap()
    d_tqsyn = nc.dram_tensor("d_tqsyn", [D, NSYN], BF16, kind="Internal").ap()
    d_tqout = nc.dram_tensor("d_tqout", [NSYN, D], BF16, kind="Internal").ap()
    d_hidT = nc.dram_tensor("d_hidT", [NSYN, TI], BF16, kind="Internal").ap()
    d_x2 = nc.dram_tensor("d_x2", [D, TI], F32, kind="Internal").ap()
    d_partial = nc.dram_tensor("d_partial", [D, TI], F32, kind="Internal").ap()
    d_rowsb = nc.dram_tensor("d_rowsb", [4, S], BF16, kind="Internal").ap()

    import contextlib
    stack = contextlib.ExitStack()
    def sbt(name, p, f, dt):
        return stack.enter_context(nc.sbuf_tensor(name, [p, f], dt))

    AR1 = sbt("AR1", 128, 16384, F32)
    AR2 = sbt("AR2", 128, 16384, F32)
    AR5 = sbt("AR5", 128, 2048, F32)
    AR6 = sbt("AR6", 128, 4096, BF16)
    xraw = sbt("xraw", 128, 2048, F32)
    xbt = sbt("xbt", 128, 2048, BF16)
    xsq = sbt("xsq", 128, 2048, BF16)
    brow = sbt("brow", 128, 4096, BF16)
    brf = sbt("brf", 128, 1024, F32)
    h1n = sbt("h1n", 128, 2048, BF16)
    qks = sbt("qks", 128, 2048, BF16)
    vaug = sbt("vaug", 128, DC * 128, BF16)
    w2t = sbt("w2t", 128, 64, BF16)
    w3b = sbt("w3b", 128, 2, BF16)
    hidst = sbt("hidst", 128, 2048, BF16)
    scr2 = sbt("scr2", 128, 2048, F32)
    stw = sbt("stw", 128, 4 * DC, BF16)
    stw2 = sbt("stw2", 128, 2 * DC, BF16)
    modp = sbt("modp", 128, 96, F32)
    mcon = sbt("mcon", 128, 112, F32)
    gcon = sbt("gcon", 128, 64, F32)
    ppst = sbt("ppst", 128, 12 * DC, F32)
    ppbf = sbt("ppbf", 128, 64, BF16)
    tmpk = sbt("tmpk", 128, 32, F32)
    iota16 = sbt("iota16", 128, 16, F32)
    sel16 = sbt("sel16", 128, 16, F32)
    cand16 = sbt("cand16", 128, 16, F32)
    tcol1 = sbt("tcol1", 128, 1, F32)
    tcol2 = sbt("tcol2", 128, 1, F32)
    idbf = sbt("idbf", 128, 128, BF16)
    idf2 = sbt("idf2", 128, 128, F32)
    onebf = sbt("onebf", 1, 512, BF16)
    onef = sbt("onef", 128, 1, F32)
    onebc = sbt("onebc", 128, 1, BF16)
    srow = sbt("srow", 1, 256, F32)
    scpp = sbt("scpp", 128, DC, F32)
    scrr = sbt("scrr", 128, DC, F32R)
    b1cs = sbt("b1cs", 128, 1, F32)
    b2cs = sbt("b2cs", 128, 1, F32)
    sc12s = sbt("sc12s", 128, 1, F32)
    bobpp = sbt("bobpp", 128, DC, F32)
    bfbpp = sbt("bfbpp", 128, DC, F32)
    gapp = sbt("gapp", 128, DC, F32)
    bapp = sbt("bapp", 128, DC, F32)
    gfpp = sbt("gfpp", 128, DC, F32)
    bfpp = sbt("bfpp", 128, DC, F32)
    bqkr = sbt("bqkr", 1, 128, BF16)
    bvr = sbt("bvr", 1, 64, BF16)
    s1row = sbt("s1row", 1, 128, BF16)
    sclr = sbt("sclr", 1, 16, F32)
    wmodr = sbt("wmodr", 128, 1024, F32R)
    psum_t = stack.enter_context(nc.psum_tensor("p_all", [128, 4096], F32))
    for s_ in SEMS:
        E.handles[s_] = stack.enter_context(nc.semaphore(s_))
    stack.enter_context(nc.allow_non_contiguous_dma(
        reason="small pp-layout rearrange DMAs (KB-scale)"))
    block = stack.enter_context(nc.Block())

    p_all = psum_t[:]
    p_bf = psum_t[:].bitcast(BF16)
    p_mm = psum_t[:, 0:2048]
    p_sc = psum_t[:, 2048:4096]
    AR1b = AR1[:].bitcast(BF16)
    AR2b = AR2[:].bitcast(BF16)
    AR5b = AR5[:].bitcast(BF16)
    wtr = AR5[:]
    mrow = AR5[:]
    wgt = brow[:, 2048:4096]
    h1b = brf[:]
    kjit = AR6[:]
    qjit = xbt[:]
    utile = hidst[:]
    woat = AR6[0:64, 0:2048]
    aotb = hidst[0:64, 0:1024]
    aot = scr2[0:64, 1024:2048]
    aden = scr2[64:128, 1024:2048]
    rowsb = scr2[0:8, 0:2048]
    denr = xraw[0:64, 0:1024]
    x2l = xraw[:, 0:1024]
    ytile = qks[:].bitcast(F32)
    otile = xbt[:].bitcast(F32)
    scr2b = scr2[:].bitcast(BF16)
    bsyn_rows = None

    def dma(dst, src, waits=(), eng="sync", sem="dx", pipe=False):
        waits = list(waits)
        if not pipe:
            waits.append((sem, E.c(sem)))
        E.op(eng, lambda e, d=dst, s=src: e.dma_start(out=d, in_=s), inc=(sem, 16),
             waits=waits)
        return E.c(sem)

    def V(fn, waits=(), dep=True):
        if dep:
            def fn2(e, fn=fn):
                e.drain()
                return fn(e)
        else:
            fn2 = fn
        E.op("vector", fn2, inc=("sv", 1), waits=waits)
        return E.c("sv")

    def SC(fn, waits=(), dep=True):
        if dep:
            def fn2(e, fn=fn):
                e.drain()
                return fn(e)
        else:
            fn2 = fn
        E.op("scalar", fn2, inc=("ss", 1), waits=waits)
        return E.c("ss")

    def T(fn, waits=()):
        E.op("tensor", fn, inc=("st", 1), waits=waits)
        return E.c("st")

    def G(fn, waits=()):
        E.op("gpsimd", fn, inc=("sg", 1), waits=waits)
        return E.c("sg")

    dma(idbf[:], da["idt_bf"])
    dma(idf2[:], da["idt_f32"])
    dma(iota16[:], da["iota16_in"])
    dma(onebf[:], da["ones_bf_row"])
    dma(onef[:], da["ones_f32_col"])
    dma(gcon[:, 0:1], da["b3s"].to_broadcast([128, 1]))
    dma(scpp[:, 0:DC], da["c_pp"])
    dma(b1cs[:], da["b1c"])
    dma(b2cs[:], da["b2st"])
    dma(sc12s[:], da["sc12"])
    dma(bobpp[:], da["bout_pp"])
    dma(bfbpp[:], da["boutf_pp"])
    dma(gapp[:], da["gammaA_pp"])
    dma(bapp[:], da["betaA_pp"])
    dma(gfpp[:], da["gammaF_pp"])
    dma(bfpp[:], da["betaF_pp"])
    dma(wtr[:, 0:64], da["w2T"])
    dma(wtr[:, 64:66], da["w3blk"])
    dma(wtr[0:1, 128:256], da["bqk_row"])
    dma(wtr[0:1, 256:320], da["bv_row"])
    ld0 = E.c("dx")
    V(lambda e: e.tensor_copy(w2t[:], wtr[:, 0:64]), [("dx", ld0)])
    V(lambda e: e.tensor_copy(w3b[:], wtr[:, 64:66]))
    V(lambda e: e.tensor_copy(bqkr[:], wtr[0:1, 128:256]))
    V(lambda e: e.tensor_copy(bvr[:], wtr[0:1, 256:320]))
    V(lambda e: e.tensor_copy(onebc[:], onef[:]))
    sm0 = E.c("sv")
    SC(lambda e: e.activation(scpp[:, 0:DC], scpp[:, 0:DC], AF.Silu), [("dx", ld0)])
    silu_v = V(lambda e: e.tensor_copy(scrr[:], scpp[:, 0:DC]), [("ss", E.c("ss"))])

    consts_v = E.c("sv")

    ev_prev = 0
    rdy_hist = [0, 0]
    last_mm = 0
    for cg in range(3):
        mm_hist = [0] * DC
        for dch in range(DC):
            slot = dch % 2
            wdeps = [("sv", silu_v)] if (cg == 0 and dch == 0) else []
            if rdy_hist[slot]:
                wdeps.append(("sv", rdy_hist[slot]))
            w = dma(AR1[:, slot * 4096:(slot + 1) * 4096],
                    da["wmod"][dch * 128:(dch + 1) * 128, cg * 4096:(cg + 1) * 4096],
                    waits=wdeps, pipe=True)
            for pc4 in range(4):
                rdeps = [("dx", w)] if pc4 == 0 else []
                if last_mm:
                    rdeps.append(("st", last_mm))
                rdy = V(lambda e, slot=slot, pc4=pc4: e.tensor_copy(
                    wmodr[:, 0:1024],
                    AR1[:, slot * 4096 + pc4 * 1024:
                           slot * 4096 + (pc4 + 1) * 1024]), rdeps)
                for n2 in range(2):
                    nch = 2 * pc4 + n2
                    mmw = [("sv", rdy)] if n2 == 0 else []
                    if cg > 0 and dch == 0 and nch == 0:
                        mmw.append(("sv", ev_prev))
                    T(lambda e, nch=nch, n2=n2, dch=dch: e.matmul(
                        p_all[0:1, nch * 512:(nch + 1) * 512],
                        scrr[:, dch:dch + 1],
                        wmodr[:, n2 * 512:(n2 + 1) * 512],
                        start=(dch == 0), stop=(dch == DC - 1)), mmw)
                last_mm = E.c("st")
            rdy_hist[slot] = rdy
            mm_hist[dch] = E.c("st")
        V(lambda e: e.tensor_copy(mrow[0:1, 0:2048], p_all[0:1, 0:2048]),
          [("st", E.c("st"))])
        ev_prev = V(lambda e: e.tensor_copy(scr2[0:1, 0:2048], p_all[0:1, 2048:4096]))
        dma(d_mod[:, cg * 4096:cg * 4096 + 2048], mrow[0:1, 0:2048],
            waits=[("sv", ev_prev)])
        dma(d_mod[:, cg * 4096 + 2048:(cg + 1) * 4096], scr2[0:1, 0:2048],
            waits=[("sv", ev_prev)])
    mod_d = E.c("dx")
    for m_ in range(6):
        w = dma(modp[:, m_ * 16:(m_ + 1) * 16],
                d_mod[:, m_ * 2048:(m_ + 1) * 2048]
                .rearrange("o (k p) -> (o p) k", k=16),
                waits=[("dx", mod_d)] if m_ == 0 else ())
    mod_pp = w

    sa_t = modp[:, 0:16]; ca_t = modp[:, 16:32]; gaw = modp[:, 32:48]
    sb_t = modp[:, 48:64]; cb_t = modp[:, 64:80]; gbw = modp[:, 80:96]
    A1 = mcon[:, 0:16]; A2 = mcon[:, 16:32]
    gaT = mcon[:, 32:48]; gbT = mcon[:, 48:64]
    A1sq = mcon[:, 64:80]; A1B1 = mcon[:, 80:96]; B1sq = mcon[:, 96:112]
    V(lambda e: e.tensor_scalar(A1, ca_t, 1.0, None, OP.add), [("dx", mod_pp)])
    V(lambda e: e.tensor_scalar(A2, cb_t, 1.0, None, OP.add))
    for (gsrc, gdst) in ((gaw, gaT), (gbw, gbT)):
        V(lambda e, gs=gsrc, gd=gdst: e.tensor_scalar(gd, gs, 0.0, 2.0, OP.is_ge,
                                                      OP.mult))
        V(lambda e, gd=gdst: e.tensor_scalar(gd, gd, 1.0, None, OP.subtract))
        V(lambda e, gs=gsrc: e.tensor_scalar(tmpk[:, 0:16], gs, -1.0, None, OP.mult))
        V(lambda e, gs=gsrc: e.tensor_tensor(tmpk[:, 0:16], tmpk[:, 0:16], gs, OP.max))
        V(lambda e: e.tensor_scalar(tmpk[:, 0:16], tmpk[:, 0:16], 0.25, None, OP.max))
        V(lambda e, gd=gdst: e.tensor_tensor(gd, gd, tmpk[:, 0:16], OP.mult))
    V(lambda e: e.tensor_tensor(A1sq, A1, A1, OP.mult))
    V(lambda e: e.tensor_tensor(A1B1, A1, sa_t, OP.mult))
    V(lambda e: e.tensor_tensor(B1sq, sa_t, sa_t, OP.mult))
    stwv = stw[:].rearrange("p (k f) -> p k f", f=4)
    stw2v = stw2[:].rearrange("p (k f) -> p k f", f=2)
    V(lambda e: e.memset(stwv[:, :, 0:1], 1.0))
    V(lambda e: e.tensor_copy(stwv[:, :, 1:2], A1.unsqueeze(2)))
    V(lambda e: e.tensor_copy(stwv[:, :, 2:3], A1sq.unsqueeze(2)))
    V(lambda e: e.tensor_copy(stwv[:, :, 3:4], A1B1.unsqueeze(2)))
    V(lambda e: e.memset(stw2v[:, :, 0:1], 1.0))
    drv_v = V(lambda e: e.tensor_copy(stw2v[:, :, 1:2], A1sq.unsqueeze(2)))
    m1 = T(lambda e: e.matmul(p_sc[0:1, 0:112], onef[:], mcon[:, 0:112],
                              start=True, stop=True), [("sv", drv_v)])
    m2 = T(lambda e: e.matmul(p_sc[0:1, 112:128], onef[:], sa_t,
                              start=True, stop=True))
    V(lambda e: e.tensor_copy(srow[0:1, 0:128], p_sc[0:1, 0:128]), [("st", m2)])
    for i, off in enumerate((0, 64, 80, 96, 112)):
        V(lambda e, off=off, i=i: e.reduce_sum(srow[0:1, 140 + i:141 + i],
                                               srow[0:1, off:off + 16], axis=AXX))
    cscal_v = V(lambda e: e.tensor_scalar(srow[0:1, 145:146], srow[0:1, 144:145],
                                          1.0 / D, None, OP.mult))
    w = dma(d_scal[:, 0:6], srow[0:1, 140:146], waits=[("sv", cscal_v)])
    gcon_scal = dma(gcon[:, 8:14], d_scal[:, 0:6].to_broadcast([128, 6]),
                    waits=[("dx", w)])

    cast_hist = [0] * DC
    sq_hist = [0] * DC
    mm_hist = [0] * DC
    for dch in range(DC):
        wd = [("sv", E.c("sv")), ("ss", E.c("ss"))]
        if dch == 0:
            wd.append(("dx", gcon_scal))
        w = dma(xraw[:, 0:2048], da["x_fm"][dch * 128:(dch + 1) * 128, :], waits=wd,
                pipe=True)
        cd = [("dx", w)]
        if dch >= 1:
            cd.append(("st", mm_hist[dch - 1]))
        cast_hist[dch] = V(lambda e: e.tensor_copy(xbt[:, 0:2048], xraw[:, 0:2048]),
                           cd)
        sd = [("sv", cast_hist[dch])]
        if dch >= 1:
            sd.append(("st", mm_hist[dch - 1]))
        sq_hist[dch] = SC(lambda e: e.activation(xsq[:, 0:2048], xbt[:, 0:2048],
                                                 AF.Square), sd)
        for nch in range(4):
            T(lambda e, dch=dch, nch=nch: e.matmul(
                p_sc[0:4, nch * 512:(nch + 1) * 512],
                stw[:, 4 * dch:4 * dch + 4],
                xbt[:, nch * 512:(nch + 1) * 512],
                start=(dch == 0), stop=(dch == DC - 1)),
              [("sv", cast_hist[dch])] if nch == 0 else ())
            T(lambda e, dch=dch, nch=nch: e.matmul(
                p_mm[0:2, nch * 512:(nch + 1) * 512],
                stw2[:, 2 * dch:2 * dch + 2],
                xsq[:, nch * 512:(nch + 1) * 512],
                start=(dch == 0), stop=(dch == DC - 1)),
              [("ss", sq_hist[dch])] if nch == 0 else ())
        mm_hist[dch] = E.c("st")
    V(lambda e: e.tensor_copy(rowsb[0:4, :], p_sc[0:4, 0:2048]), [("st", E.c("st"))])
    ev_rows = V(lambda e: e.tensor_copy(AR5[0:2, 0:2048], p_mm[0:2, 0:2048]))
    w = dma(d_rows[0:4, :], rowsb[0:4, :], waits=[("sv", ev_rows)])
    w = dma(d_rows[4:6, :], AR5[0:2, 0:2048], waits=[("sv", ev_rows)])
    for r_ in range(6):
        pp_d = dma(ppst[:, r_ * 16:(r_ + 1) * 16],
                   d_rows[r_:r_ + 1, :].rearrange("r (k p) -> (r p) k", k=16),
                   waits=[("dx", w)] if r_ == 0 else ())

    B_ = lambda i: ppst[:, i * 16:(i + 1) * 16]
    Sx, SAx, SA2x, SABx, Sxx, SA2xx = B_(0), B_(1), B_(2), B_(3), B_(4), B_(5)
    mu1, rs1, muh, rsh, t1_, t2_ = B_(6), B_(7), B_(8), B_(9), B_(10), B_(11)
    gsc = lambda i: gcon[:, 8 + i:9 + i]
    V(lambda e: e.tensor_scalar(mu1, Sx, 1.0 / D, None, OP.mult), [("dx", pp_d)])
    V(lambda e: e.tensor_tensor(t1_, mu1, mu1, OP.mult))
    V(lambda e: e.tensor_scalar(t2_, Sxx, 1.0 / D, None, OP.mult))
    V(lambda e: e.tensor_tensor(t2_, t2_, t1_, OP.subtract))
    V(lambda e: e.tensor_scalar(t2_, t2_, LN_EPS, None, OP.add))
    rv = V(lambda e: e.reciprocal(t2_, t2_))
    rs1_s = SC(lambda e: e.activation(rs1, t2_, AF.Sqrt), [("sv", rv)])
    V(lambda e: e.scalar_tensor_tensor(t1_, mu1, gsc(0), SAx, OP.mult, OP.subtract),
      [("ss", rs1_s)])
    V(lambda e: e.tensor_tensor(t1_, t1_, rs1, OP.mult))
    V(lambda e: e.tensor_scalar(t1_, t1_, -1.0 / D, None, OP.mult))
    V(lambda e: e.tensor_scalar(muh, t1_, gsc(5), None, OP.add))
    V(lambda e: e.tensor_tensor(t1_, mu1, SA2x, OP.mult))
    V(lambda e: e.scalar_tensor_tensor(t1_, t1_, -2.0, SA2xx, OP.mult, OP.add))
    V(lambda e: e.tensor_tensor(t2_, mu1, mu1, OP.mult))
    V(lambda e: e.tensor_scalar(t2_, t2_, gsc(1), None, OP.mult))
    V(lambda e: e.tensor_tensor(t1_, t1_, t2_, OP.add))
    V(lambda e: e.tensor_tensor(t2_, rs1, rs1, OP.mult))
    V(lambda e: e.tensor_tensor(t1_, t1_, t2_, OP.mult))
    V(lambda e: e.scalar_tensor_tensor(t2_, mu1, gsc(2), SABx, OP.mult, OP.subtract))
    V(lambda e: e.scalar_tensor_tensor(t2_, t2_, -2.0, rs1, OP.mult, OP.mult))
    V(lambda e: e.tensor_tensor(t1_, t1_, t2_, OP.add))
    V(lambda e: e.tensor_scalar(t1_, t1_, gsc(3), None, OP.add))
    V(lambda e: e.tensor_scalar(t1_, t1_, 1.0 / D, None, OP.mult))
    V(lambda e: e.tensor_tensor(t2_, muh, muh, OP.mult))
    V(lambda e: e.tensor_tensor(t1_, t1_, t2_, OP.subtract))
    V(lambda e: e.tensor_scalar(t1_, t1_, LN_EPS, None, OP.add))
    rv = V(lambda e: e.reciprocal(t1_, t1_))
    rsh_s = SC(lambda e: e.activation(rsh, t1_, AF.Sqrt), [("sv", rv)])
    ppc = V(lambda e: e.tensor_copy(ppbf[:].rearrange("p (r k) -> p r k", r=4),
                                    ppst[:].rearrange("p (r k) -> p r k", r=12)
                                    [:, 6:10, :]), [("ss", rsh_s)])
    for r_ in range(4):
        w = dma(d_rowsb[r_:r_ + 1, :].rearrange("r (k p) -> (r p) k", k=16),
                ppbf[:, r_ * 16:(r_ + 1) * 16], waits=[("sv", ppc)] if r_ == 0 else ())
    mu1b = brow[:, 0:2048]; rs1b = brow[:, 2048:4096]
    muhb = vaug[:, 0:2048]; rshb = qks[:, 0:2048]
    brow_v = 0
    for i, dst in enumerate((mu1b, rs1b, muhb, rshb)):
        brow_v = dma(dst, d_rowsb[i:i + 1, :].to_broadcast([128, 2048]),
                     waits=[("dx", w)])
    E.wait("vector", "dx", brow_v)
    brow_v_sem = ("dx", brow_v)

    gA1 = mcon[:, 64:80]
    gB1o = mcon[:, 80:96]
    V(lambda e: e.tensor_tensor(gA1, gaT, A1, OP.mult),
      [("st", E.c("st")), ("dx", E.c("dx"))])
    V(lambda e: e.tensor_tensor(gB1o, sa_t, bobpp[:], OP.add))
    V(lambda e: e.tensor_tensor(gB1o, gaT, gB1o, OP.mult))
    xn_hist = [0] * DC
    rstt_hist = 0
    for dch in range(DC):
        wd = [brow_v_sem, ("sv", E.c("sv")), ("ss", E.c("ss"))]
        w = dma(xraw[:, 0:2048], da["x_fm"][dch * 128:(dch + 1) * 128, :], waits=wd,
                pipe=True)
        cd = [("dx", w)]
        if dch >= 1:
            cd.append(("ss", xn_hist[dch - 1]))
        V(lambda e: e.tensor_copy(xbt[:, 0:2048], xraw[:, 0:2048]), cd)
        xb_ = xbt[:, 0:2048]
        xh_ = xsq[:, 0:2048]
        V(lambda e, xh_=xh_, xb_=xb_: e.tensor_tensor(xh_, xb_, mu1b, OP.subtract))
        xh_v = V(lambda e, xh_=xh_: e.tensor_tensor(xh_, xh_, rs1b, OP.mult))
        h_s = SC(lambda e, xh_=xh_, dch=dch: e.activation(
            scr2[:, 0:1024], xh_[:, 0:1024], AF.Identity,
            bias=gB1o[:, dch:dch + 1], scale=gA1[:, dch:dch + 1]),
            [("sv", xh_v), ("sv", rstt_hist)])
        rstt_hist = V(lambda e, dch=dch: e.tensor_tensor(
            AR2[:, dch * 1024:(dch + 1) * 1024], scr2[:, 0:1024],
            xraw[:, 0:1024], OP.add), [("ss", h_s)])
        v1_s = SC(lambda e, xb_=xb_, xh_=xh_, dch=dch: e.activation(
            xb_, xh_, AF.Identity, bias=sa_t[:, dch:dch + 1],
            scale=A1[:, dch:dch + 1]), [("sv", E.c("sv"))])
        V(lambda e, xh_=xh_, xb_=xb_: e.tensor_tensor(xh_, xb_, muhb, OP.subtract),
          [("ss", v1_s)])
        xnb_v = V(lambda e, xh_=xh_: e.tensor_tensor(xh_, xh_, rshb, OP.mult))
        xn_hist[dch] = SC(lambda e, xh_=xh_, dch=dch: e.activation(
            AR1b[:, dch * 2048:(dch + 1) * 2048], xh_, AF.Identity,
            bias=bapp[:, dch:dch + 1], scale=gapp[:, dch:dch + 1]), [("sv", xnb_v)])
    xn_s = E.c("ss")
    r_v = E.c("sv")

    xn_ = lambda dch: AR1b[:, dch * 2048:(dch + 1) * 2048]
    w = dma(wtr[:, 0:2048].rearrange("p (c m) -> p c m", m=128),
            da["w1T"].rearrange("(c p) m -> p c m", p=128),
            waits=[("ss", xn_s)])
    wgt_v = V(lambda e: e.tensor_copy(wgt, wtr[:, 0:2048]), [("dx", w)])
    evac_hist = [0, 0]
    for nch in range(4):
        bank = nch % 2
        for dch in range(DC):
            wd = []
            if dch == 0:
                wd = [("ss", xn_s), ("sv", wgt_v)]
                if nch >= 2:
                    wd.append(("sv", evac_hist[bank]))
            T(lambda e, bank=bank, nch=nch, dch=dch: e.matmul(
                p_mm[:, bank * 512:bank * 512 + 512],
                wgt[:, dch * 128:(dch + 1) * 128],
                xn_(dch)[:, nch * 512:(nch + 1) * 512],
                start=(dch == 0), stop=(dch == DC - 1)), wd)
        mmv = E.c("st")
        ev = V(lambda e, bank=bank, nch=nch: e.tensor_scalar(
            h1n[:, nch * 512:(nch + 1) * 512],
            p_mm[:, bank * 512:bank * 512 + 512], -1.0, None, OP.mult),
          [("st", mmv)])
        if nch < 2:
            ev = V(lambda e, bank=bank, nch=nch: e.tensor_scalar(
                h1b[:, nch * 512:(nch + 1) * 512],
                p_mm[:, bank * 512:bank * 512 + 512], b1cs[:, 0:1], None, OP.add))
        evac_hist[bank] = ev
    h1_v = E.c("sv")
    w = dma(wtr[:, 0:2048].rearrange("p (c m) -> p c m", m=128),
            da["wqkT"].rearrange("(c p) m -> p c m", p=128),
            waits=[("st", E.c("st"))])
    wgt_v = V(lambda e: e.tensor_copy(wgt, wtr[:, 0:2048]), [("dx", w)])
    wb = dma(wtr[0:1, 0:128], da["bqk_row"], waits=[("sv", wgt_v)])
    wgt_v = V(lambda e: e.tensor_copy(bqkr[:], wtr[0:1, 0:128]), [("dx", wb)])
    evac_hist = [0, 0]
    for nch in range(4):
        bank = nch % 2
        for dch in range(DC):
            wd = []
            if dch == 0:
                wd = [("sv", h1_v), ("sv", wgt_v)] if nch < 2 else [("sv", evac_hist[bank])]
            T(lambda e, bank=bank, nch=nch, dch=dch: e.matmul(
                p_sc[:, bank * 512:bank * 512 + 512],
                wgt[:, dch * 128:(dch + 1) * 128],
                xn_(dch)[:, nch * 512:(nch + 1) * 512],
                start=(dch == 0), stop=False), wd)
        T(lambda e, bank=bank: e.matmul(
            p_sc[:, bank * 512:bank * 512 + 512],
            bqkr[:], onebf[:, 0:512], start=False, stop=True))
        mmv = E.c("st")
        evac_hist[bank] = V(lambda e, bank=bank, nch=nch: e.tensor_scalar(
            qks[:, nch * 512:(nch + 1) * 512],
            p_sc[:, bank * 512:bank * 512 + 512],
            sc12s[:, 0:1], None, OP.mult), [("st", mmv)])
    qk_v = E.c("sv")
    w = dma(wtr[:, 0:1024].rearrange("p (c m) -> p c m", m=64),
            da["wvT"].rearrange("(c p) m -> p c m", p=128),
            waits=[("st", E.c("st"))])
    wgt_v = V(lambda e: e.tensor_copy(wgt[:, 0:1024], wtr[:, 0:1024]), [("dx", w)])
    wb = dma(wtr[0:1, 0:64], da["bv_row"], waits=[("sv", wgt_v)])
    wgt_v = V(lambda e: e.tensor_copy(bvr[:], wtr[0:1, 0:64]), [("dx", wb)])
    vgi = dma(vaug[:].rearrange("p (c m) -> p c m", m=128),
              da["vauginit"].unsqueeze(1).to_broadcast([128, DC, 128]),
              waits=[("ss", xn_s), ("st", E.c("st"))])
    vaugv = vaug[:].rearrange("p (c g f) -> p c g f", c=DC, g=32)
    evac_hist = [0, 0, 0, 0]
    for jc in range(DC):
        bank = jc % 4
        for dch in range(DC):
            wd = []
            if dch == 0:
                wd = [("sv", qk_v), ("sv", wgt_v)] if jc < 4 else [("sv", evac_hist[bank])]
            T(lambda e, bank=bank, jc=jc, dch=dch: e.matmul(
                p_mm[:, bank * 512:bank * 512 + 64],
                xn_(dch)[:, jc * 128:(jc + 1) * 128],
                wgt[:, dch * 64:(dch + 1) * 64],
                start=(dch == 0), stop=False), wd)
        T(lambda e, bank=bank: e.matmul(
            p_mm[:, bank * 512:bank * 512 + 64],
            onebf[0:1, 0:128], bvr[:], start=False, stop=True))
        mmv = E.c("st")
        evac_hist[bank] = V(lambda e, bank=bank, jc=jc: e.tensor_copy(
            vaugv[:, jc, :, 0:2],
            p_mm[:, bank * 512:bank * 512 + 64].rearrange("p (g f) -> p g f", g=32)),
            [("st", mmv), ("dx", vgi)] if jc == 0 else [("st", mmv)])
    vt_v = E.c("sv")
    csm = V(lambda e: e.reduce_max(tcol1[:], qks[:], axis=AXX,
                                   apply_absolute_value=True), [("sv", vt_v)])
    w = dma(d_col[:, 0:128].rearrange("o p -> p o"), tcol1[:, 0:1],
            waits=[("sv", csm)])
    w = dma(srow[0:1, 0:128], d_col[:, 0:128], waits=[("dx", w)])
    srv = srow[0:1, 0:128].rearrange("o (a h f) -> o a h f", a=2, h=32)
    V(lambda e: e.tensor_tensor(mrow[0:1, 0:64].rearrange("o (a h) -> o a h", a=2),
                                srv[:, :, :, 0], srv[:, :, :, 1], OP.max), [("dx", w)])
    V(lambda e: e.tensor_tensor(mrow[0:1, 64:96], mrow[0:1, 0:32], mrow[0:1, 32:64],
                                OP.mult))
    csv = V(lambda e: e.tensor_scalar(mrow[0:1, 64:96], mrow[0:1, 64:96], 60.0, -1.0,
                                      OP.min, OP.mult))
    w = dma(d_cs[:, 0:32], mrow[0:1, 64:96], waits=[("sv", csv)])
    cs_pp = dma(gcon[:, 16:48], d_cs[:, 0:32].to_broadcast([128, 32]),
                waits=[("dx", w)])
    E.wait("vector", "dx", cs_pp)

    def topk(zblk, W, k, z0, tcol, scr_bf, u_f32, pre, sw=None):
        sw = sw or W
        np_ = W // sw
        col = lambda i: tmpk[:, i:i + 1]
        pw = list(pre)
        for pc in range(np_):
            SC(lambda e, pc=pc: e.activation(
                scr_bf[:, 0:sw], zblk[:, pc * sw:(pc + 1) * sw], AF.Square,
                accum_out=col(12 + pc)), pw)
        sq_s = E.c("ss")
        V(lambda e: e.reduce_sum(col(0), zblk[:, 0:W], axis=AXX), pw)
        if np_ == 2:
            V(lambda e: e.tensor_tensor(col(1), col(12), col(13), OP.add),
              [("ss", sq_s)])
        else:
            V(lambda e: e.tensor_copy(col(1), col(12)), [("ss", sq_s)])
        V(lambda e: e.tensor_scalar(col(0), col(0), 1.0 / W, None, OP.mult))
        V(lambda e: e.tensor_scalar(col(1), col(1), 1.0 / W, None, OP.mult))
        V(lambda e: e.scalar_tensor_tensor(col(2), col(0), col(0), col(1), OP.mult,
                                           OP.subtract))
        vv = V(lambda e: e.tensor_scalar(col(2), col(2), -1.0, 1e-12, OP.mult, OP.max))
        sg_s = SC(lambda e: e.activation(col(3), col(2), AF.Sqrt), [("sv", vv)])
        V(lambda e: e.scalar_tensor_tensor(col(4), col(3), z0, col(0), OP.mult, OP.add),
          [("ss", sg_s)])
        V(lambda e: e.scalar_tensor_tensor(col(5), col(3), -8.0, col(0), OP.mult,
                                           OP.add))
        V(lambda e: e.scalar_tensor_tensor(col(6), col(3), 8.0, col(0), OP.mult,
                                           OP.add))
        V(lambda e: e.memset(col(7), float(W)))
        V(lambda e: e.memset(col(8), 0.0))

        def count_scan(tgt, op):
            for pc in range(np_):
                V(lambda e, pc=pc, op=op: e.tensor_scalar(
                    scr_bf[:, 0:sw], zblk[:, pc * sw:(pc + 1) * sw],
                    col(4) if op == OP.is_ge else col(6), None,
                    op, OP.add, accum_out=col(12 + pc)))
            if np_ == 2:
                V(lambda e, tgt=tgt: e.tensor_tensor(tgt, col(12), col(13), OP.add))
            else:
                V(lambda e, tgt=tgt: e.tensor_copy(tgt, col(12)))

        for it in range(NSEC):
            count_scan(col(11), OP.is_ge)
            V(lambda e: e.tensor_scalar(col(9), col(11), k + 0.5, None, OP.is_ge))
            V(lambda e: e.tensor_scalar(col(10), col(9), -1.0, 1.0, OP.mult, OP.add))
            V(lambda e: e.tensor_tensor(col(12), col(4), col(5), OP.subtract))
            V(lambda e: e.scalar_tensor_tensor(col(5), col(12), col(9), col(5),
                                               OP.mult, OP.add))
            V(lambda e: e.tensor_tensor(col(12), col(11), col(7), OP.subtract))
            V(lambda e: e.scalar_tensor_tensor(col(7), col(12), col(9), col(7),
                                               OP.mult, OP.add))
            V(lambda e: e.tensor_tensor(col(12), col(4), col(6), OP.subtract))
            V(lambda e: e.scalar_tensor_tensor(col(6), col(12), col(10), col(6),
                                               OP.mult, OP.add))
            V(lambda e: e.tensor_tensor(col(12), col(11), col(8), OP.subtract))
            V(lambda e: e.scalar_tensor_tensor(col(8), col(12), col(10), col(8),
                                               OP.mult, OP.add))
            if it < NSEC - 1:
                V(lambda e: e.tensor_tensor(col(12), col(7), col(8), OP.subtract))
                V(lambda e: e.reciprocal(col(12), col(12)))
                V(lambda e: e.tensor_scalar(col(13), col(7), float(-(k - 3)), None,
                                            OP.add))
                V(lambda e: e.tensor_tensor(col(13), col(13), col(12), OP.mult))
                V(lambda e: e.tensor_scalar(col(13), col(13), 0.02, 0.98, OP.max,
                                            OP.min))
                V(lambda e: e.tensor_tensor(col(12), col(6), col(5), OP.subtract))
                V(lambda e: e.scalar_tensor_tensor(col(4), col(12), col(13), col(5),
                                                   OP.mult, OP.add))
        V(lambda e: e.memset(col(14), 0.0))
        for pc in range(np_):
            V(lambda e, pc=pc: e.tensor_scalar(
                scr_bf[:, 0:sw], zblk[:, pc * sw:(pc + 1) * sw], col(6), None,
                OP.is_lt, OP.add, accum_out=col(12)))
            V(lambda e: e.tensor_tensor(col(14), col(14), col(12), OP.add))
            V(lambda e, pc=pc: e.scalar_tensor_tensor(
                u_f32[:, pc * sw:(pc + 1) * sw], scr_bf[:, 0:sw], BIGM,
                zblk[:, pc * sw:(pc + 1) * sw], OP.mult, OP.add))
        V(lambda e: e.max(out=tmpk[:, 16:24], in_=u_f32[:, 0:W]))
        V(lambda e: e.match_replace(out=u_f32[:, 0:W], in_to_replace=tmpk[:, 16:24],
                                    in_values=u_f32[:, 0:W], imm_value=-1.0e30))
        V(lambda e: e.max(out=tmpk[:, 24:32], in_=u_f32[:, 0:W]))
        V(lambda e: e.tensor_scalar(col(15), col(14), float(k - W - 1), None, OP.add))
        V(lambda e: e.tensor_scalar(col(12), col(15), 0.0, 15.0, OP.max, OP.min))
        V(lambda e: e.tensor_scalar(sel16[:], iota16[:], col(12), None, OP.is_equal))
        V(lambda e: e.tensor_scalar(cand16[:], tmpk[:, 16:32], -BIGM, None, OP.add))
        V(lambda e: e.scalar_tensor_tensor(cand16[:], cand16[:], 1.0, sel16[:],
                                           OP.mult, OP.mult, accum_out=col(13)))
        V(lambda e: e.tensor_scalar(col(9), col(15), 0.0, None, OP.is_lt))
        V(lambda e: e.tensor_tensor(col(12), col(6), col(13), OP.subtract))
        return V(lambda e, tcol=tcol: e.scalar_tensor_tensor(
            tcol[:, 0:1], col(12), col(9), col(13), OP.mult, OP.add))

    indT = AR1b[:, 0:16384]
    Ebuf = AR1b[:, 16384:32768]
    simblk = AR5[:, 0:2048]
    simsp = scr2[:, 0:2048]
    pairA = AR6[:, 0:2048]
    pairB = AR6[:, 2048:4096]

    paccum = xsq[:].bitcast(F32)[:, 0:256]
    pa_state = {"s": [0, 0], "idx": 0}
    passA_start_sv = E.c("sv")
    passA_start_ss = E.c("ss")
    pa_chunks = []
    for srcq in (da["wsynT"], da["woutfT"]):
        for pch in range(srcq.shape[0] // 128):
            for cc2 in range(srcq.shape[1] // 1024):
                pa_chunks.append(
                    srcq[pch * 128:(pch + 1) * 128,
                         cc2 * 1024:(cc2 + 1) * 1024])

    def emit_passA_chunk():
        qa_idx = pa_state["idx"]
        if qa_idx >= len(pa_chunks):
            return
        slot = qa_idx % 2
        wl = [("ss", pa_state["s"][slot])]
        if qa_idx == 0:
            wl += [("sv", passA_start_sv), ("ss", passA_start_ss)]
        w = dma(xraw[:, slot * 1024:(slot + 1) * 1024], pa_chunks[qa_idx],
                waits=wl, eng="gpsimd", sem="dg", pipe=True)
        pa_state["s"][slot] = SC(
            lambda e, slot=slot, qa_idx=qa_idx: e.activation(
                xraw[:, slot * 1024:(slot + 1) * 1024],
                xraw[:, slot * 1024:(slot + 1) * 1024],
                AF.Abs, accum_out=paccum[:, qa_idx:qa_idx + 1]),
            [("dg", w)], dep=False)
        pa_state["idx"] = qa_idx + 1

    V(lambda e: e.memset(p_sc[:, 0:2048], 0.0), [("st", E.c("st"))])

    mmA_hist = 0
    mmA_par = [0, 0]
    indT_ev = 0
    uev_reg = [("ss", 0), ("ss", 0)]
    se_pp = [0, 0]
    rearr_dx = 0
    rearr_ds = 0
    se_hist = 0
    rearr_hist = 0
    ind_v_hist = 0
    indT_done = 0
    for ic in range(NIC):
        for sb2 in range(2):
            sbi = ic * 2 + sb2
            qoff = ic * 256 + sb2 * 128
            grp_dma0 = E.c("dx")
            for g in range(32):
                emit_passA_chunk()
                conB_pp = [0, 0]
                for pp_ in range(2):
                    qA = qoff + 4 * g + 2 * pp_
                    qB = qA + 1
                    pA = pairA if pp_ == 0 else brow[:, 0:2048]
                    pB = pairB if pp_ == 0 else brow[:, 2048:4096]
                    cwA = [("st", mmA_par[pp_])]
                    if sbi == 0 and g == 0 and pp_ == 0:
                        cwA.append(("dx", cs_pp))
                    V(lambda e, qA=qA, pA=pA: e.tensor_scalar(
                        pA[:, 0:2048], h1n[:], h1b[:, qA:qA + 1], 0.0,
                        OP.add, OP.max), cwA, dep=False)
                    conB_pp[pp_] = V(lambda e, qB=qB, pB=pB: e.tensor_scalar(
                        pB[:, 0:2048], h1n[:], h1b[:, qB:qB + 1], 0.0,
                        OP.add, OP.max), dep=False)
                regs_pair = (p_mm, p_sc)
                utl = (utile[:, 0:2048], xbt[:, 0:2048])
                for pp_ in range(2):
                    ri = (g + pp_) % 2
                    reg = regs_pair[ri]
                    pA = pairA if pp_ == 0 else brow[:, 0:2048]
                    pB = pairB if pp_ == 0 else brow[:, 2048:4096]
                    mmAw_extra = [("sv", indT_ev)] if (g == 0 and pp_ == 0) else []
                    for nch2 in range(4):
                        wd = ([("sv", conB_pp[pp_]), uev_reg[ri]] +
                              ([("sv", se_pp[0])] if pp_ == 1 else []) +
                              mmAw_extra) if nch2 == 0 else []
                        T(lambda e, nch2=nch2, pA=pA, reg=reg: e.matmul(
                            reg[0:64, nch2 * 512:(nch2 + 1) * 512], w2t[:],
                            pA[:, nch2 * 512:(nch2 + 1) * 512],
                            start=True, stop=True, tile_position=(0, 0)), wd)
                        T(lambda e, nch2=nch2, pB=pB, reg=reg: e.matmul(
                            reg[64:128, nch2 * 512:(nch2 + 1) * 512], w2t[:],
                            pB[:, nch2 * 512:(nch2 + 1) * 512],
                            start=True, stop=True, tile_position=(0, 64)))
                    mmA_hist = E.c("st")
                    mmA_par[pp_] = mmA_hist
                creg = regs_pair[g % 2]
                for pp_ in range(2):
                    ri = (g + pp_) % 2
                    reg = regs_pair[ri]
                    uv = SC(lambda e, reg=reg, pp_=pp_: e.activation(
                        utl[pp_], reg[:, 0:2048], AF.Relu,
                        bias=b2cs[:, 0:1]), [("st", mmA_par[pp_])], dep=False)
                    uev_reg[ri] = ("ss", uv)
                    for nch2 in range(4):
                        wd = [uev_reg[ri]] if nch2 == 0 else []
                        T(lambda e, nch2=nch2, pp_=pp_, creg=creg: e.matmul(
                            creg[64 * pp_:64 * pp_ + 2,
                                 nch2 * 512:(nch2 + 1) * 512],
                            w3b[:, 0:2], utl[pp_][:, nch2 * 512:(nch2 + 1) * 512],
                            start=True, stop=True, tile_position=(0, 64 * pp_)), wd)
                mmC_hist = E.c("st")
                sew = [("st", mmC_hist), ("dx", rearr_dx)]
                se_hist = V(lambda e, creg=creg: e.tensor_scalar(
                    scr2[0:66, 0:2048], creg[0:66, 0:2048],
                    gcon[0:66, 0:1], None, OP.add), sew, dep=False)
                se_pp[0] = se_hist
                rw = [("sv", se_hist)]
                if g == 0:
                    rw.append(("sv", ind_v_hist))
                for j, row in enumerate((0, 1, 64, 65)):
                    dma(simblk[4 * g + j:4 * g + j + 1, :],
                        scr2[row:row + 1, 0:2048],
                        waits=rw if j == 0 else (), eng="sync", sem="dx",
                        pipe=True)
                rearr_dx = E.c("dx")
            dg_v = G(lambda e, qoff=qoff: e.affine_select(
                out=simblk[:, 0:2048], in_=simblk[:, 0:2048],
                compare_op=OP.not_equal, fill=0.0, base=qoff,
                pattern=[[-1, 2048]], channel_multiplier=1),
                [("dx", rearr_dx)])
            topk(simblk, 2048, KSIM, Z0_SIM, tcol1, pairA, scr2,
                 [("sg", dg_v), ("st", mmA_hist), ("st", indT_done)])
            ind_v_hist = V(lambda e: e.tensor_scalar(
                pairB[:, 0:2048], simblk[:, 0:2048], tcol1[:, 0:1], None, OP.is_ge))
            for jb in range(4):
                tw = [("sv", ind_v_hist)] if jb == 0 else [("sv", indT_ev)]
                for j2 in range(4):
                    jc = 4 * jb + j2
                    T(lambda e, jc=jc, j2=j2: e.transpose(
                        p_bf[:, 2048 + j2 * 128:2048 + (j2 + 1) * 128],
                        pairB[:, jc * 128:(jc + 1) * 128], idbf[:]),
                      tw if j2 == 0 else ())
                trp = E.c("st")
                indT_ev = V(lambda e, jb=jb, qoff=qoff: e.tensor_copy(
                    indT.rearrange("p (jc i) -> p jc i", i=1024)
                        [:, 4 * jb:4 * jb + 4, qoff:qoff + 128],
                    p_bf[:, 2048:2560].rearrange("p (a b) -> p a b", a=4)),
                  [("st", trp)])
            indT_done = E.c("st")
    while pa_state["idx"] < len(pa_chunks):
        emit_passA_chunk()
    passA_last_s = E.c("ss")
    topk_sim_done_v = E.c("sv")
    topk_sim_done_s = E.c("ss")

    exp_hist = [0, 0, 0, 0]
    ao_h = [0, 0, 0, 0]
    ao_last = 0
    ao_dg = 0
    score_hist = [[0, 0, 0, 0] for _ in range(10)]
    jitd = [0] * 10
    aoev_hist = 0
    aodma_hist = 0
    for quad in range(8):
        buf = quad % 2
        jw = [("sv", topk_sim_done_v)] if quad else [
            ("sv", topk_sim_done_v), ("ss", passA_last_s)]
        if quad >= 2:
            jw.append(("st", max(score_hist[quad - 2])))
        for hh in range(4):
            h = 4 * quad + hh
            dma(kjit[32 * hh:32 * hh + 2, buf * 2048:(buf + 1) * 2048],
                qks[64 + 2 * h:64 + 2 * h + 2, 0:2048], waits=jw if hh == 0 else (),
                eng="gpsimd", sem="dg", pipe=True)
            dma(qjit[32 * hh:32 * hh + 2, buf * 1024:(buf + 1) * 1024],
                qks[2 * h:2 * h + 2, 0:1024], eng="gpsimd", sem="dg", pipe=True)
        jitd[quad] = E.c("dg")
        for ic in range(NIC):
            for hh in range(4):
                for jcp in range(8):
                    for j2 in range(2):
                        jc = 2 * jcp + j2
                        wd = []
                        if j2 == 0:
                            wd = [("ss", exp_hist[hh])]
                            if jcp == 0:
                                wd.append(("dg", jitd[quad]))
                        T(lambda e, hh=hh, jc=jc, j2=j2, buf=buf, ic=ic: e.matmul(
                            p_sc[:, hh * 512 + j2 * 256: hh * 512 + (j2 + 1) * 256],
                            kjit[32 * hh:32 * hh + 2,
                                 buf * 2048 + jc * 128: buf * 2048 + (jc + 1) * 128],
                            qjit[32 * hh:32 * hh + 2,
                                 buf * 1024 + ic * 256: buf * 1024 + (ic + 1) * 256],
                            start=True, stop=True, tile_position=(32 * hh, 0)), wd)
                    scd = E.c("st")
                    hcs = 4 * quad + hh
                    ew = [("st", scd)]
                    if jcp == 0:
                        ew.append(("st", ao_h[hh]))
                    exp_hist[hh] = SC(
                        lambda e, hh=hh, jcp=jcp, hcs=hcs: e.activation(
                            Ebuf[:, hh * 4096 + jcp * 512: hh * 4096 + (jcp + 1) * 512],
                            p_sc[:, hh * 512: hh * 512 + 512], AF.Exp,
                            bias=gcon[:, 16 + hcs:17 + hcs]),
                        ew, dep=False)
                score_hist[quad][hh] = E.c("st")
            mk0 = E.c("sv")
            for hh in range(4):
                V(lambda e, hh=hh, ic=ic: e.tensor_tensor(
                    Ebuf[:, hh * 4096:(hh + 1) * 4096],
                    Ebuf[:, hh * 4096:(hh + 1) * 4096],
                    indT.rearrange("p (jc i) -> p jc i", i=1024)
                        [:, :, ic * 256:(ic + 1) * 256],
                    OP.mult), [("ss", exp_hist[hh])], dep=False)
            mk = E.c("sv")
            for hh in range(4):
                for jc in range(DC):
                    wd = []
                    if jc == 0:
                        wd = [("sv", mk), ("sv", aoev_hist)] if hh == 0 else [("sv", mk)]
                    habs = 4 * quad + hh
                    T(lambda e, hh=hh, jc=jc, habs=habs: e.matmul(
                        p_mm[32 * hh:32 * hh + 3, 0:256],
                        vaug[:, jc * 128 + 4 * habs: jc * 128 + 4 * habs + 3],
                        Ebuf[:, hh * 4096 + jc * 256: hh * 4096 + (jc + 1) * 256],
                        start=(jc == 0), stop=(jc == DC - 1),
                        tile_position=(0, 32 * hh)), wd)
                ao_h[hh] = E.c("st")
            aod = E.c("st")
            ao_last = aod
            for hh in range(4):
                V(lambda e, hh=hh, ic=ic: e.tensor_copy(
                    scr2[32 * hh:32 * hh + 3, ic * 256:(ic + 1) * 256],
                    p_mm[32 * hh:32 * hh + 3, 0:256]),
                  [("st", aod), ("dg", ao_dg)] if (hh == 0 and ic == 0)
                  else ([("st", aod)] if hh == 0 else ()), dep=False)
            aoev_hist = E.c("sv")
        for hh in range(4):
            h = 4 * quad + hh
            dma(aot[h:h + 1, 0:1024], scr2[32 * hh:32 * hh + 1, 0:1024],
                waits=[("sv", aoev_hist)] if hh == 0 else (),
                eng="gpsimd", sem="dg", pipe=True)
            dma(aot[32 + h:32 + h + 1, 0:1024],
                scr2[32 * hh + 1:32 * hh + 2, 0:1024],
                eng="gpsimd", sem="dg", pipe=True)
            dma(aden[h:h + 1, 0:1024], scr2[32 * hh + 2:32 * hh + 3, 0:1024],
                eng="gpsimd", sem="dg", pipe=True)
        ao_dg = E.c("dg")
        aodma_hist = ao_dg
    rc = V(lambda e: e.reciprocal(aden[0:32, :], aden[0:32, :]),
           [("dg", aodma_hist)])
    w1_ = dma(denr[0:32, 0:1024], aden[0:32, :],
              waits=[("sv", rc), ("ss", passA_last_s)])
    w2_ = dma(denr[32:64, 0:1024], aden[0:32, :], waits=[("dx", w1_)])
    mulv = V(lambda e: e.tensor_tensor(aot[:, :], aot[:, :], denr[0:64, 0:1024],
                                       OP.mult), [("dx", w2_)])
    aotb_v = V(lambda e: e.tensor_copy(aotb[:], aot[:]))
    w = dma(scr2[0:64, 0:2048], da["woutaT"], waits=[("sv", aotb_v)])
    woat_v = V(lambda e: e.tensor_copy(woat, scr2[0:64, 0:2048]), [("dx", w)])
    projw = woat_v
    proj_ss = E.c("ss")
    x2_hist = [0, 0]
    for dch in range(DC):
        for n2 in range(2):
            bank = (2 * dch + n2) % 2
            wd = [("sv", projw), ("ss", proj_ss)] if dch == 0 and n2 == 0 else []
            if dch > 0 or n2 > 0:
                wd.append(("sv", x2_hist[bank]))
            T(lambda e, dch=dch, n2=n2, bank=bank: e.matmul(
                p_sc[:, bank * 512:(bank + 1) * 512],
                woat[:, dch * 128:(dch + 1) * 128],
                aotb[:, n2 * 512:(n2 + 1) * 512],
                start=True, stop=True), wd + [("sv", woat_v)] if (dch == 0 and n2 == 0) else wd)
            pj = E.c("st")
            x2_hist[bank] = V(lambda e, dch=dch, n2=n2, bank=bank:
                              e.scalar_tensor_tensor(
                AR1[:, dch * 1024 + n2 * 512: dch * 1024 + (n2 + 1) * 512],
                p_sc[:, bank * 512:(bank + 1) * 512],
                gaT[:, dch:dch + 1],
                AR2[:, dch * 1024 + n2 * 512: dch * 1024 + (n2 + 1) * 512],
                OP.mult, OP.add), [("st", pj)])
    x2_v = E.c("sv")
    for dch in range(DC):
        dma(d_x2[dch * 128:(dch + 1) * 128, :], AR1[:, dch * 1024:(dch + 1) * 1024],
            waits=[("sv", x2_v)] if dch == 0 else ())
    x2_spill = E.c("dx")

    def quant_scale(qi, s_col):
        rv_ = V(lambda e, qi=qi: e.reduce_sum(
            tmpk[:, 20 + qi:21 + qi], paccum[:, qi * 128:(qi + 1) * 128],
            axis=AXX), [("ss", passA_last_s)])
        tm = T(lambda e, qi=qi: e.matmul(
            p_sc[0:1, 1536 + qi:1537 + qi], onef[:],
            tmpk[:, 20 + qi:21 + qi], start=True, stop=True), [("sv", rv_)])
        V(lambda e, qi=qi: e.tensor_copy(sclr[0:1, 4 * qi:4 * qi + 1],
                                         p_sc[0:1, 1536 + qi:1537 + qi]),
          [("st", tm)])
        V(lambda e, qi=qi: e.tensor_scalar(sclr[0:1, 4 * qi:4 * qi + 1],
                                           sclr[0:1, 4 * qi:4 * qi + 1],
                                           1.0 / (2048.0 * 8192.0), None,
                                           OP.mult))
        V(lambda e, qi=qi: e.tensor_scalar(sclr[0:1, 4 * qi + 1:4 * qi + 2],
                                           sclr[0:1, 4 * qi:4 * qi + 1],
                                           1e-8, None, OP.add))
        sc_v = V(lambda e, qi=qi: e.reciprocal(sclr[0:1, 4 * qi + 1:4 * qi + 2],
                                               sclr[0:1, 4 * qi + 1:4 * qi + 2]))
        w = dma(d_scal[:, 8 + 2 * qi:10 + 2 * qi], sclr[0:1, 4 * qi:4 * qi + 2],
                waits=[("sv", sc_v)])
        scb = dma(gcon[:, 48 + s_col:50 + s_col],
                  d_scal[:, 8 + 2 * qi:10 + 2 * qi].to_broadcast([128, 2]),
                  waits=[("dx", w)])
        return scb

    V(lambda e: e.memset(gcon[:, 50:51], MAGIC))
    V(lambda e: e.memset(gcon[:, 53:54], MAGIC))
    V(lambda e: e.memset(gcon[:, 54:55], -MAGIC))
    scb1 = quant_scale(0, 0)
    scb2 = quant_scale(1, 3)

    pb_a1 = [0, 0, 0, 0]
    pb_clamp = [0, 0, 0, 0]
    pb_sub = [0, 0, 0, 0]
    pb_odma = {"dg": 0, "ds": 0}
    scbs = {0: scb1, 1: scb2}
    browf = brow[:].bitcast(F32)
    ld_slots = (xraw[:, 0:1024], xraw[:, 1024:2048],
                browf[:, 0:1024], browf[:, 1024:2048])
    hidstf = hidst[:].bitcast(F32)
    sc_slots = (scr2[:, 0:1024], scr2[:, 1024:2048],
                hidstf[:, 0:1024], ytile[:, 0:1024])
    bi = 0
    for (srcq, dstq, qi, s_col) in ((da["wsynT"], d_tqsyn, 0, 0),
                                    (da["woutfT"], d_tqout, 1, 3)):
        for pch in range(srcq.shape[0] // 128):
            for cc2 in range(srcq.shape[1] // 1024):
                slot = bi % 4
                oslot = bi % 4
                leng, lsem = ("sync", "dx") if slot % 2 == 0 else ("scalar", "ds")
                oeng, osem = ("gpsimd", "dg")
                wl = [("ss", pb_a1[slot])]
                if bi < 4:
                    wl.append(("ss", passA_last_s))
                w = dma(ld_slots[slot],
                        srcq[pch * 128:(pch + 1) * 128,
                             cc2 * 1024:(cc2 + 1) * 1024],
                        waits=wl, eng=leng, sem=lsem, pipe=True)
                a1w = [(lsem, w), ("dx", scbs[qi]), ("sv", pb_clamp[slot])]
                pb_a1[slot] = SC(lambda e, slot=slot, s_col=s_col: e.activation(
                    sc_slots[slot], ld_slots[slot],
                    AF.Identity, bias=gcon[:, 50 + s_col:51 + s_col],
                    scale=gcon[:, 49 + s_col:50 + s_col]), a1w, dep=False)
                pb_clamp[slot] = V(lambda e, slot=slot: e.tensor_scalar(
                    sc_slots[slot], sc_slots[slot],
                    MAGIC - 1.0, MAGIC + 1.0, OP.max, OP.min),
                    [("ss", pb_a1[slot])], dep=False)
                pb_sub[oslot] = SC(lambda e, slot=slot, oslot=oslot: e.activation(
                    AR6[:, oslot * 1024:(oslot + 1) * 1024],
                    sc_slots[slot],
                    AF.Identity, bias=gcon[:, 54:55]),
                    [("sv", pb_clamp[slot]), (osem, pb_odma[osem])], dep=False)
                pb_odma[osem] = dma(
                    dstq[pch * 128:(pch + 1) * 128,
                         cc2 * 1024:(cc2 + 1) * 1024],
                    AR6[:, oslot * 1024:(oslot + 1) * 1024],
                    waits=[("ss", pb_sub[oslot])], eng=oeng, sem=osem,
                    pipe=True)
                bi += 1
    qb_done = [("dg", E.c("dg"))]
    bsyn_rows = [h1n[0:1, 0:2048], qks[0:1, 0:2048], vaug[0:1, 0:2048],
                 scr2b[0:1, 0:2048]]
    s1r_v = 0
    for a in range(4):
        w = dma(xraw[0:1, 0:2048], da["bsyn_row"][:, a * 2048:(a + 1) * 2048],
                waits=[("sv", E.c("sv")), ("dg", E.c("dg")), ("ss", E.c("ss"))])
        s1r_v = V(lambda e, a=a: e.tensor_copy(bsyn_rows[a], xraw[0:1, 0:2048]),
                  [("dx", w)])
    s1r_v = V(lambda e: e.tensor_copy(s1row[0:1, 0:128],
                                      gcon[0:1, 49:50].to_broadcast([1, 128])))

    mmh = [0] * DC
    for dch in range(DC):
        x2c = AR1[:, dch * 1024:(dch + 1) * 1024]
        cw = [("dx", x2_spill)] if dch == 0 else [("st", mmh[dch - 1])]
        cb_v = V(lambda e, x2c=x2c: e.tensor_copy(xbt[:, 0:1024], x2c), cw,
                 dep=False)
        sq_s = SC(lambda e: e.activation(xsq[:, 0:1024], xbt[:, 0:1024],
                                         AF.Square),
                  [("sv", cb_v)] + ([("st", mmh[dch - 1])] if dch else []))
        for n2 in range(2):
            T(lambda e, dch=dch, n2=n2: e.matmul(
                p_sc[0:1, 1024 + n2 * 512:1024 + (n2 + 1) * 512],
                onebc[:, 0:1],
                xbt[:, n2 * 512:(n2 + 1) * 512],
                start=(dch == 0), stop=(dch == DC - 1)),
              [("sv", cb_v)] if n2 == 0 else ())
            T(lambda e, dch=dch, n2=n2: e.matmul(
                p_mm[0:1, 1024 + n2 * 512:1024 + (n2 + 1) * 512],
                onebc[:, 0:1],
                xsq[:, n2 * 512:(n2 + 1) * 512],
                start=(dch == 0), stop=(dch == DC - 1)),
              [("ss", sq_s)] if n2 == 0 else ())
        mmh[dch] = E.c("st")
    stA = scr2[0:1, 1024:2048]
    stB = AR5[0:1, 0:1024]
    stC = AR5[0:1, 1024:2048]
    stMb = hidst[0:1, 0:1024]
    stRb = hidst[0:1, 1024:2048]
    ev2 = V(lambda e: e.tensor_scalar(stA, p_sc[0:1, 1024:2048], 1.0 / D, None,
                                      OP.mult), [("st", E.c("st"))])
    V(lambda e: e.tensor_scalar(stB, p_mm[0:1, 1024:2048], 1.0 / D, None,
                                OP.mult))
    V(lambda e: e.tensor_tensor(stC, stA, stA, OP.mult))
    V(lambda e: e.tensor_tensor(stB, stB, stC, OP.subtract))
    V(lambda e: e.tensor_scalar(stB, stB, LN_EPS, None, OP.add))
    rv = V(lambda e: e.reciprocal(stB, stB))
    rs_s = SC(lambda e: e.activation(stB, stB, AF.Sqrt), [("sv", rv)])
    cbb = V(lambda e: e.tensor_copy(stMb, stA), [("ss", rs_s)])
    cbb = V(lambda e: e.tensor_copy(stRb, stB))
    w = dma(d_rowsb[0:1, 0:1024], stMb, waits=[("sv", cbb)])
    w = dma(d_rowsb[1:2, 0:1024], stRb, waits=[("sv", cbb)])
    brfb = brf[:].bitcast(BF16)
    bw1 = dma(brfb[:, 0:1024], d_rowsb[0:1, 0:1024].to_broadcast([128, 1024]),
              waits=[("dx", w)])
    bw2 = dma(brfb[:, 1024:2048], d_rowsb[1:2, 0:1024].to_broadcast([128, 1024]),
              waits=[("dx", w)])
    mu2b = brfb[:, 0:1024]; rs2b = brfb[:, 1024:2048]
    xn2_hist = [0] * DC
    for dch in range(DC):
        x2c = AR1[:, dch * 1024:(dch + 1) * 1024]
        wd = [("dx", bw2)] if dch == 0 else [("ss", xn2_hist[dch - 1])]
        V(lambda e, x2c=x2c: e.tensor_tensor(xraw[:, 0:1024], x2c, mu2b,
                                             OP.subtract), wd)
        s2v = V(lambda e: e.tensor_tensor(xraw[:, 0:1024], xraw[:, 0:1024], rs2b,
                                          OP.mult))
        xn2_hist[dch] = SC(lambda e, dch=dch: e.activation(
            AR2b[:, dch * 1024:(dch + 1) * 1024], xraw[:, 0:1024], AF.Identity,
            bias=sb_t[:, dch:dch + 1], scale=A2[:, dch:dch + 1]), [("sv", s2v)])
    xn2_s = E.c("ss")
    xn2T = lambda dch: AR2b[:, dch * 1024:(dch + 1) * 1024]

    uP6 = AR2[:, 8192:16384]
    hidtm = AR2b[:, 16384:24576]
    xn2_all_v = E.c("sv")
    zmm_hist = [0, 0]
    zev_s = 0
    hidT_d = 0
    for tb in range(NTOKB):
        for nch in range(DC):
            for hs in range(2):
                buf = (2 * nch + hs) % 2
                wd = [("st", zmm_hist[buf])]
                if nch == 0 and hs == 0:
                    wd.append(("sv", E.c("sv")))
                if tb == 0 and nch == 0:
                    wd += [("ss", xn2_s), ("sv", s1r_v)] + qb_done
                sbuf_ = AR6[:, 0:4096] if buf == 0 else brow[:, 0:4096]
                ssem = "dx" if buf == 0 else "ds"
                w = dma(sbuf_.rearrange("p (c n) -> p c n", c=8),
                        d_tqsyn[hs * 1024:(hs + 1) * 1024,
                                nch * 512:(nch + 1) * 512]
                        .rearrange("(c p) n -> p c n", p=128), waits=wd,
                        sem=ssem, eng="sync" if buf == 0 else "scalar",
                        pipe=True)
                for mg in range(2):
                    for dc8 in range(8):
                        dch = 8 * hs + dc8
                        mw = []
                        if dc8 == 0:
                            mw = [(ssem, w)]
                            if hs == 0 and mg == 0:
                                mw.append(("ss", zev_s))
                        sb_ = AR6 if buf == 0 else brow
                        T(lambda e, sb_=sb_, mg=mg, dch=dch, dc8=dc8, tb=tb:
                          e.matmul(
                            p_sc[:, mg * 512:(mg + 1) * 512],
                            xn2T(dch)[:, tb * 256 + mg * 128:
                                       tb * 256 + (mg + 1) * 128],
                            sb_[:, dc8 * 512:(dc8 + 1) * 512],
                            start=(dch == 0), stop=False), mw)
                zmm_hist[buf] = E.c("st")
            for mg in range(2):
                T(lambda e, mg=mg, nch=nch: e.matmul(
                    p_sc[:, mg * 512:(mg + 1) * 512],
                    s1row[:],
                    bsyn_rows[nch // 4][:, (nch % 4) * 512:(nch % 4 + 1) * 512],
                    start=False, stop=True))
            zmm_hist[(2 * nch + 1) % 2] = E.c("st")
            zw = [("st", E.c("st"))]
            zev_s = SC(lambda e, nch=nch: e.activation(
                AR1[:, 0 * 8192 + nch * 512: 0 * 8192 + (nch + 1) * 512],
                p_sc[:, 0:512], AF.Identity, scale=gcon[:, 48:49]), zw,
                dep=False)
            zev_s = SC(lambda e, nch=nch: e.activation(
                AR1[:, 1 * 8192 + nch * 512: 1 * 8192 + (nch + 1) * 512],
                p_sc[:, 512:1024], AF.Identity, scale=gcon[:, 48:49]),
                dep=False)
        z_done_s = E.c("ss")
        for r2 in range(2):
            zt = AR1[:, r2 * 8192:(r2 + 1) * 8192]
            tk = topk(zt, NSYN, KFFN, Z0_FFN, tcol2, AR5b[:, 0:4096], uP6,
                      [("ss", z_done_s), ("st", E.c("st"))], sw=4096)
            V(lambda e: e.tensor_scalar(tcol2[:], tcol2[:], 1e-30, None, OP.max))
            hid_v = V(lambda e, zt=zt: e.scalar_tensor_tensor(
                hidtm[:, 0:8192], zt, tcol2[:, 0:1], zt, OP.is_ge, OP.mult))
            for sb4 in range(16):
                for j4 in range(4):
                    sc_ = 4 * sb4 + j4
                    T(lambda e, sc_=sc_, j4=j4: e.transpose(
                        p_bf[:, 2048 + j4 * 128:2048 + (j4 + 1) * 128],
                        hidtm[:, sc_ * 128:(sc_ + 1) * 128], idbf[:]),
                      [("sv", hid_v), ("dg", hidT_d)] if j4 == 0 else ())
                trp = E.c("st")
                hcp = V(lambda e, sb4=sb4: e.tensor_copy(
                    hidst[:, (sb4 % 2) * 1024:(sb4 % 2) * 1024 + 512],
                    p_bf[:, 2048:2560]), [("st", trp)], dep=False)
                hidT_d = dma(
                    d_hidT[:, tb * 256 + r2 * 128: tb * 256 + (r2 + 1) * 128]
                    .rearrange("(a b) t -> b a t", b=128)[:, 4 * sb4:4 * sb4 + 4, :],
                    hidst[:, (sb4 % 2) * 1024:(sb4 % 2) * 1024 + 512]
                    .rearrange("p (a t) -> p a t", a=4),
                    waits=[("sv", hcp)], eng="gpsimd", sem="dg", pipe=True)
        ffn_fwd_done = E.c("dg")

    ybuf = AR2b[:, 16384:32768]
    yv_hist = 0
    pd_hist = [0] * DC
    spill_dg = [0, 0]
    for half in range(2):
        hw = [("dg", ffn_fwd_done)] if half == 0 else [("st", E.c("st"))]
        hld = dma(AR1b[:, 0:32768].rearrange("p (c t) -> p c t", c=32),
                  d_hidT[half * 4096:(half + 1) * 4096, :]
                  .rearrange("(c p) t -> p c t", p=128), waits=hw, pipe=True)
        tqo_hist2 = [0, 0]
        wtq_hist = [0, 0]
        for ddch in range(DC):
            tbuf = ddch % 2
            tqb = AR6 if tbuf == 0 else brow
            tsem = "dx" if tbuf == 0 else "ds"
            wq = [("dx", hld)] if ddch < 2 else []
            if half == 0 and ddch == 0:
                wq.extend(qb_done)
            if tqo_hist2[tbuf]:
                wq.append(("st", tqo_hist2[tbuf]))
            wtq = dma(tqb[:, 0:4096].rearrange("p (c m) -> p c m", c=32),
                      d_tqout[half * 4096:(half + 1) * 4096,
                              ddch * 128:(ddch + 1) * 128]
                      .rearrange("(c p) m -> p c m", p=128), waits=wq,
                      sem=tsem, eng="sync" if tbuf == 0 else "scalar",
                      pipe=True)
            wtq_hist[tbuf] = wtq
            for n2 in range(2):
                for kch in range(32):
                    mw = []
                    if kch == 0:
                        mw = [(tsem, wtq), ("sv", yv_hist)]
                    T(lambda e, ddch=ddch, n2=n2, kch=kch, tqb=tqb: e.matmul(
                        p_mm[:, n2 * 512:(n2 + 1) * 512],
                        tqb[:, kch * 128:(kch + 1) * 128],
                        AR1b[:, kch * 1024 + n2 * 512: kch * 1024 + (n2 + 1) * 512],
                        start=(kch == 0), stop=(kch == 31)), mw)
                mmd = E.c("st")
                if half == 0:
                    pv = V(lambda e, n2=n2: e.tensor_copy(
                        scr2[:, n2 * 512:(n2 + 1) * 512],
                        p_mm[:, n2 * 512:(n2 + 1) * 512]),
                        [("st", mmd), ("dg", spill_dg[n2])], dep=False)
                    yv_hist = pv
                    pd_hist[ddch] = dma(
                        d_partial[ddch * 128:(ddch + 1) * 128,
                                  n2 * 512:(n2 + 1) * 512],
                        scr2[:, n2 * 512:(n2 + 1) * 512], waits=[("sv", pv)],
                        eng="gpsimd", sem="dg", pipe=True)
                    spill_dg[n2] = pd_hist[ddch]
                else:
                    wpl = dma(ytile[:, n2 * 512:(n2 + 1) * 512],
                              d_partial[ddch * 128:(ddch + 1) * 128,
                                        n2 * 512:(n2 + 1) * 512],
                              waits=[("dg", pd_hist[ddch]), ("sv", yv_hist)],
                              pipe=True)
                    a1 = V(lambda e, n2=n2: e.tensor_tensor(
                        scr2[:, n2 * 512:(n2 + 1) * 512],
                        p_mm[:, n2 * 512:(n2 + 1) * 512],
                        ytile[:, n2 * 512:(n2 + 1) * 512], OP.add),
                        [("st", mmd), ("dx", wpl), ("dg", spill_dg[n2])],
                        dep=False)
                    a2 = SC(lambda e, ddch=ddch, n2=n2: e.activation(
                        scr2[:, n2 * 512:(n2 + 1) * 512],
                        scr2[:, n2 * 512:(n2 + 1) * 512], AF.Identity,
                        scale=gcon[:, 51:52], bias=bfbpp[:, ddch:ddch + 1]),
                        [("sv", a1)])
                    yv_hist = V(lambda e, ddch=ddch, n2=n2: e.tensor_tensor(
                        ybuf[:, ddch * 1024 + n2 * 512: ddch * 1024 + (n2 + 1) * 512],
                        scr2[:, n2 * 512:(n2 + 1) * 512],
                        xn2T(ddch)[:, n2 * 512:(n2 + 1) * 512], OP.add),
                        [("ss", a2)], dep=False)
            tqo_hist2[tbuf] = E.c("st")
    y_done_v = E.c("sv")

    ymm = [0] * DC
    for dch in range(DC):
        for n2 in range(2):
            T(lambda e, dch=dch, n2=n2: e.matmul(
                p_sc[0:1, 1024 + n2 * 512:1024 + (n2 + 1) * 512],
                onebc[:, 0:1],
                ybuf[:, dch * 1024 + n2 * 512: dch * 1024 + (n2 + 1) * 512],
                start=(dch == 0), stop=(dch == DC - 1)),
              [("sv", y_done_v)] if (dch == 0 and n2 == 0) else ())
        sqv = SC(lambda e, dch=dch: e.activation(
            xsq[:, 0:1024], ybuf[:, dch * 1024:(dch + 1) * 1024], AF.Square),
            [("sv", y_done_v)] if dch == 0 else [("st", ymm[dch - 1])])
        for n2 in range(2):
            T(lambda e, dch=dch, n2=n2: e.matmul(
                p_mm[0:1, 1024 + n2 * 512:1024 + (n2 + 1) * 512],
                onebc[:, 0:1], xsq[:, n2 * 512:(n2 + 1) * 512],
                start=(dch == 0), stop=(dch == DC - 1)),
              [("ss", sqv)] if n2 == 0 else ())
        ymm[dch] = E.c("st")
    stA2 = scr2[0:1, 0:1024]
    stB2 = scr2[0:1, 1024:2048]
    stC2 = AR5[0:1, 0:1024]
    stMb2 = hidst[0:1, 0:1024]
    stRb2 = hidst[0:1, 1024:2048]
    ev3 = V(lambda e: e.tensor_scalar(stA2, p_sc[0:1, 1024:2048], 1.0 / D, None,
                                      OP.mult), [("st", E.c("st"))])
    V(lambda e: e.tensor_scalar(stB2, p_mm[0:1, 1024:2048], 1.0 / D, None,
                                OP.mult))
    V(lambda e: e.tensor_tensor(stC2, stA2, stA2, OP.mult))
    V(lambda e: e.tensor_tensor(stB2, stB2, stC2, OP.subtract))
    V(lambda e: e.tensor_scalar(stB2, stB2, LN_EPS, None, OP.add))
    rv = V(lambda e: e.reciprocal(stB2, stB2))
    rsy_s = SC(lambda e: e.activation(stB2, stB2, AF.Sqrt), [("sv", rv)])
    cbb = V(lambda e: e.tensor_copy(stMb2, stA2), [("ss", rsy_s)])
    cbb = V(lambda e: e.tensor_copy(stRb2, stB2))
    w = dma(d_rowsb[0:1, 0:1024], stMb2, waits=[("sv", cbb)])
    w = dma(d_rowsb[1:2, 0:1024], stRb2, waits=[("sv", cbb)])
    brfb = brf[:].bitcast(BF16)
    bw1 = dma(brfb[:, 0:1024], d_rowsb[0:1, 0:1024].to_broadcast([128, 1024]),
              waits=[("dx", w)])
    bw2 = dma(brfb[:, 1024:2048], d_rowsb[1:2, 0:1024].to_broadcast([128, 1024]),
              waits=[("dx", w)])
    muyb = brfb[:, 0:1024]; rsyb = brfb[:, 1024:2048]
    x3v = 0
    outd = 0
    for dch in range(DC):
        wd = [("dx", bw2), ("sv", x3v)]
        wx2 = dma(x2l[:, 0:1024], d_x2[dch * 128:(dch + 1) * 128, :], waits=wd)
        V(lambda e, dch=dch: e.tensor_tensor(scr2[:, 0:1024],
                                             ybuf[:, dch * 1024:(dch + 1) * 1024],
                                             muyb, OP.subtract),
          [("dx", bw2), ("st", E.c("st"))])
        s3 = V(lambda e: e.tensor_tensor(scr2[:, 0:1024], scr2[:, 0:1024], rsyb,
                                         OP.mult))
        fs = SC(lambda e, dch=dch: e.activation(
            scr2[:, 1024:2048], scr2[:, 0:1024], AF.Identity,
            bias=bfpp[:, dch:dch + 1], scale=gfpp[:, dch:dch + 1]), [("sv", s3)])
        x3v = V(lambda e, dch=dch: e.scalar_tensor_tensor(
            scr2[:, 0:1024], scr2[:, 1024:2048], gbT[:, dch:dch + 1],
            x2l[:, 0:1024],
            OP.mult, OP.add), [("ss", fs), ("dx", wx2)])
        for t4 in range(2):
            tev = [("sv", x3v)] if t4 == 0 else [("sv", E.c("sv"))]
            if dch > 0 and t4 == 0:
                tev.append(("dout", outd))
            for j4 in range(4):
                tc = 4 * t4 + j4
                T(lambda e, tc=tc, j4=j4: e.transpose(
                    p_sc[:, 1024 + j4 * 128:1024 + (j4 + 1) * 128],
                    scr2[:, tc * 128:(tc + 1) * 128], idf2[:]),
                  tev if j4 == 0 else ())
            trp = E.c("st")
            V(lambda e, t4=t4: e.tensor_copy(otile[:, t4 * 512:(t4 + 1) * 512],
                                             p_sc[:, 1024:1536]), [("st", trp)],
              dep=False)
        oev = E.c("sv")
        outd = dma(out_tm[:, dch * 128:(dch + 1) * 128]
                   .rearrange("(a p) m -> p a m", p=128),
                   otile[:, 0:1024].rearrange("p (a m) -> p a m", m=128),
                   waits=[("sv", oev)], sem="dout")
    E.wait("sync", "dout", E.c("dout"))

    @block.tensor
    def _(e):
        for f in E.q["tensor"]:
            f(e)

    @block.sync
    def _(e):
        for f in E.q["sync"]:
            f(e)

    @block.vector
    def _(e):
        for f in E.q["vector"]:
            f(e)

    @block.scalar
    def _(e):
        for f in E.q["scalar"]:
            f(e)

    @block.gpsimd
    def _(e):
        for f in E.q["gpsimd"]:
            f(e)

    stack.close()
    return nc


def _host_prep(inputs, core):
    b, half = core // 2, core % 2
    qi0 = half * TI
    f32 = lambda a: np.ascontiguousarray(a, dtype=np.float32)
    bf = lambda a: np.ascontiguousarray(a).astype(ml_dtypes.bfloat16)
    x = np.asarray(inputs["L0_attn_x"], np.float32)[b]
    if half == 0:
        xp = x
    else:
        xp = np.concatenate([x[TI:], x[:TI]], axis=0)
    Wqkv = np.asarray(inputs["L0_attn_Wqkv"], np.float32)
    bqkv = np.asarray(inputs["L0_attn_bqkv"], np.float32)
    W3 = np.asarray(inputs["L0_attn_W3"], np.float32)
    w3blk = np.zeros((128, 2), np.float32)
    w3blk[0:64, 0] = W3[0]
    w3blk[64:128, 1] = W3[0]
    b2 = np.asarray(inputs["L0_attn_b2"], np.float32)
    iota16 = np.broadcast_to(np.arange(16, dtype=np.float32)[None, :], (128, 16))
    vauginit = np.zeros((128, 128), np.float32)
    for h in range(32):
        vauginit[:, 4 * h + 2] = 1.0
    ident = np.eye(128, dtype=np.float32)
    sc12 = np.ones((128, 1), np.float32)
    sc12[0:64, 0] = ISQ2
    pp = lambda v: np.ascontiguousarray(
        np.asarray(v, np.float32).reshape(DC, 128).T)
    d = {
        "x_fm": f32(xp.T),
        "w1T": f32(np.asarray(inputs["L0_attn_W1"], np.float32).T),
        "b1c": f32(np.asarray(inputs["L0_attn_b1"], np.float32)[:, None]),
        "w2T": f32(np.asarray(inputs["L0_attn_W2"], np.float32).T),
        "b2st": f32(np.concatenate([b2, b2])[:, None]),
        "w3blk": f32(w3blk),
        "b3s": f32(np.asarray(inputs["L0_attn_b3"], np.float32).reshape(1, 1)),
        "wqkT": f32(Wqkv[0:128].T),
        "bqk_row": f32(bqkv[0:128][None, :]),
        "wvT": f32(Wqkv[128:192].T),
        "bv_row": f32(bqkv[128:192][None, :]),
        "woutaT": f32(np.asarray(inputs["L0_attn_Wout"], np.float32).T[
            np.array([2 * h + c for c in (0, 1) for h in range(32)])]),
        "bout_pp": pp(inputs["L0_attn_bout"]),
        "wsynT": f32(np.asarray(inputs["L0_ffn_W_syn"], np.float32).T),
        "bsyn_row": f32(np.asarray(inputs["L0_ffn_b_syn"], np.float32)[None, :]),
        "woutfT": f32(np.asarray(inputs["L0_ffn_W_out"], np.float32).T),
        "boutf_pp": pp(inputs["L0_ffn_b_out"]),
        "gammaA_pp": pp(inputs["L0_attn_gamma"]),
        "betaA_pp": pp(inputs["L0_attn_beta"]),
        "gammaF_pp": pp(inputs["L0_ffn_gamma"]),
        "betaF_pp": pp(inputs["L0_ffn_beta"]),
        "c_pp": pp(np.asarray(inputs["L0_c"], np.float32)[b]),
        "wmod": f32(np.asarray(inputs["L0_wmod"], np.float32)),
        "vauginit": bf(vauginit),
        "idt_bf": bf(ident),
        "idt_f32": f32(ident),
        "iota16_in": f32(iota16),
        "ones_bf_row": bf(np.ones((1, 512), np.float32)),
        "ones_f32_col": f32(np.ones((128, 1), np.float32)),
        "sc12": sc12,
    }
    return d


_NC_CACHE = None


def kernel(**inputs):
    global _NC_CACHE
    if _NC_CACHE is None:
        _NC_CACHE = build_nc()
    nc = _NC_CACHE
    in_maps = [_host_prep(inputs, c) for c in range(8)]
    res = run_bass_kernel_spmd(nc, in_maps, core_ids=list(range(8)))
    out = np.zeros((4, S, D), np.float32)
    for c in range(8):
        b, half = c // 2, c % 2
        out[b, half * TI:(half + 1) * TI, :] = res.results[c]["out_tm"]
    return out


# version 16
# speedup vs baseline: 1.1021x; 1.1021x over previous
import math
import numpy as np
import ml_dtypes
import concourse.bass as bass
import concourse.mybir as mybir
from concourse.bass_utils import run_bass_kernel_spmd
from concourse.alu_op_type import AluOpType as OP

F32 = mybir.dt.float32
F32R = mybir.dt.float32r
BF16 = mybir.dt.bfloat16
AF = mybir.ActivationFunctionType
AXX = mybir.AxisListType.X

D = 2048
S = 2048
TI = 1024
DC = 16
NSYN = 8192
KSIM = 204
KFFN = 819
LN_EPS = 1e-5
MAGIC = 12582912.0
NSEC = 6
IC = 256
NIC = TI // IC
TOKB = 256
NTOKB = TI // TOKB
Z0_SIM = 1.2872
Z0_FFN = 1.2816
ISQ2 = 1.0 / math.sqrt(2.0)
BIGM = 1.0e7


class Emit:
    ENGS = ("sync", "vector", "scalar", "tensor", "gpsimd")

    def __init__(self):
        self.q = {e: [] for e in self.ENGS}
        self.cnt = {}
        self.handles = {}

    def sem(self, name):
        self.cnt.setdefault(name, 0)

    def op(self, eng, fn, inc=None, waits=()):
        waits = tuple((s, v) for (s, v) in waits if v > 0)

        def cl(e, fn=fn, inc=inc, waits=waits):
            for (s, v) in waits:
                e.wait_ge(self.handles[s], v)
            inst = fn(e)
            if inc is not None and inst is not None:
                inst.then_inc(self.handles[inc[0]], inc[1])

        self.q[eng].append(cl)
        if inc is not None:
            self.cnt[inc[0]] += inc[1]

    def wait(self, eng, sem, val):
        if val > 0:
            def cl(e, sem=sem, val=val):
                e.wait_ge(self.handles[sem], val)
            self.q[eng].append(cl)

    def c(self, name):
        return self.cnt.get(name, 0)


def build_nc():
    nc = bass.Bass("TRN2", target_bir_lowering=False,
                   detect_race_conditions=False)
    E = Emit()
    SEMS = ("dx", "dg", "sv", "ss", "st", "sg", "dout", "dt", "ds", "scc")
    for s_ in SEMS:
        E.sem(s_)

    da = {}
    def din(name, shape, dt=F32):
        da[name] = nc.dram_tensor(name, list(shape), dt, kind="ExternalInput").ap()

    din("x_fm", (D, S))
    din("w1T", (D, 128)); din("b1c", (128, 1))
    din("w2T", (128, 64)); din("b2st", (128, 1)); din("w3blk", (128, 2))
    din("b3s", (1, 1))
    din("wqkT", (D, 128)); din("bqk_row", (1, 128))
    din("wvT", (D, 64)); din("bv_row", (1, 64))
    din("woutaT", (64, D)); din("bout_pp", (128, DC))
    din("wsynT", (D // 8, NSYN)); din("bsyn_row", (1, NSYN))
    din("woutfT", (NSYN // 8, D)); din("boutf_pp", (128, DC))
    din("gammaA_pp", (128, DC)); din("betaA_pp", (128, DC))
    din("gammaF_pp", (128, DC)); din("betaF_pp", (128, DC))
    din("c_pp", (128, DC))
    din("wmod", (D, 6 * D))
    din("vauginit", (128, 128), BF16)
    din("idt_bf", (128, 128), BF16)
    din("idt_f32", (128, 128))
    din("iota16_in", (128, 16))
    din("ones_bf_row", (1, 512), BF16)
    din("ones_f32_col", (128, 1))
    din("sc12", (128, 1))
    out_tm = nc.dram_tensor("out_tm", [TI, D], F32, kind="ExternalOutput").ap()

    d_rows = nc.dram_tensor("d_rows", [16, S], F32, kind="Internal").ap()
    d_mod = nc.dram_tensor("d_mod", [1, 6 * D], F32, kind="Internal").ap()
    d_scal = nc.dram_tensor("d_scal", [1, 64], F32, kind="Internal").ap()
    d_cs = nc.dram_tensor("d_cs", [1, 32], F32, kind="Internal").ap()
    d_col = nc.dram_tensor("d_col", [1, 128], F32, kind="Internal").ap()
    d_tqsyn = nc.dram_tensor("d_tqsyn", [D, NSYN], BF16, kind="Internal",
                             addr_space="Shared").ap()
    d_tqout = nc.dram_tensor("d_tqout", [NSYN, D], BF16, kind="Internal",
                             addr_space="Shared").ap()
    d_tqsyn_sl = nc.dram_tensor("d_tqsyn_sl", [D // 8, NSYN], BF16,
                                kind="Internal").ap()
    d_tqout_sl = nc.dram_tensor("d_tqout_sl", [NSYN // 8, D], BF16,
                                kind="Internal").ap()
    d_spart = nc.dram_tensor("d_spart", [1, 2], F32, kind="Internal").ap()
    d_ssum = nc.dram_tensor("d_ssum", [1, 2], F32, kind="Internal",
                            addr_space="Shared").ap()
    d_hidT = nc.dram_tensor("d_hidT", [NSYN, TI], BF16, kind="Internal").ap()
    d_x2 = nc.dram_tensor("d_x2", [D, TI], F32, kind="Internal").ap()
    d_partial = nc.dram_tensor("d_partial", [D, TI], F32, kind="Internal").ap()
    d_rowsb = nc.dram_tensor("d_rowsb", [4, S], BF16, kind="Internal").ap()

    import contextlib
    stack = contextlib.ExitStack()
    def sbt(name, p, f, dt):
        return stack.enter_context(nc.sbuf_tensor(name, [p, f], dt))

    AR1 = sbt("AR1", 128, 16384, F32)
    AR2 = sbt("AR2", 128, 16384, F32)
    AR5 = sbt("AR5", 128, 2048, F32)
    AR6 = sbt("AR6", 128, 4096, BF16)
    xraw = sbt("xraw", 128, 2048, F32)
    xbt = sbt("xbt", 128, 2048, BF16)
    xsq = sbt("xsq", 128, 2048, BF16)
    brow = sbt("brow", 128, 4096, BF16)
    brf = sbt("brf", 128, 1024, F32)
    h1n = sbt("h1n", 128, 2048, BF16)
    qks = sbt("qks", 128, 2048, BF16)
    vaug = sbt("vaug", 128, DC * 128, BF16)
    w2t = sbt("w2t", 128, 64, BF16)
    w3b = sbt("w3b", 128, 2, BF16)
    hidst = sbt("hidst", 128, 2048, BF16)
    scr2 = sbt("scr2", 128, 2048, F32)
    stw = sbt("stw", 128, 4 * DC, BF16)
    stw2 = sbt("stw2", 128, 2 * DC, BF16)
    modp = sbt("modp", 128, 96, F32)
    mcon = sbt("mcon", 128, 112, F32)
    gcon = sbt("gcon", 128, 64, F32)
    ppst = sbt("ppst", 128, 12 * DC, F32)
    ppbf = sbt("ppbf", 128, 64, BF16)
    tmpk = sbt("tmpk", 128, 32, F32)
    iota16 = sbt("iota16", 128, 16, F32)
    sel16 = sbt("sel16", 128, 16, F32)
    cand16 = sbt("cand16", 128, 16, F32)
    tcol1 = sbt("tcol1", 128, 1, F32)
    tcol2 = sbt("tcol2", 128, 1, F32)
    idbf = sbt("idbf", 128, 128, BF16)
    idf2 = sbt("idf2", 128, 128, F32)
    onebf = sbt("onebf", 1, 512, BF16)
    onef = sbt("onef", 128, 1, F32)
    onebc = sbt("onebc", 128, 1, BF16)
    srow = sbt("srow", 1, 256, F32)
    scpp = sbt("scpp", 128, DC, F32)
    scrr = sbt("scrr", 128, DC, F32R)
    b1cs = sbt("b1cs", 128, 1, F32)
    b2cs = sbt("b2cs", 128, 1, F32)
    sc12s = sbt("sc12s", 128, 1, F32)
    bobpp = sbt("bobpp", 128, DC, F32)
    bfbpp = sbt("bfbpp", 128, DC, F32)
    gapp = sbt("gapp", 128, DC, F32)
    bapp = sbt("bapp", 128, DC, F32)
    gfpp = sbt("gfpp", 128, DC, F32)
    bfpp = sbt("bfpp", 128, DC, F32)
    bqkr = sbt("bqkr", 1, 128, BF16)
    bvr = sbt("bvr", 1, 64, BF16)
    s1row = sbt("s1row", 1, 128, BF16)
    sclr = sbt("sclr", 1, 16, F32)
    wmodr = sbt("wmodr", 128, 1024, F32R)
    psum_t = stack.enter_context(nc.psum_tensor("p_all", [128, 4096], F32))
    for s_ in SEMS:
        E.handles[s_] = stack.enter_context(nc.semaphore(s_))
    stack.enter_context(nc.allow_non_contiguous_dma(
        reason="small pp-layout rearrange DMAs (KB-scale)"))
    block = stack.enter_context(nc.Block())

    p_all = psum_t[:]
    p_bf = psum_t[:].bitcast(BF16)
    p_mm = psum_t[:, 0:2048]
    p_sc = psum_t[:, 2048:4096]
    AR1b = AR1[:].bitcast(BF16)
    AR2b = AR2[:].bitcast(BF16)
    AR5b = AR5[:].bitcast(BF16)
    wtr = AR5[:]
    mrow = AR5[:]
    wgt = brow[:, 2048:4096]
    h1b = brf[:]
    kjit = AR6[:]
    qjit = xbt[:]
    utile = hidst[:]
    woat = AR6[0:64, 0:2048]
    aotb = hidst[0:64, 0:1024]
    aot = scr2[0:64, 1024:2048]
    aden = scr2[64:128, 1024:2048]
    rowsb = scr2[0:8, 0:2048]
    denr = xraw[0:64, 0:1024]
    x2l = xraw[:, 0:1024]
    ytile = qks[:].bitcast(F32)
    otile = xbt[:].bitcast(F32)
    scr2b = scr2[:].bitcast(BF16)
    bsyn_rows = None

    def dma(dst, src, waits=(), eng="sync", sem="dx", pipe=False):
        waits = list(waits)
        if not pipe:
            waits.append((sem, E.c(sem)))
        E.op(eng, lambda e, d=dst, s=src: e.dma_start(out=d, in_=s), inc=(sem, 16),
             waits=waits)
        return E.c(sem)

    def V(fn, waits=(), dep=True):
        if dep:
            def fn2(e, fn=fn):
                e.drain()
                return fn(e)
        else:
            fn2 = fn
        E.op("vector", fn2, inc=("sv", 1), waits=waits)
        return E.c("sv")

    def SC(fn, waits=(), dep=True):
        if dep:
            def fn2(e, fn=fn):
                e.drain()
                return fn(e)
        else:
            fn2 = fn
        E.op("scalar", fn2, inc=("ss", 1), waits=waits)
        return E.c("ss")

    def T(fn, waits=()):
        E.op("tensor", fn, inc=("st", 1), waits=waits)
        return E.c("st")

    def G(fn, waits=()):
        E.op("gpsimd", fn, inc=("sg", 1), waits=waits)
        return E.c("sg")

    dma(idbf[:], da["idt_bf"])
    dma(idf2[:], da["idt_f32"])
    dma(iota16[:], da["iota16_in"])
    dma(onebf[:], da["ones_bf_row"])
    dma(onef[:], da["ones_f32_col"])
    dma(gcon[:, 0:1], da["b3s"].to_broadcast([128, 1]))
    dma(scpp[:, 0:DC], da["c_pp"])
    dma(b1cs[:], da["b1c"])
    dma(b2cs[:], da["b2st"])
    dma(sc12s[:], da["sc12"])
    dma(bobpp[:], da["bout_pp"])
    dma(bfbpp[:], da["boutf_pp"])
    dma(gapp[:], da["gammaA_pp"])
    dma(bapp[:], da["betaA_pp"])
    dma(gfpp[:], da["gammaF_pp"])
    dma(bfpp[:], da["betaF_pp"])
    dma(wtr[:, 0:64], da["w2T"])
    dma(wtr[:, 64:66], da["w3blk"])
    dma(wtr[0:1, 128:256], da["bqk_row"])
    dma(wtr[0:1, 256:320], da["bv_row"])
    ld0 = E.c("dx")
    V(lambda e: e.tensor_copy(w2t[:], wtr[:, 0:64]), [("dx", ld0)])
    V(lambda e: e.tensor_copy(w3b[:], wtr[:, 64:66]))
    V(lambda e: e.tensor_copy(bqkr[:], wtr[0:1, 128:256]))
    V(lambda e: e.tensor_copy(bvr[:], wtr[0:1, 256:320]))
    V(lambda e: e.tensor_copy(onebc[:], onef[:]))
    sm0 = E.c("sv")
    SC(lambda e: e.activation(scpp[:, 0:DC], scpp[:, 0:DC], AF.Silu), [("dx", ld0)])
    silu_v = V(lambda e: e.tensor_copy(scrr[:], scpp[:, 0:DC]), [("ss", E.c("ss"))])

    consts_v = E.c("sv")

    ev_prev = 0
    rdy_hist = [0, 0]
    last_mm = 0
    for cg in range(3):
        mm_hist = [0] * DC
        for dch in range(DC):
            slot = dch % 2
            wdeps = [("sv", silu_v)] if (cg == 0 and dch == 0) else []
            if rdy_hist[slot]:
                wdeps.append(("sv", rdy_hist[slot]))
            w = dma(AR1[:, slot * 4096:(slot + 1) * 4096],
                    da["wmod"][dch * 128:(dch + 1) * 128, cg * 4096:(cg + 1) * 4096],
                    waits=wdeps, pipe=True)
            for pc4 in range(4):
                rdeps = [("dx", w)] if pc4 == 0 else []
                if last_mm:
                    rdeps.append(("st", last_mm))
                rdy = V(lambda e, slot=slot, pc4=pc4: e.tensor_copy(
                    wmodr[:, 0:1024],
                    AR1[:, slot * 4096 + pc4 * 1024:
                           slot * 4096 + (pc4 + 1) * 1024]), rdeps)
                for n2 in range(2):
                    nch = 2 * pc4 + n2
                    mmw = [("sv", rdy)] if n2 == 0 else []
                    if cg > 0 and dch == 0 and nch == 0:
                        mmw.append(("sv", ev_prev))
                    T(lambda e, nch=nch, n2=n2, dch=dch: e.matmul(
                        p_all[0:1, nch * 512:(nch + 1) * 512],
                        scrr[:, dch:dch + 1],
                        wmodr[:, n2 * 512:(n2 + 1) * 512],
                        start=(dch == 0), stop=(dch == DC - 1)), mmw)
                last_mm = E.c("st")
            rdy_hist[slot] = rdy
            mm_hist[dch] = E.c("st")
        V(lambda e: e.tensor_copy(mrow[0:1, 0:2048], p_all[0:1, 0:2048]),
          [("st", E.c("st"))])
        ev_prev = V(lambda e: e.tensor_copy(scr2[0:1, 0:2048], p_all[0:1, 2048:4096]))
        dma(d_mod[:, cg * 4096:cg * 4096 + 2048], mrow[0:1, 0:2048],
            waits=[("sv", ev_prev)])
        dma(d_mod[:, cg * 4096 + 2048:(cg + 1) * 4096], scr2[0:1, 0:2048],
            waits=[("sv", ev_prev)])
    mod_d = E.c("dx")
    for m_ in range(6):
        w = dma(modp[:, m_ * 16:(m_ + 1) * 16],
                d_mod[:, m_ * 2048:(m_ + 1) * 2048]
                .rearrange("o (k p) -> (o p) k", k=16),
                waits=[("dx", mod_d)] if m_ == 0 else ())
    mod_pp = w

    sa_t = modp[:, 0:16]; ca_t = modp[:, 16:32]; gaw = modp[:, 32:48]
    sb_t = modp[:, 48:64]; cb_t = modp[:, 64:80]; gbw = modp[:, 80:96]
    A1 = mcon[:, 0:16]; A2 = mcon[:, 16:32]
    gaT = mcon[:, 32:48]; gbT = mcon[:, 48:64]
    A1sq = mcon[:, 64:80]; A1B1 = mcon[:, 80:96]; B1sq = mcon[:, 96:112]
    V(lambda e: e.tensor_scalar(A1, ca_t, 1.0, None, OP.add), [("dx", mod_pp)])
    V(lambda e: e.tensor_scalar(A2, cb_t, 1.0, None, OP.add))
    for (gsrc, gdst) in ((gaw, gaT), (gbw, gbT)):
        V(lambda e, gs=gsrc, gd=gdst: e.tensor_scalar(gd, gs, 0.0, 2.0, OP.is_ge,
                                                      OP.mult))
        V(lambda e, gd=gdst: e.tensor_scalar(gd, gd, 1.0, None, OP.subtract))
        V(lambda e, gs=gsrc: e.tensor_scalar(tmpk[:, 0:16], gs, -1.0, None, OP.mult))
        V(lambda e, gs=gsrc: e.tensor_tensor(tmpk[:, 0:16], tmpk[:, 0:16], gs, OP.max))
        V(lambda e: e.tensor_scalar(tmpk[:, 0:16], tmpk[:, 0:16], 0.25, None, OP.max))
        V(lambda e, gd=gdst: e.tensor_tensor(gd, gd, tmpk[:, 0:16], OP.mult))
    V(lambda e: e.tensor_tensor(A1sq, A1, A1, OP.mult))
    V(lambda e: e.tensor_tensor(A1B1, A1, sa_t, OP.mult))
    V(lambda e: e.tensor_tensor(B1sq, sa_t, sa_t, OP.mult))
    stwv = stw[:].rearrange("p (k f) -> p k f", f=4)
    stw2v = stw2[:].rearrange("p (k f) -> p k f", f=2)
    V(lambda e: e.memset(stwv[:, :, 0:1], 1.0))
    V(lambda e: e.tensor_copy(stwv[:, :, 1:2], A1.unsqueeze(2)))
    V(lambda e: e.tensor_copy(stwv[:, :, 2:3], A1sq.unsqueeze(2)))
    V(lambda e: e.tensor_copy(stwv[:, :, 3:4], A1B1.unsqueeze(2)))
    V(lambda e: e.memset(stw2v[:, :, 0:1], 1.0))
    drv_v = V(lambda e: e.tensor_copy(stw2v[:, :, 1:2], A1sq.unsqueeze(2)))
    m1 = T(lambda e: e.matmul(p_sc[0:1, 0:112], onef[:], mcon[:, 0:112],
                              start=True, stop=True), [("sv", drv_v)])
    m2 = T(lambda e: e.matmul(p_sc[0:1, 112:128], onef[:], sa_t,
                              start=True, stop=True))
    V(lambda e: e.tensor_copy(srow[0:1, 0:128], p_sc[0:1, 0:128]), [("st", m2)])
    for i, off in enumerate((0, 64, 80, 96, 112)):
        V(lambda e, off=off, i=i: e.reduce_sum(srow[0:1, 140 + i:141 + i],
                                               srow[0:1, off:off + 16], axis=AXX))
    cscal_v = V(lambda e: e.tensor_scalar(srow[0:1, 145:146], srow[0:1, 144:145],
                                          1.0 / D, None, OP.mult))
    w = dma(d_scal[:, 0:6], srow[0:1, 140:146], waits=[("sv", cscal_v)])
    gcon_scal = dma(gcon[:, 8:14], d_scal[:, 0:6].to_broadcast([128, 6]),
                    waits=[("dx", w)])

    cast_hist = [0] * DC
    sq_hist = [0] * DC
    mm_hist = [0] * DC
    for dch in range(DC):
        wd = [("sv", E.c("sv")), ("ss", E.c("ss"))]
        if dch == 0:
            wd.append(("dx", gcon_scal))
        w = dma(xraw[:, 0:2048], da["x_fm"][dch * 128:(dch + 1) * 128, :], waits=wd,
                pipe=True)
        cd = [("dx", w)]
        if dch >= 1:
            cd.append(("st", mm_hist[dch - 1]))
        cast_hist[dch] = V(lambda e: e.tensor_copy(xbt[:, 0:2048], xraw[:, 0:2048]),
                           cd)
        sd = [("sv", cast_hist[dch])]
        if dch >= 1:
            sd.append(("st", mm_hist[dch - 1]))
        sq_hist[dch] = SC(lambda e: e.activation(xsq[:, 0:2048], xbt[:, 0:2048],
                                                 AF.Square), sd)
        for nch in range(4):
            T(lambda e, dch=dch, nch=nch: e.matmul(
                p_sc[0:4, nch * 512:(nch + 1) * 512],
                stw[:, 4 * dch:4 * dch + 4],
                xbt[:, nch * 512:(nch + 1) * 512],
                start=(dch == 0), stop=(dch == DC - 1)),
              [("sv", cast_hist[dch])] if nch == 0 else ())
            T(lambda e, dch=dch, nch=nch: e.matmul(
                p_mm[0:2, nch * 512:(nch + 1) * 512],
                stw2[:, 2 * dch:2 * dch + 2],
                xsq[:, nch * 512:(nch + 1) * 512],
                start=(dch == 0), stop=(dch == DC - 1)),
              [("ss", sq_hist[dch])] if nch == 0 else ())
        mm_hist[dch] = E.c("st")
    V(lambda e: e.tensor_copy(rowsb[0:4, :], p_sc[0:4, 0:2048]), [("st", E.c("st"))])
    ev_rows = V(lambda e: e.tensor_copy(AR5[0:2, 0:2048], p_mm[0:2, 0:2048]))
    w = dma(d_rows[0:4, :], rowsb[0:4, :], waits=[("sv", ev_rows)])
    w = dma(d_rows[4:6, :], AR5[0:2, 0:2048], waits=[("sv", ev_rows)])
    for r_ in range(6):
        pp_d = dma(ppst[:, r_ * 16:(r_ + 1) * 16],
                   d_rows[r_:r_ + 1, :].rearrange("r (k p) -> (r p) k", k=16),
                   waits=[("dx", w)] if r_ == 0 else ())

    B_ = lambda i: ppst[:, i * 16:(i + 1) * 16]
    Sx, SAx, SA2x, SABx, Sxx, SA2xx = B_(0), B_(1), B_(2), B_(3), B_(4), B_(5)
    mu1, rs1, muh, rsh, t1_, t2_ = B_(6), B_(7), B_(8), B_(9), B_(10), B_(11)
    gsc = lambda i: gcon[:, 8 + i:9 + i]
    V(lambda e: e.tensor_scalar(mu1, Sx, 1.0 / D, None, OP.mult), [("dx", pp_d)])
    V(lambda e: e.tensor_tensor(t1_, mu1, mu1, OP.mult))
    V(lambda e: e.tensor_scalar(t2_, Sxx, 1.0 / D, None, OP.mult))
    V(lambda e: e.tensor_tensor(t2_, t2_, t1_, OP.subtract))
    V(lambda e: e.tensor_scalar(t2_, t2_, LN_EPS, None, OP.add))
    rv = V(lambda e: e.reciprocal(t2_, t2_))
    rs1_s = SC(lambda e: e.activation(rs1, t2_, AF.Sqrt), [("sv", rv)])
    V(lambda e: e.scalar_tensor_tensor(t1_, mu1, gsc(0), SAx, OP.mult, OP.subtract),
      [("ss", rs1_s)])
    V(lambda e: e.tensor_tensor(t1_, t1_, rs1, OP.mult))
    V(lambda e: e.tensor_scalar(t1_, t1_, -1.0 / D, None, OP.mult))
    V(lambda e: e.tensor_scalar(muh, t1_, gsc(5), None, OP.add))
    V(lambda e: e.tensor_tensor(t1_, mu1, SA2x, OP.mult))
    V(lambda e: e.scalar_tensor_tensor(t1_, t1_, -2.0, SA2xx, OP.mult, OP.add))
    V(lambda e: e.tensor_tensor(t2_, mu1, mu1, OP.mult))
    V(lambda e: e.tensor_scalar(t2_, t2_, gsc(1), None, OP.mult))
    V(lambda e: e.tensor_tensor(t1_, t1_, t2_, OP.add))
    V(lambda e: e.tensor_tensor(t2_, rs1, rs1, OP.mult))
    V(lambda e: e.tensor_tensor(t1_, t1_, t2_, OP.mult))
    V(lambda e: e.scalar_tensor_tensor(t2_, mu1, gsc(2), SABx, OP.mult, OP.subtract))
    V(lambda e: e.scalar_tensor_tensor(t2_, t2_, -2.0, rs1, OP.mult, OP.mult))
    V(lambda e: e.tensor_tensor(t1_, t1_, t2_, OP.add))
    V(lambda e: e.tensor_scalar(t1_, t1_, gsc(3), None, OP.add))
    V(lambda e: e.tensor_scalar(t1_, t1_, 1.0 / D, None, OP.mult))
    V(lambda e: e.tensor_tensor(t2_, muh, muh, OP.mult))
    V(lambda e: e.tensor_tensor(t1_, t1_, t2_, OP.subtract))
    V(lambda e: e.tensor_scalar(t1_, t1_, LN_EPS, None, OP.add))
    rv = V(lambda e: e.reciprocal(t1_, t1_))
    rsh_s = SC(lambda e: e.activation(rsh, t1_, AF.Sqrt), [("sv", rv)])
    ppc = V(lambda e: e.tensor_copy(ppbf[:].rearrange("p (r k) -> p r k", r=4),
                                    ppst[:].rearrange("p (r k) -> p r k", r=12)
                                    [:, 6:10, :]), [("ss", rsh_s)])
    for r_ in range(4):
        w = dma(d_rowsb[r_:r_ + 1, :].rearrange("r (k p) -> (r p) k", k=16),
                ppbf[:, r_ * 16:(r_ + 1) * 16], waits=[("sv", ppc)] if r_ == 0 else ())
    mu1b = brow[:, 0:2048]; rs1b = brow[:, 2048:4096]
    muhb = vaug[:, 0:2048]; rshb = qks[:, 0:2048]
    brow_v = 0
    for i, dst in enumerate((mu1b, rs1b, muhb, rshb)):
        brow_v = dma(dst, d_rowsb[i:i + 1, :].to_broadcast([128, 2048]),
                     waits=[("dx", w)])
    E.wait("vector", "dx", brow_v)
    brow_v_sem = ("dx", brow_v)

    gA1 = mcon[:, 64:80]
    gB1o = mcon[:, 80:96]
    V(lambda e: e.tensor_tensor(gA1, gaT, A1, OP.mult),
      [("st", E.c("st")), ("dx", E.c("dx"))])
    V(lambda e: e.tensor_tensor(gB1o, sa_t, bobpp[:], OP.add))
    V(lambda e: e.tensor_tensor(gB1o, gaT, gB1o, OP.mult))
    xn_hist = [0] * DC
    rstt_hist = 0
    for dch in range(DC):
        wd = [brow_v_sem, ("sv", E.c("sv")), ("ss", E.c("ss"))]
        w = dma(xraw[:, 0:2048], da["x_fm"][dch * 128:(dch + 1) * 128, :], waits=wd,
                pipe=True)
        cd = [("dx", w)]
        if dch >= 1:
            cd.append(("ss", xn_hist[dch - 1]))
        V(lambda e: e.tensor_copy(xbt[:, 0:2048], xraw[:, 0:2048]), cd)
        xb_ = xbt[:, 0:2048]
        xh_ = xsq[:, 0:2048]
        V(lambda e, xh_=xh_, xb_=xb_: e.tensor_tensor(xh_, xb_, mu1b, OP.subtract))
        xh_v = V(lambda e, xh_=xh_: e.tensor_tensor(xh_, xh_, rs1b, OP.mult))
        h_s = SC(lambda e, xh_=xh_, dch=dch: e.activation(
            scr2[:, 0:1024], xh_[:, 0:1024], AF.Identity,
            bias=gB1o[:, dch:dch + 1], scale=gA1[:, dch:dch + 1]),
            [("sv", xh_v), ("sv", rstt_hist)])
        rstt_hist = V(lambda e, dch=dch: e.tensor_tensor(
            AR2[:, dch * 1024:(dch + 1) * 1024], scr2[:, 0:1024],
            xraw[:, 0:1024], OP.add), [("ss", h_s)])
        v1_s = SC(lambda e, xb_=xb_, xh_=xh_, dch=dch: e.activation(
            xb_, xh_, AF.Identity, bias=sa_t[:, dch:dch + 1],
            scale=A1[:, dch:dch + 1]), [("sv", E.c("sv"))])
        V(lambda e, xh_=xh_, xb_=xb_: e.tensor_tensor(xh_, xb_, muhb, OP.subtract),
          [("ss", v1_s)])
        xnb_v = V(lambda e, xh_=xh_: e.tensor_tensor(xh_, xh_, rshb, OP.mult))
        xn_hist[dch] = SC(lambda e, xh_=xh_, dch=dch: e.activation(
            AR1b[:, dch * 2048:(dch + 1) * 2048], xh_, AF.Identity,
            bias=bapp[:, dch:dch + 1], scale=gapp[:, dch:dch + 1]), [("sv", xnb_v)])
    xn_s = E.c("ss")
    r_v = E.c("sv")

    xn_ = lambda dch: AR1b[:, dch * 2048:(dch + 1) * 2048]
    w = dma(wtr[:, 0:2048].rearrange("p (c m) -> p c m", m=128),
            da["w1T"].rearrange("(c p) m -> p c m", p=128),
            waits=[("ss", xn_s)])
    wgt_v = V(lambda e: e.tensor_copy(wgt, wtr[:, 0:2048]), [("dx", w)])
    evac_hist = [0, 0]
    for nch in range(4):
        bank = nch % 2
        for dch in range(DC):
            wd = []
            if dch == 0:
                wd = [("ss", xn_s), ("sv", wgt_v)]
                if nch >= 2:
                    wd.append(("sv", evac_hist[bank]))
            T(lambda e, bank=bank, nch=nch, dch=dch: e.matmul(
                p_mm[:, bank * 512:bank * 512 + 512],
                wgt[:, dch * 128:(dch + 1) * 128],
                xn_(dch)[:, nch * 512:(nch + 1) * 512],
                start=(dch == 0), stop=(dch == DC - 1)), wd)
        mmv = E.c("st")
        ev = V(lambda e, bank=bank, nch=nch: e.tensor_scalar(
            h1n[:, nch * 512:(nch + 1) * 512],
            p_mm[:, bank * 512:bank * 512 + 512], -1.0, None, OP.mult),
          [("st", mmv)])
        if nch < 2:
            ev = V(lambda e, bank=bank, nch=nch: e.tensor_scalar(
                h1b[:, nch * 512:(nch + 1) * 512],
                p_mm[:, bank * 512:bank * 512 + 512], b1cs[:, 0:1], None, OP.add))
        evac_hist[bank] = ev
    h1_v = E.c("sv")
    w = dma(wtr[:, 0:2048].rearrange("p (c m) -> p c m", m=128),
            da["wqkT"].rearrange("(c p) m -> p c m", p=128),
            waits=[("st", E.c("st"))])
    wgt_v = V(lambda e: e.tensor_copy(wgt, wtr[:, 0:2048]), [("dx", w)])
    wb = dma(wtr[0:1, 0:128], da["bqk_row"], waits=[("sv", wgt_v)])
    wgt_v = V(lambda e: e.tensor_copy(bqkr[:], wtr[0:1, 0:128]), [("dx", wb)])
    evac_hist = [0, 0]
    for nch in range(4):
        bank = nch % 2
        for dch in range(DC):
            wd = []
            if dch == 0:
                wd = [("sv", h1_v), ("sv", wgt_v)] if nch < 2 else [("sv", evac_hist[bank])]
            T(lambda e, bank=bank, nch=nch, dch=dch: e.matmul(
                p_sc[:, bank * 512:bank * 512 + 512],
                wgt[:, dch * 128:(dch + 1) * 128],
                xn_(dch)[:, nch * 512:(nch + 1) * 512],
                start=(dch == 0), stop=False), wd)
        T(lambda e, bank=bank: e.matmul(
            p_sc[:, bank * 512:bank * 512 + 512],
            bqkr[:], onebf[:, 0:512], start=False, stop=True))
        mmv = E.c("st")
        evac_hist[bank] = V(lambda e, bank=bank, nch=nch: e.tensor_scalar(
            qks[:, nch * 512:(nch + 1) * 512],
            p_sc[:, bank * 512:bank * 512 + 512],
            sc12s[:, 0:1], None, OP.mult), [("st", mmv)])
    qk_v = E.c("sv")
    w = dma(wtr[:, 0:1024].rearrange("p (c m) -> p c m", m=64),
            da["wvT"].rearrange("(c p) m -> p c m", p=128),
            waits=[("st", E.c("st"))])
    wgt_v = V(lambda e: e.tensor_copy(wgt[:, 0:1024], wtr[:, 0:1024]), [("dx", w)])
    wb = dma(wtr[0:1, 0:64], da["bv_row"], waits=[("sv", wgt_v)])
    wgt_v = V(lambda e: e.tensor_copy(bvr[:], wtr[0:1, 0:64]), [("dx", wb)])
    vgi = dma(vaug[:].rearrange("p (c m) -> p c m", m=128),
              da["vauginit"].unsqueeze(1).to_broadcast([128, DC, 128]),
              waits=[("ss", xn_s), ("st", E.c("st"))])
    vaugv = vaug[:].rearrange("p (c g f) -> p c g f", c=DC, g=32)
    evac_hist = [0, 0, 0, 0]
    for jc in range(DC):
        bank = jc % 4
        for dch in range(DC):
            wd = []
            if dch == 0:
                wd = [("sv", qk_v), ("sv", wgt_v)] if jc < 4 else [("sv", evac_hist[bank])]
            T(lambda e, bank=bank, jc=jc, dch=dch: e.matmul(
                p_mm[:, bank * 512:bank * 512 + 64],
                xn_(dch)[:, jc * 128:(jc + 1) * 128],
                wgt[:, dch * 64:(dch + 1) * 64],
                start=(dch == 0), stop=False), wd)
        T(lambda e, bank=bank: e.matmul(
            p_mm[:, bank * 512:bank * 512 + 64],
            onebf[0:1, 0:128], bvr[:], start=False, stop=True))
        mmv = E.c("st")
        evac_hist[bank] = V(lambda e, bank=bank, jc=jc: e.tensor_copy(
            vaugv[:, jc, :, 0:2],
            p_mm[:, bank * 512:bank * 512 + 64].rearrange("p (g f) -> p g f", g=32)),
            [("st", mmv), ("dx", vgi)] if jc == 0 else [("st", mmv)])
    vt_v = E.c("sv")
    csm = V(lambda e: e.reduce_max(tcol1[:], qks[:], axis=AXX,
                                   apply_absolute_value=True), [("sv", vt_v)])
    w = dma(d_col[:, 0:128].rearrange("o p -> p o"), tcol1[:, 0:1],
            waits=[("sv", csm)])
    w = dma(srow[0:1, 0:128], d_col[:, 0:128], waits=[("dx", w)])
    srv = srow[0:1, 0:128].rearrange("o (a h f) -> o a h f", a=2, h=32)
    V(lambda e: e.tensor_tensor(mrow[0:1, 0:64].rearrange("o (a h) -> o a h", a=2),
                                srv[:, :, :, 0], srv[:, :, :, 1], OP.max), [("dx", w)])
    V(lambda e: e.tensor_tensor(mrow[0:1, 64:96], mrow[0:1, 0:32], mrow[0:1, 32:64],
                                OP.mult))
    csv = V(lambda e: e.tensor_scalar(mrow[0:1, 64:96], mrow[0:1, 64:96], 60.0, -1.0,
                                      OP.min, OP.mult))
    w = dma(d_cs[:, 0:32], mrow[0:1, 64:96], waits=[("sv", csv)])
    cs_pp = dma(gcon[:, 16:48], d_cs[:, 0:32].to_broadcast([128, 32]),
                waits=[("dx", w)])
    E.wait("vector", "dx", cs_pp)

    def topk(zblk, W, k, z0, tcol, scr_bf, u_f32, pre, sw=None):
        sw = sw or W
        np_ = W // sw
        col = lambda i: tmpk[:, i:i + 1]
        pw = list(pre)
        for pc in range(np_):
            SC(lambda e, pc=pc: e.activation(
                scr_bf[:, 0:sw], zblk[:, pc * sw:(pc + 1) * sw], AF.Square,
                accum_out=col(12 + pc)), pw)
        sq_s = E.c("ss")
        V(lambda e: e.reduce_sum(col(0), zblk[:, 0:W], axis=AXX), pw)
        if np_ == 2:
            V(lambda e: e.tensor_tensor(col(1), col(12), col(13), OP.add),
              [("ss", sq_s)])
        else:
            V(lambda e: e.tensor_copy(col(1), col(12)), [("ss", sq_s)])
        V(lambda e: e.tensor_scalar(col(0), col(0), 1.0 / W, None, OP.mult))
        V(lambda e: e.tensor_scalar(col(1), col(1), 1.0 / W, None, OP.mult))
        V(lambda e: e.scalar_tensor_tensor(col(2), col(0), col(0), col(1), OP.mult,
                                           OP.subtract))
        vv = V(lambda e: e.tensor_scalar(col(2), col(2), -1.0, 1e-12, OP.mult, OP.max))
        sg_s = SC(lambda e: e.activation(col(3), col(2), AF.Sqrt), [("sv", vv)])
        V(lambda e: e.scalar_tensor_tensor(col(4), col(3), z0, col(0), OP.mult, OP.add),
          [("ss", sg_s)])
        V(lambda e: e.scalar_tensor_tensor(col(5), col(3), -8.0, col(0), OP.mult,
                                           OP.add))
        V(lambda e: e.scalar_tensor_tensor(col(6), col(3), 8.0, col(0), OP.mult,
                                           OP.add))
        V(lambda e: e.memset(col(7), float(W)))
        V(lambda e: e.memset(col(8), 0.0))

        def count_scan(tgt, op):
            for pc in range(np_):
                V(lambda e, pc=pc, op=op: e.tensor_scalar(
                    scr_bf[:, 0:sw], zblk[:, pc * sw:(pc + 1) * sw],
                    col(4) if op == OP.is_ge else col(6), None,
                    op, OP.add, accum_out=col(12 + pc)))
            if np_ == 2:
                V(lambda e, tgt=tgt: e.tensor_tensor(tgt, col(12), col(13), OP.add))
            else:
                V(lambda e, tgt=tgt: e.tensor_copy(tgt, col(12)))

        for it in range(NSEC):
            count_scan(col(11), OP.is_ge)
            V(lambda e: e.tensor_scalar(col(9), col(11), k + 0.5, None, OP.is_ge))
            V(lambda e: e.tensor_scalar(col(10), col(9), -1.0, 1.0, OP.mult, OP.add))
            V(lambda e: e.tensor_tensor(col(12), col(4), col(5), OP.subtract))
            V(lambda e: e.scalar_tensor_tensor(col(5), col(12), col(9), col(5),
                                               OP.mult, OP.add))
            V(lambda e: e.tensor_tensor(col(12), col(11), col(7), OP.subtract))
            V(lambda e: e.scalar_tensor_tensor(col(7), col(12), col(9), col(7),
                                               OP.mult, OP.add))
            V(lambda e: e.tensor_tensor(col(12), col(4), col(6), OP.subtract))
            V(lambda e: e.scalar_tensor_tensor(col(6), col(12), col(10), col(6),
                                               OP.mult, OP.add))
            V(lambda e: e.tensor_tensor(col(12), col(11), col(8), OP.subtract))
            V(lambda e: e.scalar_tensor_tensor(col(8), col(12), col(10), col(8),
                                               OP.mult, OP.add))
            if it < NSEC - 1:
                V(lambda e: e.tensor_tensor(col(12), col(7), col(8), OP.subtract))
                V(lambda e: e.reciprocal(col(12), col(12)))
                V(lambda e: e.tensor_scalar(col(13), col(7), float(-(k - 3)), None,
                                            OP.add))
                V(lambda e: e.tensor_tensor(col(13), col(13), col(12), OP.mult))
                V(lambda e: e.tensor_scalar(col(13), col(13), 0.02, 0.98, OP.max,
                                            OP.min))
                V(lambda e: e.tensor_tensor(col(12), col(6), col(5), OP.subtract))
                V(lambda e: e.scalar_tensor_tensor(col(4), col(12), col(13), col(5),
                                                   OP.mult, OP.add))
        V(lambda e: e.memset(col(14), 0.0))
        for pc in range(np_):
            V(lambda e, pc=pc: e.tensor_scalar(
                scr_bf[:, 0:sw], zblk[:, pc * sw:(pc + 1) * sw], col(6), None,
                OP.is_lt, OP.add, accum_out=col(12)))
            V(lambda e: e.tensor_tensor(col(14), col(14), col(12), OP.add))
            V(lambda e, pc=pc: e.scalar_tensor_tensor(
                u_f32[:, pc * sw:(pc + 1) * sw], scr_bf[:, 0:sw], BIGM,
                zblk[:, pc * sw:(pc + 1) * sw], OP.mult, OP.add))
        V(lambda e: e.max(out=tmpk[:, 16:24], in_=u_f32[:, 0:W]))
        V(lambda e: e.match_replace(out=u_f32[:, 0:W], in_to_replace=tmpk[:, 16:24],
                                    in_values=u_f32[:, 0:W], imm_value=-1.0e30))
        V(lambda e: e.max(out=tmpk[:, 24:32], in_=u_f32[:, 0:W]))
        V(lambda e: e.tensor_scalar(col(15), col(14), float(k - W - 1), None, OP.add))
        V(lambda e: e.tensor_scalar(col(12), col(15), 0.0, 15.0, OP.max, OP.min))
        V(lambda e: e.tensor_scalar(sel16[:], iota16[:], col(12), None, OP.is_equal))
        V(lambda e: e.tensor_scalar(cand16[:], tmpk[:, 16:32], -BIGM, None, OP.add))
        V(lambda e: e.scalar_tensor_tensor(cand16[:], cand16[:], 1.0, sel16[:],
                                           OP.mult, OP.mult, accum_out=col(13)))
        V(lambda e: e.tensor_scalar(col(9), col(15), 0.0, None, OP.is_lt))
        V(lambda e: e.tensor_tensor(col(12), col(6), col(13), OP.subtract))
        return V(lambda e, tcol=tcol: e.scalar_tensor_tensor(
            tcol[:, 0:1], col(12), col(9), col(13), OP.mult, OP.add))

    indT = AR1b[:, 0:16384]
    Ebuf = AR1b[:, 16384:32768]
    simblk = AR5[:, 0:2048]
    simsp = scr2[:, 0:2048]
    pairA = AR6[:, 0:2048]
    pairB = AR6[:, 2048:4096]

    paccum = xsq[:].bitcast(F32)[:, 0:256]
    pa_state = {"s": [0, 0], "idx": 0}
    passA_start_sv = E.c("sv")
    passA_start_ss = E.c("ss")
    pa_chunks = []
    for srcq in (da["wsynT"], da["woutfT"]):
        for pch in range(srcq.shape[0] // 128):
            for cc2 in range(srcq.shape[1] // 1024):
                pa_chunks.append(
                    srcq[pch * 128:(pch + 1) * 128,
                         cc2 * 1024:(cc2 + 1) * 1024])

    def emit_passA_chunk():
        qa_idx = pa_state["idx"]
        if qa_idx >= len(pa_chunks):
            return
        slot = qa_idx % 2
        wl = [("ss", pa_state["s"][slot])]
        if qa_idx == 0:
            wl += [("sv", passA_start_sv), ("ss", passA_start_ss)]
        w = dma(xraw[:, slot * 1024:(slot + 1) * 1024], pa_chunks[qa_idx],
                waits=wl, eng="gpsimd", sem="dg", pipe=True)
        pa_state["s"][slot] = SC(
            lambda e, slot=slot, qa_idx=qa_idx: e.activation(
                xraw[:, slot * 1024:(slot + 1) * 1024],
                xraw[:, slot * 1024:(slot + 1) * 1024],
                AF.Abs, accum_out=paccum[:, qa_idx:qa_idx + 1]),
            [("dg", w)], dep=False)
        pa_state["idx"] = qa_idx + 1

    V(lambda e: e.memset(p_sc[:, 0:2048], 0.0), [("st", E.c("st"))])

    mmA_hist = 0
    mmA_par = [0, 0]
    indT_ev = 0
    uev_reg = [("ss", 0), ("ss", 0)]
    se_pp = [0, 0]
    rearr_dx = 0
    rearr_ds = 0
    se_hist = 0
    rearr_hist = 0
    ind_v_hist = 0
    indT_done = 0
    for ic in range(NIC):
        for sb2 in range(2):
            sbi = ic * 2 + sb2
            qoff = ic * 256 + sb2 * 128
            grp_dma0 = E.c("dx")
            for g in range(32):
                emit_passA_chunk()
                conB_pp = [0, 0]
                for pp_ in range(2):
                    qA = qoff + 4 * g + 2 * pp_
                    qB = qA + 1
                    pA = pairA if pp_ == 0 else brow[:, 0:2048]
                    pB = pairB if pp_ == 0 else brow[:, 2048:4096]
                    cwA = [("st", mmA_par[pp_])]
                    if sbi == 0 and g == 0 and pp_ == 0:
                        cwA.append(("dx", cs_pp))
                    V(lambda e, qA=qA, pA=pA: e.tensor_scalar(
                        pA[:, 0:2048], h1n[:], h1b[:, qA:qA + 1], 0.0,
                        OP.add, OP.max), cwA, dep=False)
                    conB_pp[pp_] = V(lambda e, qB=qB, pB=pB: e.tensor_scalar(
                        pB[:, 0:2048], h1n[:], h1b[:, qB:qB + 1], 0.0,
                        OP.add, OP.max), dep=False)
                regs_pair = (p_mm, p_sc)
                utl = (utile[:, 0:2048], xbt[:, 0:2048])
                for pp_ in range(2):
                    ri = (g + pp_) % 2
                    reg = regs_pair[ri]
                    pA = pairA if pp_ == 0 else brow[:, 0:2048]
                    pB = pairB if pp_ == 0 else brow[:, 2048:4096]
                    mmAw_extra = [("sv", indT_ev)] if (g == 0 and pp_ == 0) else []
                    for nch2 in range(4):
                        wd = ([("sv", conB_pp[pp_]), uev_reg[ri]] +
                              ([("sv", se_pp[0])] if pp_ == 1 else []) +
                              mmAw_extra) if nch2 == 0 else []
                        T(lambda e, nch2=nch2, pA=pA, reg=reg: e.matmul(
                            reg[0:64, nch2 * 512:(nch2 + 1) * 512], w2t[:],
                            pA[:, nch2 * 512:(nch2 + 1) * 512],
                            start=True, stop=True, tile_position=(0, 0)), wd)
                        T(lambda e, nch2=nch2, pB=pB, reg=reg: e.matmul(
                            reg[64:128, nch2 * 512:(nch2 + 1) * 512], w2t[:],
                            pB[:, nch2 * 512:(nch2 + 1) * 512],
                            start=True, stop=True, tile_position=(0, 64)))
                    mmA_hist = E.c("st")
                    mmA_par[pp_] = mmA_hist
                creg = regs_pair[g % 2]
                for pp_ in range(2):
                    ri = (g + pp_) % 2
                    reg = regs_pair[ri]
                    uv = SC(lambda e, reg=reg, pp_=pp_: e.activation(
                        utl[pp_], reg[:, 0:2048], AF.Relu,
                        bias=b2cs[:, 0:1]), [("st", mmA_par[pp_])], dep=False)
                    uev_reg[ri] = ("ss", uv)
                    for nch2 in range(4):
                        wd = [uev_reg[ri]] if nch2 == 0 else []
                        T(lambda e, nch2=nch2, pp_=pp_, creg=creg: e.matmul(
                            creg[64 * pp_:64 * pp_ + 2,
                                 nch2 * 512:(nch2 + 1) * 512],
                            w3b[:, 0:2], utl[pp_][:, nch2 * 512:(nch2 + 1) * 512],
                            start=True, stop=True, tile_position=(0, 64 * pp_)), wd)
                mmC_hist = E.c("st")
                sew = [("st", mmC_hist), ("dx", rearr_dx)]
                se_hist = V(lambda e, creg=creg: e.tensor_scalar(
                    scr2[0:66, 0:2048], creg[0:66, 0:2048],
                    gcon[0:66, 0:1], None, OP.add), sew, dep=False)
                se_pp[0] = se_hist
                rw = [("sv", se_hist)]
                if g == 0:
                    rw.append(("sv", ind_v_hist))
                for j, row in enumerate((0, 1, 64, 65)):
                    dma(simblk[4 * g + j:4 * g + j + 1, :],
                        scr2[row:row + 1, 0:2048],
                        waits=rw if j == 0 else (), eng="sync", sem="dx",
                        pipe=True)
                rearr_dx = E.c("dx")
            dg_v = G(lambda e, qoff=qoff: e.affine_select(
                out=simblk[:, 0:2048], in_=simblk[:, 0:2048],
                compare_op=OP.not_equal, fill=0.0, base=qoff,
                pattern=[[-1, 2048]], channel_multiplier=1),
                [("dx", rearr_dx)])
            topk(simblk, 2048, KSIM, Z0_SIM, tcol1, pairA, scr2,
                 [("sg", dg_v), ("st", mmA_hist), ("st", indT_done)])
            ind_v_hist = V(lambda e: e.tensor_scalar(
                pairB[:, 0:2048], simblk[:, 0:2048], tcol1[:, 0:1], None, OP.is_ge))
            for jb in range(4):
                tw = [("sv", ind_v_hist)] if jb == 0 else [("sv", indT_ev)]
                for j2 in range(4):
                    jc = 4 * jb + j2
                    T(lambda e, jc=jc, j2=j2: e.transpose(
                        p_bf[:, 2048 + j2 * 128:2048 + (j2 + 1) * 128],
                        pairB[:, jc * 128:(jc + 1) * 128], idbf[:]),
                      tw if j2 == 0 else ())
                trp = E.c("st")
                indT_ev = V(lambda e, jb=jb, qoff=qoff: e.tensor_copy(
                    indT.rearrange("p (jc i) -> p jc i", i=1024)
                        [:, 4 * jb:4 * jb + 4, qoff:qoff + 128],
                    p_bf[:, 2048:2560].rearrange("p (a b) -> p a b", a=4)),
                  [("st", trp)])
            indT_done = E.c("st")
    while pa_state["idx"] < len(pa_chunks):
        emit_passA_chunk()
    passA_last_s = E.c("ss")
    topk_sim_done_v = E.c("sv")
    topk_sim_done_s = E.c("ss")

    exp_hist = [0, 0, 0, 0]
    ao_h = [0, 0, 0, 0]
    ao_last = 0
    ao_dg = 0
    score_hist = [[0, 0, 0, 0] for _ in range(10)]
    jitd = [0] * 10
    aoev_hist = 0
    aodma_hist = 0
    for quad in range(8):
        buf = quad % 2
        jw = [("sv", topk_sim_done_v)] if quad else [
            ("sv", topk_sim_done_v), ("ss", passA_last_s)]
        if quad >= 2:
            jw.append(("st", max(score_hist[quad - 2])))
        for hh in range(4):
            h = 4 * quad + hh
            dma(kjit[32 * hh:32 * hh + 2, buf * 2048:(buf + 1) * 2048],
                qks[64 + 2 * h:64 + 2 * h + 2, 0:2048], waits=jw if hh == 0 else (),
                eng="gpsimd", sem="dg", pipe=True)
            dma(qjit[32 * hh:32 * hh + 2, buf * 1024:(buf + 1) * 1024],
                qks[2 * h:2 * h + 2, 0:1024], eng="gpsimd", sem="dg", pipe=True)
        jitd[quad] = E.c("dg")
        for ic in range(NIC):
            for hh in range(4):
                for jcp in range(8):
                    for j2 in range(2):
                        jc = 2 * jcp + j2
                        wd = []
                        if j2 == 0:
                            wd = [("ss", exp_hist[hh])]
                            if jcp == 0:
                                wd.append(("dg", jitd[quad]))
                        T(lambda e, hh=hh, jc=jc, j2=j2, buf=buf, ic=ic: e.matmul(
                            p_sc[:, hh * 512 + j2 * 256: hh * 512 + (j2 + 1) * 256],
                            kjit[32 * hh:32 * hh + 2,
                                 buf * 2048 + jc * 128: buf * 2048 + (jc + 1) * 128],
                            qjit[32 * hh:32 * hh + 2,
                                 buf * 1024 + ic * 256: buf * 1024 + (ic + 1) * 256],
                            start=True, stop=True, tile_position=(32 * hh, 0)), wd)
                    scd = E.c("st")
                    hcs = 4 * quad + hh
                    ew = [("st", scd)]
                    if jcp == 0:
                        ew.append(("st", ao_h[hh]))
                    exp_hist[hh] = SC(
                        lambda e, hh=hh, jcp=jcp, hcs=hcs: e.activation(
                            Ebuf[:, hh * 4096 + jcp * 512: hh * 4096 + (jcp + 1) * 512],
                            p_sc[:, hh * 512: hh * 512 + 512], AF.Exp,
                            bias=gcon[:, 16 + hcs:17 + hcs]),
                        ew, dep=False)
                score_hist[quad][hh] = E.c("st")
            mk0 = E.c("sv")
            for hh in range(4):
                V(lambda e, hh=hh, ic=ic: e.tensor_tensor(
                    Ebuf[:, hh * 4096:(hh + 1) * 4096],
                    Ebuf[:, hh * 4096:(hh + 1) * 4096],
                    indT.rearrange("p (jc i) -> p jc i", i=1024)
                        [:, :, ic * 256:(ic + 1) * 256],
                    OP.mult), [("ss", exp_hist[hh])], dep=False)
            mk = E.c("sv")
            for hh in range(4):
                for jc in range(DC):
                    wd = []
                    if jc == 0:
                        wd = [("sv", mk), ("sv", aoev_hist)] if hh == 0 else [("sv", mk)]
                    habs = 4 * quad + hh
                    T(lambda e, hh=hh, jc=jc, habs=habs: e.matmul(
                        p_mm[32 * hh:32 * hh + 3, 0:256],
                        vaug[:, jc * 128 + 4 * habs: jc * 128 + 4 * habs + 3],
                        Ebuf[:, hh * 4096 + jc * 256: hh * 4096 + (jc + 1) * 256],
                        start=(jc == 0), stop=(jc == DC - 1),
                        tile_position=(0, 32 * hh)), wd)
                ao_h[hh] = E.c("st")
            aod = E.c("st")
            ao_last = aod
            for hh in range(4):
                V(lambda e, hh=hh, ic=ic: e.tensor_copy(
                    scr2[32 * hh:32 * hh + 3, ic * 256:(ic + 1) * 256],
                    p_mm[32 * hh:32 * hh + 3, 0:256]),
                  [("st", aod), ("dg", ao_dg)] if (hh == 0 and ic == 0)
                  else ([("st", aod)] if hh == 0 else ()), dep=False)
            aoev_hist = E.c("sv")
        for hh in range(4):
            h = 4 * quad + hh
            dma(aot[h:h + 1, 0:1024], scr2[32 * hh:32 * hh + 1, 0:1024],
                waits=[("sv", aoev_hist)] if hh == 0 else (),
                eng="gpsimd", sem="dg", pipe=True)
            dma(aot[32 + h:32 + h + 1, 0:1024],
                scr2[32 * hh + 1:32 * hh + 2, 0:1024],
                eng="gpsimd", sem="dg", pipe=True)
            dma(aden[h:h + 1, 0:1024], scr2[32 * hh + 2:32 * hh + 3, 0:1024],
                eng="gpsimd", sem="dg", pipe=True)
        ao_dg = E.c("dg")
        aodma_hist = ao_dg
    rc = V(lambda e: e.reciprocal(aden[0:32, :], aden[0:32, :]),
           [("dg", aodma_hist)])
    w1_ = dma(denr[0:32, 0:1024], aden[0:32, :],
              waits=[("sv", rc), ("ss", passA_last_s)])
    w2_ = dma(denr[32:64, 0:1024], aden[0:32, :], waits=[("dx", w1_)])
    mulv = V(lambda e: e.tensor_tensor(aot[:, :], aot[:, :], denr[0:64, 0:1024],
                                       OP.mult), [("dx", w2_)])
    aotb_v = V(lambda e: e.tensor_copy(aotb[:], aot[:]))
    w = dma(scr2[0:64, 0:2048], da["woutaT"], waits=[("sv", aotb_v)])
    woat_v = V(lambda e: e.tensor_copy(woat, scr2[0:64, 0:2048]), [("dx", w)])
    projw = woat_v
    proj_ss = E.c("ss")
    x2_hist = [0, 0]
    for dch in range(DC):
        for n2 in range(2):
            bank = (2 * dch + n2) % 2
            wd = [("sv", projw), ("ss", proj_ss)] if dch == 0 and n2 == 0 else []
            if dch > 0 or n2 > 0:
                wd.append(("sv", x2_hist[bank]))
            T(lambda e, dch=dch, n2=n2, bank=bank: e.matmul(
                p_sc[:, bank * 512:(bank + 1) * 512],
                woat[:, dch * 128:(dch + 1) * 128],
                aotb[:, n2 * 512:(n2 + 1) * 512],
                start=True, stop=True), wd + [("sv", woat_v)] if (dch == 0 and n2 == 0) else wd)
            pj = E.c("st")
            x2_hist[bank] = V(lambda e, dch=dch, n2=n2, bank=bank:
                              e.scalar_tensor_tensor(
                AR1[:, dch * 1024 + n2 * 512: dch * 1024 + (n2 + 1) * 512],
                p_sc[:, bank * 512:(bank + 1) * 512],
                gaT[:, dch:dch + 1],
                AR2[:, dch * 1024 + n2 * 512: dch * 1024 + (n2 + 1) * 512],
                OP.mult, OP.add), [("st", pj)])
    x2_v = E.c("sv")
    for dch in range(DC):
        dma(d_x2[dch * 128:(dch + 1) * 128, :], AR1[:, dch * 1024:(dch + 1) * 1024],
            waits=[("sv", x2_v)] if dch == 0 else ())
    x2_spill = E.c("dx")

    for qi in range(2):
        rv_ = V(lambda e, qi=qi: e.reduce_sum(
            tmpk[:, 20 + qi:21 + qi], paccum[:, qi * 16:(qi + 1) * 16],
            axis=AXX), [("ss", passA_last_s)] if qi == 0 else ())
        tm = T(lambda e, qi=qi: e.matmul(
            p_sc[0:1, 1536 + qi:1537 + qi], onef[:],
            tmpk[:, 20 + qi:21 + qi], start=True, stop=True), [("sv", rv_)])
        pcp = V(lambda e, qi=qi: e.tensor_copy(sclr[0:1, 8 + qi:9 + qi],
                                               p_sc[0:1, 1536 + qi:1537 + qi]),
                [("st", tm)])
    w = dma(d_spart[:, 0:2], sclr[0:1, 8:10], waits=[("sv", E.c("sv"))])
    E.op("gpsimd", lambda e: e.collective_compute(
        "AllReduce", OP.add, replica_groups=[list(range(8))],
        ins=[d_spart.opt()], outs=[d_ssum.opt()]), inc=("scc", 1),
        waits=[("dx", w)])
    ssum_cc = E.c("scc")
    w = dma(sclr[0:1, 12:14], d_ssum[:, 0:2], waits=[("scc", ssum_cc)])
    sload = w

    def quant_scale(qi, s_col):
        V(lambda e, qi=qi: e.tensor_scalar(sclr[0:1, 4 * qi:4 * qi + 1],
                                           sclr[0:1, 12 + qi:13 + qi],
                                           1.0 / (2048.0 * 8192.0), None,
                                           OP.mult),
          [("dx", sload)] if qi == 0 else ())
        V(lambda e, qi=qi: e.tensor_scalar(sclr[0:1, 4 * qi + 1:4 * qi + 2],
                                           sclr[0:1, 4 * qi:4 * qi + 1],
                                           1e-8, None, OP.add))
        sc_v = V(lambda e, qi=qi: e.reciprocal(sclr[0:1, 4 * qi + 1:4 * qi + 2],
                                               sclr[0:1, 4 * qi + 1:4 * qi + 2]))
        w = dma(d_scal[:, 8 + 2 * qi:10 + 2 * qi], sclr[0:1, 4 * qi:4 * qi + 2],
                waits=[("sv", sc_v)])
        scb = dma(gcon[:, 48 + s_col:50 + s_col],
                  d_scal[:, 8 + 2 * qi:10 + 2 * qi].to_broadcast([128, 2]),
                  waits=[("dx", w)])
        return scb

    V(lambda e: e.memset(gcon[:, 50:51], MAGIC))
    V(lambda e: e.memset(gcon[:, 53:54], MAGIC))
    V(lambda e: e.memset(gcon[:, 54:55], -MAGIC))
    scb1 = quant_scale(0, 0)
    scb2 = quant_scale(1, 3)

    pb_a1 = [0, 0, 0, 0]
    pb_clamp = [0, 0, 0, 0]
    pb_sub = [0, 0, 0, 0]
    pb_odma = {"dg": 0, "ds": 0}
    scbs = {0: scb1, 1: scb2}
    browf = brow[:].bitcast(F32)
    ld_slots = (xraw[:, 0:1024], xraw[:, 1024:2048],
                browf[:, 0:1024], browf[:, 1024:2048])
    hidstf = hidst[:].bitcast(F32)
    sc_slots = (scr2[:, 0:1024], scr2[:, 1024:2048],
                hidstf[:, 0:1024], ytile[:, 0:1024])
    bi = 0
    for (srcq, dstq, qi, s_col) in ((da["wsynT"], d_tqsyn_sl, 0, 0),
                                    (da["woutfT"], d_tqout_sl, 1, 3)):
        for pch in range(srcq.shape[0] // 128):
            for cc2 in range(srcq.shape[1] // 1024):
                slot = bi % 4
                oslot = bi % 4
                leng, lsem = ("sync", "dx") if slot % 2 == 0 else ("scalar", "ds")
                oeng, osem = ("gpsimd", "dg")
                wl = [("ss", pb_a1[slot])]
                if bi < 4:
                    wl.append(("ss", passA_last_s))
                w = dma(ld_slots[slot],
                        srcq[pch * 128:(pch + 1) * 128,
                             cc2 * 1024:(cc2 + 1) * 1024],
                        waits=wl, eng=leng, sem=lsem, pipe=True)
                a1w = [(lsem, w), ("dx", scbs[qi]), ("sv", pb_clamp[slot])]
                pb_a1[slot] = SC(lambda e, slot=slot, s_col=s_col: e.activation(
                    sc_slots[slot], ld_slots[slot],
                    AF.Identity, bias=gcon[:, 50 + s_col:51 + s_col],
                    scale=gcon[:, 49 + s_col:50 + s_col]), a1w, dep=False)
                pb_clamp[slot] = V(lambda e, slot=slot: e.tensor_scalar(
                    sc_slots[slot], sc_slots[slot],
                    MAGIC - 1.0, MAGIC + 1.0, OP.max, OP.min),
                    [("ss", pb_a1[slot])], dep=False)
                pb_sub[oslot] = SC(lambda e, slot=slot, oslot=oslot: e.activation(
                    AR6[:, oslot * 1024:(oslot + 1) * 1024],
                    sc_slots[slot],
                    AF.Identity, bias=gcon[:, 54:55]),
                    [("sv", pb_clamp[slot]), (osem, pb_odma[osem])], dep=False)
                pb_odma[osem] = dma(
                    dstq[pch * 128:(pch + 1) * 128,
                         cc2 * 1024:(cc2 + 1) * 1024],
                    AR6[:, oslot * 1024:(oslot + 1) * 1024],
                    waits=[("ss", pb_sub[oslot])], eng=oeng, sem=osem,
                    pipe=True)
                bi += 1
    E.op("gpsimd", lambda e: e.collective_compute(
        "AllGather", OP.bypass, replica_groups=[list(range(8))],
        ins=[d_tqsyn_sl.opt()], outs=[d_tqsyn.opt()]), inc=("scc", 1),
        waits=[("dg", E.c("dg"))])
    tqsyn_cc = E.c("scc")
    E.op("gpsimd", lambda e: e.collective_compute(
        "AllGather", OP.bypass, replica_groups=[list(range(8))],
        ins=[d_tqout_sl.opt()], outs=[d_tqout.opt()]), inc=("scc", 1))
    tqout_cc = E.c("scc")
    qb_done = [("scc", tqsyn_cc)]
    bsyn_rows = [h1n[0:1, 0:2048], qks[0:1, 0:2048], vaug[0:1, 0:2048],
                 scr2b[0:1, 0:2048]]
    s1r_v = 0
    for a in range(4):
        w = dma(xraw[0:1, 0:2048], da["bsyn_row"][:, a * 2048:(a + 1) * 2048],
                waits=[("sv", E.c("sv")), ("dg", E.c("dg")), ("ss", E.c("ss"))])
        s1r_v = V(lambda e, a=a: e.tensor_copy(bsyn_rows[a], xraw[0:1, 0:2048]),
                  [("dx", w)])
    s1r_v = V(lambda e: e.tensor_copy(s1row[0:1, 0:128],
                                      gcon[0:1, 49:50].to_broadcast([1, 128])))

    mmh = [0] * DC
    for dch in range(DC):
        x2c = AR1[:, dch * 1024:(dch + 1) * 1024]
        cw = [("dx", x2_spill)] if dch == 0 else [("st", mmh[dch - 1])]
        cb_v = V(lambda e, x2c=x2c: e.tensor_copy(xbt[:, 0:1024], x2c), cw,
                 dep=False)
        sq_s = SC(lambda e: e.activation(xsq[:, 0:1024], xbt[:, 0:1024],
                                         AF.Square),
                  [("sv", cb_v)] + ([("st", mmh[dch - 1])] if dch else []))
        for n2 in range(2):
            T(lambda e, dch=dch, n2=n2: e.matmul(
                p_sc[0:1, 1024 + n2 * 512:1024 + (n2 + 1) * 512],
                onebc[:, 0:1],
                xbt[:, n2 * 512:(n2 + 1) * 512],
                start=(dch == 0), stop=(dch == DC - 1)),
              [("sv", cb_v)] if n2 == 0 else ())
            T(lambda e, dch=dch, n2=n2: e.matmul(
                p_mm[0:1, 1024 + n2 * 512:1024 + (n2 + 1) * 512],
                onebc[:, 0:1],
                xsq[:, n2 * 512:(n2 + 1) * 512],
                start=(dch == 0), stop=(dch == DC - 1)),
              [("ss", sq_s)] if n2 == 0 else ())
        mmh[dch] = E.c("st")
    stA = scr2[0:1, 1024:2048]
    stB = AR5[0:1, 0:1024]
    stC = AR5[0:1, 1024:2048]
    stMb = hidst[0:1, 0:1024]
    stRb = hidst[0:1, 1024:2048]
    ev2 = V(lambda e: e.tensor_scalar(stA, p_sc[0:1, 1024:2048], 1.0 / D, None,
                                      OP.mult), [("st", E.c("st"))])
    V(lambda e: e.tensor_scalar(stB, p_mm[0:1, 1024:2048], 1.0 / D, None,
                                OP.mult))
    V(lambda e: e.tensor_tensor(stC, stA, stA, OP.mult))
    V(lambda e: e.tensor_tensor(stB, stB, stC, OP.subtract))
    V(lambda e: e.tensor_scalar(stB, stB, LN_EPS, None, OP.add))
    rv = V(lambda e: e.reciprocal(stB, stB))
    rs_s = SC(lambda e: e.activation(stB, stB, AF.Sqrt), [("sv", rv)])
    cbb = V(lambda e: e.tensor_copy(stMb, stA), [("ss", rs_s)])
    cbb = V(lambda e: e.tensor_copy(stRb, stB))
    w = dma(d_rowsb[0:1, 0:1024], stMb, waits=[("sv", cbb)])
    w = dma(d_rowsb[1:2, 0:1024], stRb, waits=[("sv", cbb)])
    brfb = brf[:].bitcast(BF16)
    bw1 = dma(brfb[:, 0:1024], d_rowsb[0:1, 0:1024].to_broadcast([128, 1024]),
              waits=[("dx", w)])
    bw2 = dma(brfb[:, 1024:2048], d_rowsb[1:2, 0:1024].to_broadcast([128, 1024]),
              waits=[("dx", w)])
    mu2b = brfb[:, 0:1024]; rs2b = brfb[:, 1024:2048]
    xn2_hist = [0] * DC
    for dch in range(DC):
        x2c = AR1[:, dch * 1024:(dch + 1) * 1024]
        wd = [("dx", bw2)] if dch == 0 else [("ss", xn2_hist[dch - 1])]
        V(lambda e, x2c=x2c: e.tensor_tensor(xraw[:, 0:1024], x2c, mu2b,
                                             OP.subtract), wd)
        s2v = V(lambda e: e.tensor_tensor(xraw[:, 0:1024], xraw[:, 0:1024], rs2b,
                                          OP.mult))
        xn2_hist[dch] = SC(lambda e, dch=dch: e.activation(
            AR2b[:, dch * 1024:(dch + 1) * 1024], xraw[:, 0:1024], AF.Identity,
            bias=sb_t[:, dch:dch + 1], scale=A2[:, dch:dch + 1]), [("sv", s2v)])
    xn2_s = E.c("ss")
    xn2T = lambda dch: AR2b[:, dch * 1024:(dch + 1) * 1024]

    uP6 = AR2[:, 8192:16384]
    hidtm = AR2b[:, 16384:24576]
    xn2_all_v = E.c("sv")
    zmm_hist = [0, 0]
    zev_s = 0
    hidT_d = 0
    for tb in range(NTOKB):
        for nch in range(DC):
            for hs in range(2):
                buf = (2 * nch + hs) % 2
                wd = [("st", zmm_hist[buf])]
                if nch == 0 and hs == 0:
                    wd.append(("sv", E.c("sv")))
                if tb == 0 and nch == 0:
                    wd += [("ss", xn2_s), ("sv", s1r_v)] + qb_done
                sbuf_ = AR6[:, 0:4096] if buf == 0 else brow[:, 0:4096]
                ssem = "dx" if buf == 0 else "ds"
                w = dma(sbuf_.rearrange("p (c n) -> p c n", c=8),
                        d_tqsyn[hs * 1024:(hs + 1) * 1024,
                                nch * 512:(nch + 1) * 512]
                        .rearrange("(c p) n -> p c n", p=128), waits=wd,
                        sem=ssem, eng="sync" if buf == 0 else "scalar",
                        pipe=True)
                for mg in range(2):
                    for dc8 in range(8):
                        dch = 8 * hs + dc8
                        mw = []
                        if dc8 == 0:
                            mw = [(ssem, w)]
                            if hs == 0 and mg == 0:
                                mw.append(("ss", zev_s))
                        sb_ = AR6 if buf == 0 else brow
                        T(lambda e, sb_=sb_, mg=mg, dch=dch, dc8=dc8, tb=tb:
                          e.matmul(
                            p_sc[:, mg * 512:(mg + 1) * 512],
                            xn2T(dch)[:, tb * 256 + mg * 128:
                                       tb * 256 + (mg + 1) * 128],
                            sb_[:, dc8 * 512:(dc8 + 1) * 512],
                            start=(dch == 0), stop=False), mw)
                zmm_hist[buf] = E.c("st")
            for mg in range(2):
                T(lambda e, mg=mg, nch=nch: e.matmul(
                    p_sc[:, mg * 512:(mg + 1) * 512],
                    s1row[:],
                    bsyn_rows[nch // 4][:, (nch % 4) * 512:(nch % 4 + 1) * 512],
                    start=False, stop=True))
            zmm_hist[(2 * nch + 1) % 2] = E.c("st")
            zw = [("st", E.c("st"))]
            zev_s = SC(lambda e, nch=nch: e.activation(
                AR1[:, 0 * 8192 + nch * 512: 0 * 8192 + (nch + 1) * 512],
                p_sc[:, 0:512], AF.Identity, scale=gcon[:, 48:49]), zw,
                dep=False)
            zev_s = SC(lambda e, nch=nch: e.activation(
                AR1[:, 1 * 8192 + nch * 512: 1 * 8192 + (nch + 1) * 512],
                p_sc[:, 512:1024], AF.Identity, scale=gcon[:, 48:49]),
                dep=False)
        z_done_s = E.c("ss")
        for r2 in range(2):
            zt = AR1[:, r2 * 8192:(r2 + 1) * 8192]
            tk = topk(zt, NSYN, KFFN, Z0_FFN, tcol2, AR5b[:, 0:4096], uP6,
                      [("ss", z_done_s), ("st", E.c("st"))], sw=4096)
            V(lambda e: e.tensor_scalar(tcol2[:], tcol2[:], 1e-30, None, OP.max))
            hid_v = V(lambda e, zt=zt: e.scalar_tensor_tensor(
                hidtm[:, 0:8192], zt, tcol2[:, 0:1], zt, OP.is_ge, OP.mult))
            for sb4 in range(16):
                for j4 in range(4):
                    sc_ = 4 * sb4 + j4
                    T(lambda e, sc_=sc_, j4=j4: e.transpose(
                        p_bf[:, 2048 + j4 * 128:2048 + (j4 + 1) * 128],
                        hidtm[:, sc_ * 128:(sc_ + 1) * 128], idbf[:]),
                      [("sv", hid_v), ("dg", hidT_d)] if j4 == 0 else ())
                trp = E.c("st")
                hcp = V(lambda e, sb4=sb4: e.tensor_copy(
                    hidst[:, (sb4 % 2) * 1024:(sb4 % 2) * 1024 + 512],
                    p_bf[:, 2048:2560]), [("st", trp)], dep=False)
                hidT_d = dma(
                    d_hidT[:, tb * 256 + r2 * 128: tb * 256 + (r2 + 1) * 128]
                    .rearrange("(a b) t -> b a t", b=128)[:, 4 * sb4:4 * sb4 + 4, :],
                    hidst[:, (sb4 % 2) * 1024:(sb4 % 2) * 1024 + 512]
                    .rearrange("p (a t) -> p a t", a=4),
                    waits=[("sv", hcp)], eng="gpsimd", sem="dg", pipe=True)
        ffn_fwd_done = E.c("dg")

    ybuf = AR2b[:, 16384:32768]
    yv_hist = 0
    pd_hist = [0] * DC
    spill_dg = [0, 0]
    for half in range(2):
        hw = [("dg", ffn_fwd_done)] if half == 0 else [("st", E.c("st"))]
        hld = dma(AR1b[:, 0:32768].rearrange("p (c t) -> p c t", c=32),
                  d_hidT[half * 4096:(half + 1) * 4096, :]
                  .rearrange("(c p) t -> p c t", p=128), waits=hw, pipe=True)
        tqo_hist2 = [0, 0]
        wtq_hist = [0, 0]
        for ddch in range(DC):
            tbuf = ddch % 2
            tqb = AR6 if tbuf == 0 else brow
            tsem = "dx" if tbuf == 0 else "ds"
            wq = [("dx", hld)] if ddch < 2 else []
            if half == 0 and ddch == 0:
                wq.append(("scc", tqout_cc))
            if tqo_hist2[tbuf]:
                wq.append(("st", tqo_hist2[tbuf]))
            wtq = dma(tqb[:, 0:4096].rearrange("p (c m) -> p c m", c=32),
                      d_tqout[half * 4096:(half + 1) * 4096,
                              ddch * 128:(ddch + 1) * 128]
                      .rearrange("(c p) m -> p c m", p=128), waits=wq,
                      sem=tsem, eng="sync" if tbuf == 0 else "scalar",
                      pipe=True)
            wtq_hist[tbuf] = wtq
            for n2 in range(2):
                for kch in range(32):
                    mw = []
                    if kch == 0:
                        mw = [(tsem, wtq), ("sv", yv_hist)]
                    T(lambda e, ddch=ddch, n2=n2, kch=kch, tqb=tqb: e.matmul(
                        p_mm[:, n2 * 512:(n2 + 1) * 512],
                        tqb[:, kch * 128:(kch + 1) * 128],
                        AR1b[:, kch * 1024 + n2 * 512: kch * 1024 + (n2 + 1) * 512],
                        start=(kch == 0), stop=(kch == 31)), mw)
                mmd = E.c("st")
                if half == 0:
                    pv = V(lambda e, n2=n2: e.tensor_copy(
                        scr2[:, n2 * 512:(n2 + 1) * 512],
                        p_mm[:, n2 * 512:(n2 + 1) * 512]),
                        [("st", mmd), ("dg", spill_dg[n2])], dep=False)
                    yv_hist = pv
                    pd_hist[ddch] = dma(
                        d_partial[ddch * 128:(ddch + 1) * 128,
                                  n2 * 512:(n2 + 1) * 512],
                        scr2[:, n2 * 512:(n2 + 1) * 512], waits=[("sv", pv)],
                        eng="gpsimd", sem="dg", pipe=True)
                    spill_dg[n2] = pd_hist[ddch]
                else:
                    wpl = dma(ytile[:, n2 * 512:(n2 + 1) * 512],
                              d_partial[ddch * 128:(ddch + 1) * 128,
                                        n2 * 512:(n2 + 1) * 512],
                              waits=[("dg", pd_hist[ddch]), ("sv", yv_hist)],
                              pipe=True)
                    a1 = V(lambda e, n2=n2: e.tensor_tensor(
                        scr2[:, n2 * 512:(n2 + 1) * 512],
                        p_mm[:, n2 * 512:(n2 + 1) * 512],
                        ytile[:, n2 * 512:(n2 + 1) * 512], OP.add),
                        [("st", mmd), ("dx", wpl), ("dg", spill_dg[n2])],
                        dep=False)
                    a2 = SC(lambda e, ddch=ddch, n2=n2: e.activation(
                        scr2[:, n2 * 512:(n2 + 1) * 512],
                        scr2[:, n2 * 512:(n2 + 1) * 512], AF.Identity,
                        scale=gcon[:, 51:52], bias=bfbpp[:, ddch:ddch + 1]),
                        [("sv", a1)])
                    yv_hist = V(lambda e, ddch=ddch, n2=n2: e.tensor_tensor(
                        ybuf[:, ddch * 1024 + n2 * 512: ddch * 1024 + (n2 + 1) * 512],
                        scr2[:, n2 * 512:(n2 + 1) * 512],
                        xn2T(ddch)[:, n2 * 512:(n2 + 1) * 512], OP.add),
                        [("ss", a2)], dep=False)
            tqo_hist2[tbuf] = E.c("st")
    y_done_v = E.c("sv")

    ymm = [0] * DC
    for dch in range(DC):
        for n2 in range(2):
            T(lambda e, dch=dch, n2=n2: e.matmul(
                p_sc[0:1, 1024 + n2 * 512:1024 + (n2 + 1) * 512],
                onebc[:, 0:1],
                ybuf[:, dch * 1024 + n2 * 512: dch * 1024 + (n2 + 1) * 512],
                start=(dch == 0), stop=(dch == DC - 1)),
              [("sv", y_done_v)] if (dch == 0 and n2 == 0) else ())
        sqv = SC(lambda e, dch=dch: e.activation(
            xsq[:, 0:1024], ybuf[:, dch * 1024:(dch + 1) * 1024], AF.Square),
            [("sv", y_done_v)] if dch == 0 else [("st", ymm[dch - 1])])
        for n2 in range(2):
            T(lambda e, dch=dch, n2=n2: e.matmul(
                p_mm[0:1, 1024 + n2 * 512:1024 + (n2 + 1) * 512],
                onebc[:, 0:1], xsq[:, n2 * 512:(n2 + 1) * 512],
                start=(dch == 0), stop=(dch == DC - 1)),
              [("ss", sqv)] if n2 == 0 else ())
        ymm[dch] = E.c("st")
    stA2 = scr2[0:1, 0:1024]
    stB2 = scr2[0:1, 1024:2048]
    stC2 = AR5[0:1, 0:1024]
    stMb2 = hidst[0:1, 0:1024]
    stRb2 = hidst[0:1, 1024:2048]
    ev3 = V(lambda e: e.tensor_scalar(stA2, p_sc[0:1, 1024:2048], 1.0 / D, None,
                                      OP.mult), [("st", E.c("st"))])
    V(lambda e: e.tensor_scalar(stB2, p_mm[0:1, 1024:2048], 1.0 / D, None,
                                OP.mult))
    V(lambda e: e.tensor_tensor(stC2, stA2, stA2, OP.mult))
    V(lambda e: e.tensor_tensor(stB2, stB2, stC2, OP.subtract))
    V(lambda e: e.tensor_scalar(stB2, stB2, LN_EPS, None, OP.add))
    rv = V(lambda e: e.reciprocal(stB2, stB2))
    rsy_s = SC(lambda e: e.activation(stB2, stB2, AF.Sqrt), [("sv", rv)])
    cbb = V(lambda e: e.tensor_copy(stMb2, stA2), [("ss", rsy_s)])
    cbb = V(lambda e: e.tensor_copy(stRb2, stB2))
    w = dma(d_rowsb[0:1, 0:1024], stMb2, waits=[("sv", cbb)])
    w = dma(d_rowsb[1:2, 0:1024], stRb2, waits=[("sv", cbb)])
    brfb = brf[:].bitcast(BF16)
    bw1 = dma(brfb[:, 0:1024], d_rowsb[0:1, 0:1024].to_broadcast([128, 1024]),
              waits=[("dx", w)])
    bw2 = dma(brfb[:, 1024:2048], d_rowsb[1:2, 0:1024].to_broadcast([128, 1024]),
              waits=[("dx", w)])
    muyb = brfb[:, 0:1024]; rsyb = brfb[:, 1024:2048]
    x3v = 0
    outd = 0
    for dch in range(DC):
        wd = [("dx", bw2), ("sv", x3v)]
        wx2 = dma(x2l[:, 0:1024], d_x2[dch * 128:(dch + 1) * 128, :], waits=wd)
        V(lambda e, dch=dch: e.tensor_tensor(scr2[:, 0:1024],
                                             ybuf[:, dch * 1024:(dch + 1) * 1024],
                                             muyb, OP.subtract),
          [("dx", bw2), ("st", E.c("st"))])
        s3 = V(lambda e: e.tensor_tensor(scr2[:, 0:1024], scr2[:, 0:1024], rsyb,
                                         OP.mult))
        fs = SC(lambda e, dch=dch: e.activation(
            scr2[:, 1024:2048], scr2[:, 0:1024], AF.Identity,
            bias=bfpp[:, dch:dch + 1], scale=gfpp[:, dch:dch + 1]), [("sv", s3)])
        x3v = V(lambda e, dch=dch: e.scalar_tensor_tensor(
            scr2[:, 0:1024], scr2[:, 1024:2048], gbT[:, dch:dch + 1],
            x2l[:, 0:1024],
            OP.mult, OP.add), [("ss", fs), ("dx", wx2)])
        for t4 in range(2):
            tev = [("sv", x3v)] if t4 == 0 else [("sv", E.c("sv"))]
            if dch > 0 and t4 == 0:
                tev.append(("dout", outd))
            for j4 in range(4):
                tc = 4 * t4 + j4
                T(lambda e, tc=tc, j4=j4: e.transpose(
                    p_sc[:, 1024 + j4 * 128:1024 + (j4 + 1) * 128],
                    scr2[:, tc * 128:(tc + 1) * 128], idf2[:]),
                  tev if j4 == 0 else ())
            trp = E.c("st")
            V(lambda e, t4=t4: e.tensor_copy(otile[:, t4 * 512:(t4 + 1) * 512],
                                             p_sc[:, 1024:1536]), [("st", trp)],
              dep=False)
        oev = E.c("sv")
        outd = dma(out_tm[:, dch * 128:(dch + 1) * 128]
                   .rearrange("(a p) m -> p a m", p=128),
                   otile[:, 0:1024].rearrange("p (a m) -> p a m", m=128),
                   waits=[("sv", oev)], sem="dout")
    E.wait("sync", "dout", E.c("dout"))

    @block.tensor
    def _(e):
        for f in E.q["tensor"]:
            f(e)

    @block.sync
    def _(e):
        for f in E.q["sync"]:
            f(e)

    @block.vector
    def _(e):
        for f in E.q["vector"]:
            f(e)

    @block.scalar
    def _(e):
        for f in E.q["scalar"]:
            f(e)

    @block.gpsimd
    def _(e):
        for f in E.q["gpsimd"]:
            f(e)

    stack.close()
    return nc


def _host_prep(inputs, core):
    b, half = core // 2, core % 2
    qi0 = half * TI
    f32 = lambda a: np.ascontiguousarray(a, dtype=np.float32)
    bf = lambda a: np.ascontiguousarray(a).astype(ml_dtypes.bfloat16)
    x = np.asarray(inputs["L0_attn_x"], np.float32)[b]
    if half == 0:
        xp = x
    else:
        xp = np.concatenate([x[TI:], x[:TI]], axis=0)
    Wqkv = np.asarray(inputs["L0_attn_Wqkv"], np.float32)
    bqkv = np.asarray(inputs["L0_attn_bqkv"], np.float32)
    W3 = np.asarray(inputs["L0_attn_W3"], np.float32)
    w3blk = np.zeros((128, 2), np.float32)
    w3blk[0:64, 0] = W3[0]
    w3blk[64:128, 1] = W3[0]
    b2 = np.asarray(inputs["L0_attn_b2"], np.float32)
    iota16 = np.broadcast_to(np.arange(16, dtype=np.float32)[None, :], (128, 16))
    vauginit = np.zeros((128, 128), np.float32)
    for h in range(32):
        vauginit[:, 4 * h + 2] = 1.0
    ident = np.eye(128, dtype=np.float32)
    sc12 = np.ones((128, 1), np.float32)
    sc12[0:64, 0] = ISQ2
    pp = lambda v: np.ascontiguousarray(
        np.asarray(v, np.float32).reshape(DC, 128).T)
    d = {
        "x_fm": f32(xp.T),
        "w1T": f32(np.asarray(inputs["L0_attn_W1"], np.float32).T),
        "b1c": f32(np.asarray(inputs["L0_attn_b1"], np.float32)[:, None]),
        "w2T": f32(np.asarray(inputs["L0_attn_W2"], np.float32).T),
        "b2st": f32(np.concatenate([b2, b2])[:, None]),
        "w3blk": f32(w3blk),
        "b3s": f32(np.asarray(inputs["L0_attn_b3"], np.float32).reshape(1, 1)),
        "wqkT": f32(Wqkv[0:128].T),
        "bqk_row": f32(bqkv[0:128][None, :]),
        "wvT": f32(Wqkv[128:192].T),
        "bv_row": f32(bqkv[128:192][None, :]),
        "woutaT": f32(np.asarray(inputs["L0_attn_Wout"], np.float32).T[
            np.array([2 * h + c for c in (0, 1) for h in range(32)])]),
        "bout_pp": pp(inputs["L0_attn_bout"]),
        "wsynT": f32(np.asarray(inputs["L0_ffn_W_syn"], np.float32)
                     .T[core * (D // 8):(core + 1) * (D // 8)]),
        "bsyn_row": f32(np.asarray(inputs["L0_ffn_b_syn"], np.float32)[None, :]),
        "woutfT": f32(np.asarray(inputs["L0_ffn_W_out"], np.float32)
                      .T[core * (NSYN // 8):(core + 1) * (NSYN // 8)]),
        "boutf_pp": pp(inputs["L0_ffn_b_out"]),
        "gammaA_pp": pp(inputs["L0_attn_gamma"]),
        "betaA_pp": pp(inputs["L0_attn_beta"]),
        "gammaF_pp": pp(inputs["L0_ffn_gamma"]),
        "betaF_pp": pp(inputs["L0_ffn_beta"]),
        "c_pp": pp(np.asarray(inputs["L0_c"], np.float32)[b]),
        "wmod": f32(np.asarray(inputs["L0_wmod"], np.float32)),
        "vauginit": bf(vauginit),
        "idt_bf": bf(ident),
        "idt_f32": f32(ident),
        "iota16_in": f32(iota16),
        "ones_bf_row": bf(np.ones((1, 512), np.float32)),
        "ones_f32_col": f32(np.ones((128, 1), np.float32)),
        "sc12": sc12,
    }
    return d


_NC_CACHE = None


def kernel(**inputs):
    global _NC_CACHE
    if _NC_CACHE is None:
        _NC_CACHE = build_nc()
    nc = _NC_CACHE
    in_maps = [_host_prep(inputs, c) for c in range(8)]
    res = run_bass_kernel_spmd(nc, in_maps, core_ids=list(range(8)))
    out = np.zeros((4, S, D), np.float32)
    for c in range(8):
        b, half = c // 2, c % 2
        out[b, half * TI:(half + 1) * TI, :] = res.results[c]["out_tm"]
    return out


# version 17
# speedup vs baseline: 1.1175x; 1.0140x over previous
import math
import numpy as np
import ml_dtypes
import concourse.bass as bass
import concourse.mybir as mybir
from concourse.bass_utils import run_bass_kernel_spmd
from concourse.alu_op_type import AluOpType as OP

F32 = mybir.dt.float32
F32R = mybir.dt.float32r
BF16 = mybir.dt.bfloat16
AF = mybir.ActivationFunctionType
AXX = mybir.AxisListType.X

D = 2048
S = 2048
TI = 1024
DC = 16
NSYN = 8192
KSIM = 204
KFFN = 819
LN_EPS = 1e-5
MAGIC = 12582912.0
NSEC = 6
IC = 256
NIC = TI // IC
TOKB = 256
NTOKB = TI // TOKB
Z0_SIM = 1.2872
Z0_FFN = 1.2816
ISQ2 = 1.0 / math.sqrt(2.0)
BIGM = 1.0e7


class Emit:
    ENGS = ("sync", "vector", "scalar", "tensor", "gpsimd")

    def __init__(self):
        self.q = {e: [] for e in self.ENGS}
        self.cnt = {}
        self.handles = {}

    def sem(self, name):
        self.cnt.setdefault(name, 0)

    def op(self, eng, fn, inc=None, waits=()):
        waits = tuple((s, v) for (s, v) in waits if v > 0)

        def cl(e, fn=fn, inc=inc, waits=waits):
            for (s, v) in waits:
                e.wait_ge(self.handles[s], v)
            inst = fn(e)
            if inc is not None and inst is not None:
                inst.then_inc(self.handles[inc[0]], inc[1])

        self.q[eng].append(cl)
        if inc is not None:
            self.cnt[inc[0]] += inc[1]

    def wait(self, eng, sem, val):
        if val > 0:
            def cl(e, sem=sem, val=val):
                e.wait_ge(self.handles[sem], val)
            self.q[eng].append(cl)

    def c(self, name):
        return self.cnt.get(name, 0)


def build_nc():
    nc = bass.Bass("TRN2", target_bir_lowering=False,
                   detect_race_conditions=False)
    E = Emit()
    SEMS = ("dx", "dg", "sv", "ss", "st", "sg", "dout", "dt", "ds", "scc")
    for s_ in SEMS:
        E.sem(s_)

    da = {}
    def din(name, shape, dt=F32):
        da[name] = nc.dram_tensor(name, list(shape), dt, kind="ExternalInput").ap()

    din("x_fm", (D, S))
    din("w1T", (D, 128)); din("b1c", (128, 1))
    din("w2T", (128, 64)); din("b2st", (128, 1)); din("w3blk", (128, 2))
    din("b3s", (1, 1))
    din("wqkT", (D, 128)); din("bqk_row", (1, 128))
    din("wvT", (D, 64)); din("bv_row", (1, 64))
    din("woutaT", (64, D)); din("bout_pp", (128, DC))
    din("wsynT", (D // 8, NSYN)); din("bsyn_row", (1, NSYN))
    din("woutfT", (NSYN // 8, D)); din("boutf_pp", (128, DC))
    din("gammaA_pp", (128, DC)); din("betaA_pp", (128, DC))
    din("gammaF_pp", (128, DC)); din("betaF_pp", (128, DC))
    din("c_pp", (128, 8))
    din("wmod", (TI, 6 * D))
    din("vauginit", (128, 128), BF16)
    din("idt_bf", (128, 128), BF16)
    din("idt_f32", (128, 128))
    din("iota16_in", (128, 16))
    din("ones_bf_row", (1, 512), BF16)
    din("ones_f32_col", (128, 1))
    din("sc12", (128, 1))
    out_tm = nc.dram_tensor("out_tm", [TI, D], F32, kind="ExternalOutput").ap()

    d_rows = nc.dram_tensor("d_rows", [16, S], F32, kind="Internal").ap()
    d_mod = nc.dram_tensor("d_mod", [1, 6 * D], F32, kind="Internal").ap()
    d_modp = nc.dram_tensor("d_modp", [1, 6 * D], F32, kind="Internal").ap()
    d_scal = nc.dram_tensor("d_scal", [1, 64], F32, kind="Internal").ap()
    d_cs = nc.dram_tensor("d_cs", [1, 32], F32, kind="Internal").ap()
    d_col = nc.dram_tensor("d_col", [1, 128], F32, kind="Internal").ap()
    d_tqsyn = nc.dram_tensor("d_tqsyn", [D, NSYN], BF16, kind="Internal",
                             addr_space="Shared").ap()
    d_tqout = nc.dram_tensor("d_tqout", [NSYN, D], BF16, kind="Internal",
                             addr_space="Shared").ap()
    d_tqsyn_sl = nc.dram_tensor("d_tqsyn_sl", [D // 8, NSYN], BF16,
                                kind="Internal").ap()
    d_tqout_sl = nc.dram_tensor("d_tqout_sl", [NSYN // 8, D], BF16,
                                kind="Internal").ap()
    d_spart = nc.dram_tensor("d_spart", [1, 2], F32, kind="Internal").ap()
    d_ssum = nc.dram_tensor("d_ssum", [1, 2], F32, kind="Internal",
                            addr_space="Shared").ap()
    d_hidT = nc.dram_tensor("d_hidT", [NSYN, TI], BF16, kind="Internal").ap()
    d_x2 = nc.dram_tensor("d_x2", [D, TI], F32, kind="Internal").ap()
    d_partial = nc.dram_tensor("d_partial", [D, TI], F32, kind="Internal").ap()
    d_rowsb = nc.dram_tensor("d_rowsb", [4, S], BF16, kind="Internal").ap()

    import contextlib
    stack = contextlib.ExitStack()
    def sbt(name, p, f, dt):
        return stack.enter_context(nc.sbuf_tensor(name, [p, f], dt))

    AR1 = sbt("AR1", 128, 16384, F32)
    AR2 = sbt("AR2", 128, 16384, F32)
    AR5 = sbt("AR5", 128, 2048, F32)
    AR6 = sbt("AR6", 128, 4096, BF16)
    xraw = sbt("xraw", 128, 2048, F32)
    xbt = sbt("xbt", 128, 2048, BF16)
    xsq = sbt("xsq", 128, 2048, BF16)
    brow = sbt("brow", 128, 4096, BF16)
    brf = sbt("brf", 128, 1024, F32)
    h1n = sbt("h1n", 128, 2048, BF16)
    qks = sbt("qks", 128, 2048, BF16)
    vaug = sbt("vaug", 128, DC * 128, BF16)
    w2t = sbt("w2t", 128, 64, BF16)
    w3b = sbt("w3b", 128, 2, BF16)
    hidst = sbt("hidst", 128, 2048, BF16)
    scr2 = sbt("scr2", 128, 2048, F32)
    stw = sbt("stw", 128, 4 * DC, BF16)
    stw2 = sbt("stw2", 128, 2 * DC, BF16)
    modp = sbt("modp", 128, 96, F32)
    mcon = sbt("mcon", 128, 112, F32)
    gcon = sbt("gcon", 128, 64, F32)
    ppst = sbt("ppst", 128, 12 * DC, F32)
    ppbf = sbt("ppbf", 128, 64, BF16)
    tmpk = sbt("tmpk", 128, 32, F32)
    iota16 = sbt("iota16", 128, 16, F32)
    sel16 = sbt("sel16", 128, 16, F32)
    cand16 = sbt("cand16", 128, 16, F32)
    tcol1 = sbt("tcol1", 128, 1, F32)
    tcol2 = sbt("tcol2", 128, 1, F32)
    idbf = sbt("idbf", 128, 128, BF16)
    idf2 = sbt("idf2", 128, 128, F32)
    onebf = sbt("onebf", 1, 512, BF16)
    onef = sbt("onef", 128, 1, F32)
    onebc = sbt("onebc", 128, 1, BF16)
    srow = sbt("srow", 1, 256, F32)
    scpp = sbt("scpp", 128, DC, F32)
    scrr = sbt("scrr", 128, DC, F32R)
    b1cs = sbt("b1cs", 128, 1, F32)
    b2cs = sbt("b2cs", 128, 1, F32)
    sc12s = sbt("sc12s", 128, 1, F32)
    bobpp = sbt("bobpp", 128, DC, F32)
    bfbpp = sbt("bfbpp", 128, DC, F32)
    gapp = sbt("gapp", 128, DC, F32)
    bapp = sbt("bapp", 128, DC, F32)
    gfpp = sbt("gfpp", 128, DC, F32)
    bfpp = sbt("bfpp", 128, DC, F32)
    bqkr = sbt("bqkr", 1, 128, BF16)
    bvr = sbt("bvr", 1, 64, BF16)
    s1row = sbt("s1row", 1, 128, BF16)
    sclr = sbt("sclr", 1, 16, F32)
    wmodr = sbt("wmodr", 128, 1024, F32R)
    psum_t = stack.enter_context(nc.psum_tensor("p_all", [128, 4096], F32))
    for s_ in SEMS:
        E.handles[s_] = stack.enter_context(nc.semaphore(s_))
    stack.enter_context(nc.allow_non_contiguous_dma(
        reason="small pp-layout rearrange DMAs (KB-scale)"))
    block = stack.enter_context(nc.Block())

    p_all = psum_t[:]
    p_bf = psum_t[:].bitcast(BF16)
    p_mm = psum_t[:, 0:2048]
    p_sc = psum_t[:, 2048:4096]
    AR1b = AR1[:].bitcast(BF16)
    AR2b = AR2[:].bitcast(BF16)
    AR5b = AR5[:].bitcast(BF16)
    wtr = AR5[:]
    mrow = AR5[:]
    wgt = brow[:, 2048:4096]
    h1b = brf[:]
    kjit = AR6[:]
    qjit = xbt[:]
    utile = hidst[:]
    woat = AR6[0:64, 0:2048]
    aotb = hidst[0:64, 0:1024]
    aot = scr2[0:64, 1024:2048]
    aden = scr2[64:128, 1024:2048]
    rowsb = scr2[0:8, 0:2048]
    denr = xraw[0:64, 0:1024]
    x2l = xraw[:, 0:1024]
    ytile = qks[:].bitcast(F32)
    otile = xbt[:].bitcast(F32)
    scr2b = scr2[:].bitcast(BF16)
    bsyn_rows = None

    def dma(dst, src, waits=(), eng="sync", sem="dx", pipe=False):
        waits = list(waits)
        if not pipe:
            waits.append((sem, E.c(sem)))
        E.op(eng, lambda e, d=dst, s=src: e.dma_start(out=d, in_=s), inc=(sem, 16),
             waits=waits)
        return E.c(sem)

    def V(fn, waits=(), dep=True):
        if dep:
            def fn2(e, fn=fn):
                e.drain()
                return fn(e)
        else:
            fn2 = fn
        E.op("vector", fn2, inc=("sv", 1), waits=waits)
        return E.c("sv")

    def SC(fn, waits=(), dep=True):
        if dep:
            def fn2(e, fn=fn):
                e.drain()
                return fn(e)
        else:
            fn2 = fn
        E.op("scalar", fn2, inc=("ss", 1), waits=waits)
        return E.c("ss")

    def T(fn, waits=()):
        E.op("tensor", fn, inc=("st", 1), waits=waits)
        return E.c("st")

    def G(fn, waits=()):
        E.op("gpsimd", fn, inc=("sg", 1), waits=waits)
        return E.c("sg")

    dma(idbf[:], da["idt_bf"])
    dma(idf2[:], da["idt_f32"])
    dma(iota16[:], da["iota16_in"])
    dma(onebf[:], da["ones_bf_row"])
    dma(onef[:], da["ones_f32_col"])
    dma(gcon[:, 0:1], da["b3s"].to_broadcast([128, 1]))
    dma(scpp[:, 0:8], da["c_pp"])
    dma(b1cs[:], da["b1c"])
    dma(b2cs[:], da["b2st"])
    dma(sc12s[:], da["sc12"])
    dma(bobpp[:], da["bout_pp"])
    dma(bfbpp[:], da["boutf_pp"])
    dma(gapp[:], da["gammaA_pp"])
    dma(bapp[:], da["betaA_pp"])
    dma(gfpp[:], da["gammaF_pp"])
    dma(bfpp[:], da["betaF_pp"])
    dma(wtr[:, 0:64], da["w2T"])
    dma(wtr[:, 64:66], da["w3blk"])
    dma(wtr[0:1, 128:256], da["bqk_row"])
    dma(wtr[0:1, 256:320], da["bv_row"])
    ld0 = E.c("dx")
    V(lambda e: e.tensor_copy(w2t[:], wtr[:, 0:64]), [("dx", ld0)])
    V(lambda e: e.tensor_copy(w3b[:], wtr[:, 64:66]))
    V(lambda e: e.tensor_copy(bqkr[:], wtr[0:1, 128:256]))
    V(lambda e: e.tensor_copy(bvr[:], wtr[0:1, 256:320]))
    V(lambda e: e.tensor_copy(onebc[:], onef[:]))
    sm0 = E.c("sv")
    SC(lambda e: e.activation(scpp[:, 0:8], scpp[:, 0:8], AF.Silu), [("dx", ld0)])
    silu_v = V(lambda e: e.tensor_copy(scrr[:, 0:8], scpp[:, 0:8]),
               [("ss", E.c("ss"))])

    consts_v = E.c("sv")

    ev_prev = 0
    rdy_hist = [0, 0]
    last_mm = 0
    KCH = TI // 128
    for cg in range(3):
        for dch in range(KCH):
            slot = dch % 2
            wdeps = [("sv", silu_v)] if (cg == 0 and dch == 0) else []
            if rdy_hist[slot]:
                wdeps.append(("sv", rdy_hist[slot]))
            w = dma(AR1[:, slot * 4096:(slot + 1) * 4096],
                    da["wmod"][dch * 128:(dch + 1) * 128, cg * 4096:(cg + 1) * 4096],
                    waits=wdeps, pipe=True)
            for pc4 in range(4):
                rdeps = [("dx", w)] if pc4 == 0 else []
                if last_mm:
                    rdeps.append(("st", last_mm))
                rdy = V(lambda e, slot=slot, pc4=pc4: e.tensor_copy(
                    wmodr[:, 0:1024],
                    AR1[:, slot * 4096 + pc4 * 1024:
                           slot * 4096 + (pc4 + 1) * 1024]), rdeps)
                for n2 in range(2):
                    nch = 2 * pc4 + n2
                    mmw = [("sv", rdy)] if n2 == 0 else []
                    if cg > 0 and dch == 0 and nch == 0:
                        mmw.append(("sv", ev_prev))
                    T(lambda e, nch=nch, n2=n2, dch=dch: e.matmul(
                        p_all[0:1, nch * 512:(nch + 1) * 512],
                        scrr[:, dch:dch + 1],
                        wmodr[:, n2 * 512:(n2 + 1) * 512],
                        start=(dch == 0), stop=(dch == KCH - 1)), mmw)
                last_mm = E.c("st")
            rdy_hist[slot] = rdy
        V(lambda e: e.tensor_copy(mrow[0:1, 0:2048], p_all[0:1, 0:2048]),
          [("st", E.c("st"))])
        ev_prev = V(lambda e: e.tensor_copy(scr2[0:1, 0:2048], p_all[0:1, 2048:4096]))
        dma(d_modp[:, cg * 4096:cg * 4096 + 2048], mrow[0:1, 0:2048],
            waits=[("sv", ev_prev)])
        dma(d_modp[:, cg * 4096 + 2048:(cg + 1) * 4096], scr2[0:1, 0:2048],
            waits=[("sv", ev_prev)])
    mod_d = E.c("dx")
    E.op("gpsimd", lambda e: e.collective_compute(
        "AllReduce", OP.add, replica_groups=[[0, 1], [2, 3], [4, 5], [6, 7]],
        ins=[d_modp.opt()], outs=[d_mod.opt()]), inc=("scc", 1),
        waits=[("dx", mod_d)])
    mod_cc = E.c("scc")
    for m_ in range(6):
        w = dma(modp[:, m_ * 16:(m_ + 1) * 16],
                d_mod[:, m_ * 2048:(m_ + 1) * 2048]
                .rearrange("o (k p) -> (o p) k", k=16),
                waits=[("scc", mod_cc)] if m_ == 0 else ())
    mod_pp = w

    sa_t = modp[:, 0:16]; ca_t = modp[:, 16:32]; gaw = modp[:, 32:48]
    sb_t = modp[:, 48:64]; cb_t = modp[:, 64:80]; gbw = modp[:, 80:96]
    A1 = mcon[:, 0:16]; A2 = mcon[:, 16:32]
    gaT = mcon[:, 32:48]; gbT = mcon[:, 48:64]
    A1sq = mcon[:, 64:80]; A1B1 = mcon[:, 80:96]; B1sq = mcon[:, 96:112]
    V(lambda e: e.tensor_scalar(A1, ca_t, 1.0, None, OP.add), [("dx", mod_pp)])
    V(lambda e: e.tensor_scalar(A2, cb_t, 1.0, None, OP.add))
    for (gsrc, gdst) in ((gaw, gaT), (gbw, gbT)):
        V(lambda e, gs=gsrc, gd=gdst: e.tensor_scalar(gd, gs, 0.0, 2.0, OP.is_ge,
                                                      OP.mult))
        V(lambda e, gd=gdst: e.tensor_scalar(gd, gd, 1.0, None, OP.subtract))
        V(lambda e, gs=gsrc: e.tensor_scalar(tmpk[:, 0:16], gs, -1.0, None, OP.mult))
        V(lambda e, gs=gsrc: e.tensor_tensor(tmpk[:, 0:16], tmpk[:, 0:16], gs, OP.max))
        V(lambda e: e.tensor_scalar(tmpk[:, 0:16], tmpk[:, 0:16], 0.25, None, OP.max))
        V(lambda e, gd=gdst: e.tensor_tensor(gd, gd, tmpk[:, 0:16], OP.mult))
    V(lambda e: e.tensor_tensor(A1sq, A1, A1, OP.mult))
    V(lambda e: e.tensor_tensor(A1B1, A1, sa_t, OP.mult))
    V(lambda e: e.tensor_tensor(B1sq, sa_t, sa_t, OP.mult))
    stwv = stw[:].rearrange("p (k f) -> p k f", f=4)
    stw2v = stw2[:].rearrange("p (k f) -> p k f", f=2)
    V(lambda e: e.memset(stwv[:, :, 0:1], 1.0))
    V(lambda e: e.tensor_copy(stwv[:, :, 1:2], A1.unsqueeze(2)))
    V(lambda e: e.tensor_copy(stwv[:, :, 2:3], A1sq.unsqueeze(2)))
    V(lambda e: e.tensor_copy(stwv[:, :, 3:4], A1B1.unsqueeze(2)))
    V(lambda e: e.memset(stw2v[:, :, 0:1], 1.0))
    drv_v = V(lambda e: e.tensor_copy(stw2v[:, :, 1:2], A1sq.unsqueeze(2)))
    m1 = T(lambda e: e.matmul(p_sc[0:1, 0:112], onef[:], mcon[:, 0:112],
                              start=True, stop=True), [("sv", drv_v)])
    m2 = T(lambda e: e.matmul(p_sc[0:1, 112:128], onef[:], sa_t,
                              start=True, stop=True))
    V(lambda e: e.tensor_copy(srow[0:1, 0:128], p_sc[0:1, 0:128]), [("st", m2)])
    for i, off in enumerate((0, 64, 80, 96, 112)):
        V(lambda e, off=off, i=i: e.reduce_sum(srow[0:1, 140 + i:141 + i],
                                               srow[0:1, off:off + 16], axis=AXX))
    cscal_v = V(lambda e: e.tensor_scalar(srow[0:1, 145:146], srow[0:1, 144:145],
                                          1.0 / D, None, OP.mult))
    w = dma(d_scal[:, 0:6], srow[0:1, 140:146], waits=[("sv", cscal_v)])
    gcon_scal = dma(gcon[:, 8:14], d_scal[:, 0:6].to_broadcast([128, 6]),
                    waits=[("dx", w)])

    cast_hist = [0] * DC
    sq_hist = [0] * DC
    mm_hist = [0] * DC
    for dch in range(DC):
        wd = [("sv", E.c("sv")), ("ss", E.c("ss"))]
        if dch == 0:
            wd.append(("dx", gcon_scal))
        w = dma(xraw[:, 0:2048], da["x_fm"][dch * 128:(dch + 1) * 128, :], waits=wd,
                pipe=True)
        cd = [("dx", w)]
        if dch >= 1:
            cd.append(("st", mm_hist[dch - 1]))
        cast_hist[dch] = V(lambda e: e.tensor_copy(xbt[:, 0:2048], xraw[:, 0:2048]),
                           cd)
        sd = [("sv", cast_hist[dch])]
        if dch >= 1:
            sd.append(("st", mm_hist[dch - 1]))
        sq_hist[dch] = SC(lambda e: e.activation(xsq[:, 0:2048], xbt[:, 0:2048],
                                                 AF.Square), sd)
        for nch in range(4):
            T(lambda e, dch=dch, nch=nch: e.matmul(
                p_sc[0:4, nch * 512:(nch + 1) * 512],
                stw[:, 4 * dch:4 * dch + 4],
                xbt[:, nch * 512:(nch + 1) * 512],
                start=(dch == 0), stop=(dch == DC - 1)),
              [("sv", cast_hist[dch])] if nch == 0 else ())
            T(lambda e, dch=dch, nch=nch: e.matmul(
                p_mm[0:2, nch * 512:(nch + 1) * 512],
                stw2[:, 2 * dch:2 * dch + 2],
                xsq[:, nch * 512:(nch + 1) * 512],
                start=(dch == 0), stop=(dch == DC - 1)),
              [("ss", sq_hist[dch])] if nch == 0 else ())
        mm_hist[dch] = E.c("st")
    V(lambda e: e.tensor_copy(rowsb[0:4, :], p_sc[0:4, 0:2048]), [("st", E.c("st"))])
    ev_rows = V(lambda e: e.tensor_copy(AR5[0:2, 0:2048], p_mm[0:2, 0:2048]))
    w = dma(d_rows[0:4, :], rowsb[0:4, :], waits=[("sv", ev_rows)])
    w = dma(d_rows[4:6, :], AR5[0:2, 0:2048], waits=[("sv", ev_rows)])
    for r_ in range(6):
        pp_d = dma(ppst[:, r_ * 16:(r_ + 1) * 16],
                   d_rows[r_:r_ + 1, :].rearrange("r (k p) -> (r p) k", k=16),
                   waits=[("dx", w)] if r_ == 0 else ())

    B_ = lambda i: ppst[:, i * 16:(i + 1) * 16]
    Sx, SAx, SA2x, SABx, Sxx, SA2xx = B_(0), B_(1), B_(2), B_(3), B_(4), B_(5)
    mu1, rs1, muh, rsh, t1_, t2_ = B_(6), B_(7), B_(8), B_(9), B_(10), B_(11)
    gsc = lambda i: gcon[:, 8 + i:9 + i]
    V(lambda e: e.tensor_scalar(mu1, Sx, 1.0 / D, None, OP.mult), [("dx", pp_d)])
    V(lambda e: e.tensor_tensor(t1_, mu1, mu1, OP.mult))
    V(lambda e: e.tensor_scalar(t2_, Sxx, 1.0 / D, None, OP.mult))
    V(lambda e: e.tensor_tensor(t2_, t2_, t1_, OP.subtract))
    V(lambda e: e.tensor_scalar(t2_, t2_, LN_EPS, None, OP.add))
    rv = V(lambda e: e.reciprocal(t2_, t2_))
    rs1_s = SC(lambda e: e.activation(rs1, t2_, AF.Sqrt), [("sv", rv)])
    V(lambda e: e.scalar_tensor_tensor(t1_, mu1, gsc(0), SAx, OP.mult, OP.subtract),
      [("ss", rs1_s)])
    V(lambda e: e.tensor_tensor(t1_, t1_, rs1, OP.mult))
    V(lambda e: e.tensor_scalar(t1_, t1_, -1.0 / D, None, OP.mult))
    V(lambda e: e.tensor_scalar(muh, t1_, gsc(5), None, OP.add))
    V(lambda e: e.tensor_tensor(t1_, mu1, SA2x, OP.mult))
    V(lambda e: e.scalar_tensor_tensor(t1_, t1_, -2.0, SA2xx, OP.mult, OP.add))
    V(lambda e: e.tensor_tensor(t2_, mu1, mu1, OP.mult))
    V(lambda e: e.tensor_scalar(t2_, t2_, gsc(1), None, OP.mult))
    V(lambda e: e.tensor_tensor(t1_, t1_, t2_, OP.add))
    V(lambda e: e.tensor_tensor(t2_, rs1, rs1, OP.mult))
    V(lambda e: e.tensor_tensor(t1_, t1_, t2_, OP.mult))
    V(lambda e: e.scalar_tensor_tensor(t2_, mu1, gsc(2), SABx, OP.mult, OP.subtract))
    V(lambda e: e.scalar_tensor_tensor(t2_, t2_, -2.0, rs1, OP.mult, OP.mult))
    V(lambda e: e.tensor_tensor(t1_, t1_, t2_, OP.add))
    V(lambda e: e.tensor_scalar(t1_, t1_, gsc(3), None, OP.add))
    V(lambda e: e.tensor_scalar(t1_, t1_, 1.0 / D, None, OP.mult))
    V(lambda e: e.tensor_tensor(t2_, muh, muh, OP.mult))
    V(lambda e: e.tensor_tensor(t1_, t1_, t2_, OP.subtract))
    V(lambda e: e.tensor_scalar(t1_, t1_, LN_EPS, None, OP.add))
    rv = V(lambda e: e.reciprocal(t1_, t1_))
    rsh_s = SC(lambda e: e.activation(rsh, t1_, AF.Sqrt), [("sv", rv)])
    ppc = V(lambda e: e.tensor_copy(ppbf[:].rearrange("p (r k) -> p r k", r=4),
                                    ppst[:].rearrange("p (r k) -> p r k", r=12)
                                    [:, 6:10, :]), [("ss", rsh_s)])
    for r_ in range(4):
        w = dma(d_rowsb[r_:r_ + 1, :].rearrange("r (k p) -> (r p) k", k=16),
                ppbf[:, r_ * 16:(r_ + 1) * 16], waits=[("sv", ppc)] if r_ == 0 else ())
    mu1b = brow[:, 0:2048]; rs1b = brow[:, 2048:4096]
    muhb = vaug[:, 0:2048]; rshb = qks[:, 0:2048]
    brow_v = 0
    for i, dst in enumerate((mu1b, rs1b, muhb, rshb)):
        brow_v = dma(dst, d_rowsb[i:i + 1, :].to_broadcast([128, 2048]),
                     waits=[("dx", w)])
    E.wait("vector", "dx", brow_v)
    brow_v_sem = ("dx", brow_v)

    gA1 = mcon[:, 64:80]
    gB1o = mcon[:, 80:96]
    V(lambda e: e.tensor_tensor(gA1, gaT, A1, OP.mult),
      [("st", E.c("st")), ("dx", E.c("dx"))])
    V(lambda e: e.tensor_tensor(gB1o, sa_t, bobpp[:], OP.add))
    V(lambda e: e.tensor_tensor(gB1o, gaT, gB1o, OP.mult))
    xn_hist = [0] * DC
    rstt_hist = 0
    for dch in range(DC):
        wd = [brow_v_sem, ("sv", E.c("sv")), ("ss", E.c("ss"))]
        w = dma(xraw[:, 0:2048], da["x_fm"][dch * 128:(dch + 1) * 128, :], waits=wd,
                pipe=True)
        cd = [("dx", w)]
        if dch >= 1:
            cd.append(("ss", xn_hist[dch - 1]))
        V(lambda e: e.tensor_copy(xbt[:, 0:2048], xraw[:, 0:2048]), cd)
        xb_ = xbt[:, 0:2048]
        xh_ = xsq[:, 0:2048]
        V(lambda e, xh_=xh_, xb_=xb_: e.tensor_tensor(xh_, xb_, mu1b, OP.subtract))
        xh_v = V(lambda e, xh_=xh_: e.tensor_tensor(xh_, xh_, rs1b, OP.mult))
        h_s = SC(lambda e, xh_=xh_, dch=dch: e.activation(
            scr2[:, 0:1024], xh_[:, 0:1024], AF.Identity,
            bias=gB1o[:, dch:dch + 1], scale=gA1[:, dch:dch + 1]),
            [("sv", xh_v), ("sv", rstt_hist)])
        rstt_hist = V(lambda e, dch=dch: e.tensor_tensor(
            AR2[:, dch * 1024:(dch + 1) * 1024], scr2[:, 0:1024],
            xraw[:, 0:1024], OP.add), [("ss", h_s)])
        v1_s = SC(lambda e, xb_=xb_, xh_=xh_, dch=dch: e.activation(
            xb_, xh_, AF.Identity, bias=sa_t[:, dch:dch + 1],
            scale=A1[:, dch:dch + 1]), [("sv", E.c("sv"))])
        V(lambda e, xh_=xh_, xb_=xb_: e.tensor_tensor(xh_, xb_, muhb, OP.subtract),
          [("ss", v1_s)])
        xnb_v = V(lambda e, xh_=xh_: e.tensor_tensor(xh_, xh_, rshb, OP.mult))
        xn_hist[dch] = SC(lambda e, xh_=xh_, dch=dch: e.activation(
            AR1b[:, dch * 2048:(dch + 1) * 2048], xh_, AF.Identity,
            bias=bapp[:, dch:dch + 1], scale=gapp[:, dch:dch + 1]), [("sv", xnb_v)])
    xn_s = E.c("ss")
    r_v = E.c("sv")

    xn_ = lambda dch: AR1b[:, dch * 2048:(dch + 1) * 2048]
    w = dma(wtr[:, 0:2048].rearrange("p (c m) -> p c m", m=128),
            da["w1T"].rearrange("(c p) m -> p c m", p=128),
            waits=[("ss", xn_s)])
    wgt_v = V(lambda e: e.tensor_copy(wgt, wtr[:, 0:2048]), [("dx", w)])
    evac_hist = [0, 0]
    for nch in range(4):
        bank = nch % 2
        for dch in range(DC):
            wd = []
            if dch == 0:
                wd = [("ss", xn_s), ("sv", wgt_v)]
                if nch >= 2:
                    wd.append(("sv", evac_hist[bank]))
            T(lambda e, bank=bank, nch=nch, dch=dch: e.matmul(
                p_mm[:, bank * 512:bank * 512 + 512],
                wgt[:, dch * 128:(dch + 1) * 128],
                xn_(dch)[:, nch * 512:(nch + 1) * 512],
                start=(dch == 0), stop=(dch == DC - 1)), wd)
        mmv = E.c("st")
        ev = V(lambda e, bank=bank, nch=nch: e.tensor_scalar(
            h1n[:, nch * 512:(nch + 1) * 512],
            p_mm[:, bank * 512:bank * 512 + 512], -1.0, None, OP.mult),
          [("st", mmv)])
        if nch < 2:
            ev = V(lambda e, bank=bank, nch=nch: e.tensor_scalar(
                h1b[:, nch * 512:(nch + 1) * 512],
                p_mm[:, bank * 512:bank * 512 + 512], b1cs[:, 0:1], None, OP.add))
        evac_hist[bank] = ev
    h1_v = E.c("sv")
    w = dma(wtr[:, 0:2048].rearrange("p (c m) -> p c m", m=128),
            da["wqkT"].rearrange("(c p) m -> p c m", p=128),
            waits=[("st", E.c("st"))])
    wgt_v = V(lambda e: e.tensor_copy(wgt, wtr[:, 0:2048]), [("dx", w)])
    wb = dma(wtr[0:1, 0:128], da["bqk_row"], waits=[("sv", wgt_v)])
    wgt_v = V(lambda e: e.tensor_copy(bqkr[:], wtr[0:1, 0:128]), [("dx", wb)])
    evac_hist = [0, 0]
    for nch in range(4):
        bank = nch % 2
        for dch in range(DC):
            wd = []
            if dch == 0:
                wd = [("sv", h1_v), ("sv", wgt_v)] if nch < 2 else [("sv", evac_hist[bank])]
            T(lambda e, bank=bank, nch=nch, dch=dch: e.matmul(
                p_sc[:, bank * 512:bank * 512 + 512],
                wgt[:, dch * 128:(dch + 1) * 128],
                xn_(dch)[:, nch * 512:(nch + 1) * 512],
                start=(dch == 0), stop=False), wd)
        T(lambda e, bank=bank: e.matmul(
            p_sc[:, bank * 512:bank * 512 + 512],
            bqkr[:], onebf[:, 0:512], start=False, stop=True))
        mmv = E.c("st")
        evac_hist[bank] = V(lambda e, bank=bank, nch=nch: e.tensor_scalar(
            qks[:, nch * 512:(nch + 1) * 512],
            p_sc[:, bank * 512:bank * 512 + 512],
            sc12s[:, 0:1], None, OP.mult), [("st", mmv)])
    qk_v = E.c("sv")
    w = dma(wtr[:, 0:1024].rearrange("p (c m) -> p c m", m=64),
            da["wvT"].rearrange("(c p) m -> p c m", p=128),
            waits=[("st", E.c("st"))])
    wgt_v = V(lambda e: e.tensor_copy(wgt[:, 0:1024], wtr[:, 0:1024]), [("dx", w)])
    wb = dma(wtr[0:1, 0:64], da["bv_row"], waits=[("sv", wgt_v)])
    wgt_v = V(lambda e: e.tensor_copy(bvr[:], wtr[0:1, 0:64]), [("dx", wb)])
    vgi = dma(vaug[:].rearrange("p (c m) -> p c m", m=128),
              da["vauginit"].unsqueeze(1).to_broadcast([128, DC, 128]),
              waits=[("ss", xn_s), ("st", E.c("st"))])
    vaugv = vaug[:].rearrange("p (c g f) -> p c g f", c=DC, g=32)
    evac_hist = [0, 0, 0, 0]
    for jc in range(DC):
        bank = jc % 4
        for dch in range(DC):
            wd = []
            if dch == 0:
                wd = [("sv", qk_v), ("sv", wgt_v)] if jc < 4 else [("sv", evac_hist[bank])]
            T(lambda e, bank=bank, jc=jc, dch=dch: e.matmul(
                p_mm[:, bank * 512:bank * 512 + 64],
                xn_(dch)[:, jc * 128:(jc + 1) * 128],
                wgt[:, dch * 64:(dch + 1) * 64],
                start=(dch == 0), stop=False), wd)
        T(lambda e, bank=bank: e.matmul(
            p_mm[:, bank * 512:bank * 512 + 64],
            onebf[0:1, 0:128], bvr[:], start=False, stop=True))
        mmv = E.c("st")
        evac_hist[bank] = V(lambda e, bank=bank, jc=jc: e.tensor_copy(
            vaugv[:, jc, :, 0:2],
            p_mm[:, bank * 512:bank * 512 + 64].rearrange("p (g f) -> p g f", g=32)),
            [("st", mmv), ("dx", vgi)] if jc == 0 else [("st", mmv)])
    vt_v = E.c("sv")
    csm = V(lambda e: e.reduce_max(tcol1[:], qks[:], axis=AXX,
                                   apply_absolute_value=True), [("sv", vt_v)])
    w = dma(d_col[:, 0:128].rearrange("o p -> p o"), tcol1[:, 0:1],
            waits=[("sv", csm)])
    w = dma(srow[0:1, 0:128], d_col[:, 0:128], waits=[("dx", w)])
    srv = srow[0:1, 0:128].rearrange("o (a h f) -> o a h f", a=2, h=32)
    V(lambda e: e.tensor_tensor(mrow[0:1, 0:64].rearrange("o (a h) -> o a h", a=2),
                                srv[:, :, :, 0], srv[:, :, :, 1], OP.max), [("dx", w)])
    V(lambda e: e.tensor_tensor(mrow[0:1, 64:96], mrow[0:1, 0:32], mrow[0:1, 32:64],
                                OP.mult))
    csv = V(lambda e: e.tensor_scalar(mrow[0:1, 64:96], mrow[0:1, 64:96], 60.0, -1.0,
                                      OP.min, OP.mult))
    w = dma(d_cs[:, 0:32], mrow[0:1, 64:96], waits=[("sv", csv)])
    cs_pp = dma(gcon[:, 16:48], d_cs[:, 0:32].to_broadcast([128, 32]),
                waits=[("dx", w)])
    E.wait("vector", "dx", cs_pp)

    def topk(zblk, W, k, z0, tcol, scr_bf, u_f32, pre, sw=None):
        sw = sw or W
        np_ = W // sw
        col = lambda i: tmpk[:, i:i + 1]
        pw = list(pre)
        for pc in range(np_):
            SC(lambda e, pc=pc: e.activation(
                scr_bf[:, 0:sw], zblk[:, pc * sw:(pc + 1) * sw], AF.Square,
                accum_out=col(12 + pc)), pw)
        sq_s = E.c("ss")
        V(lambda e: e.reduce_sum(col(0), zblk[:, 0:W], axis=AXX), pw)
        if np_ == 2:
            V(lambda e: e.tensor_tensor(col(1), col(12), col(13), OP.add),
              [("ss", sq_s)])
        else:
            V(lambda e: e.tensor_copy(col(1), col(12)), [("ss", sq_s)])
        V(lambda e: e.tensor_scalar(col(0), col(0), 1.0 / W, None, OP.mult))
        V(lambda e: e.tensor_scalar(col(1), col(1), 1.0 / W, None, OP.mult))
        V(lambda e: e.scalar_tensor_tensor(col(2), col(0), col(0), col(1), OP.mult,
                                           OP.subtract))
        vv = V(lambda e: e.tensor_scalar(col(2), col(2), -1.0, 1e-12, OP.mult, OP.max))
        sg_s = SC(lambda e: e.activation(col(3), col(2), AF.Sqrt), [("sv", vv)])
        V(lambda e: e.scalar_tensor_tensor(col(4), col(3), z0, col(0), OP.mult, OP.add),
          [("ss", sg_s)])
        V(lambda e: e.scalar_tensor_tensor(col(5), col(3), -8.0, col(0), OP.mult,
                                           OP.add))
        V(lambda e: e.scalar_tensor_tensor(col(6), col(3), 8.0, col(0), OP.mult,
                                           OP.add))
        V(lambda e: e.memset(col(7), float(W)))
        V(lambda e: e.memset(col(8), 0.0))

        def count_scan(tgt, op):
            for pc in range(np_):
                V(lambda e, pc=pc, op=op: e.tensor_scalar(
                    scr_bf[:, 0:sw], zblk[:, pc * sw:(pc + 1) * sw],
                    col(4) if op == OP.is_ge else col(6), None,
                    op, OP.add, accum_out=col(12 + pc)))
            if np_ == 2:
                V(lambda e, tgt=tgt: e.tensor_tensor(tgt, col(12), col(13), OP.add))
            else:
                V(lambda e, tgt=tgt: e.tensor_copy(tgt, col(12)))

        for it in range(NSEC):
            count_scan(col(11), OP.is_ge)
            V(lambda e: e.tensor_scalar(col(9), col(11), k + 0.5, None, OP.is_ge))
            V(lambda e: e.tensor_scalar(col(10), col(9), -1.0, 1.0, OP.mult, OP.add))
            V(lambda e: e.tensor_tensor(col(12), col(4), col(5), OP.subtract))
            V(lambda e: e.scalar_tensor_tensor(col(5), col(12), col(9), col(5),
                                               OP.mult, OP.add))
            V(lambda e: e.tensor_tensor(col(12), col(11), col(7), OP.subtract))
            V(lambda e: e.scalar_tensor_tensor(col(7), col(12), col(9), col(7),
                                               OP.mult, OP.add))
            V(lambda e: e.tensor_tensor(col(12), col(4), col(6), OP.subtract))
            V(lambda e: e.scalar_tensor_tensor(col(6), col(12), col(10), col(6),
                                               OP.mult, OP.add))
            V(lambda e: e.tensor_tensor(col(12), col(11), col(8), OP.subtract))
            V(lambda e: e.scalar_tensor_tensor(col(8), col(12), col(10), col(8),
                                               OP.mult, OP.add))
            if it < NSEC - 1:
                V(lambda e: e.tensor_tensor(col(12), col(7), col(8), OP.subtract))
                V(lambda e: e.reciprocal(col(12), col(12)))
                V(lambda e: e.tensor_scalar(col(13), col(7), float(-(k - 3)), None,
                                            OP.add))
                V(lambda e: e.tensor_tensor(col(13), col(13), col(12), OP.mult))
                V(lambda e: e.tensor_scalar(col(13), col(13), 0.02, 0.98, OP.max,
                                            OP.min))
                V(lambda e: e.tensor_tensor(col(12), col(6), col(5), OP.subtract))
                V(lambda e: e.scalar_tensor_tensor(col(4), col(12), col(13), col(5),
                                                   OP.mult, OP.add))
        V(lambda e: e.memset(col(14), 0.0))
        for pc in range(np_):
            V(lambda e, pc=pc: e.tensor_scalar(
                scr_bf[:, 0:sw], zblk[:, pc * sw:(pc + 1) * sw], col(6), None,
                OP.is_lt, OP.add, accum_out=col(12)))
            V(lambda e: e.tensor_tensor(col(14), col(14), col(12), OP.add))
            V(lambda e, pc=pc: e.scalar_tensor_tensor(
                u_f32[:, pc * sw:(pc + 1) * sw], scr_bf[:, 0:sw], BIGM,
                zblk[:, pc * sw:(pc + 1) * sw], OP.mult, OP.add))
        V(lambda e: e.max(out=tmpk[:, 16:24], in_=u_f32[:, 0:W]))
        V(lambda e: e.match_replace(out=u_f32[:, 0:W], in_to_replace=tmpk[:, 16:24],
                                    in_values=u_f32[:, 0:W], imm_value=-1.0e30))
        V(lambda e: e.max(out=tmpk[:, 24:32], in_=u_f32[:, 0:W]))
        V(lambda e: e.tensor_scalar(col(15), col(14), float(k - W - 1), None, OP.add))
        V(lambda e: e.tensor_scalar(col(12), col(15), 0.0, 15.0, OP.max, OP.min))
        V(lambda e: e.tensor_scalar(sel16[:], iota16[:], col(12), None, OP.is_equal))
        V(lambda e: e.tensor_scalar(cand16[:], tmpk[:, 16:32], -BIGM, None, OP.add))
        V(lambda e: e.scalar_tensor_tensor(cand16[:], cand16[:], 1.0, sel16[:],
                                           OP.mult, OP.mult, accum_out=col(13)))
        V(lambda e: e.tensor_scalar(col(9), col(15), 0.0, None, OP.is_lt))
        V(lambda e: e.tensor_tensor(col(12), col(6), col(13), OP.subtract))
        return V(lambda e, tcol=tcol: e.scalar_tensor_tensor(
            tcol[:, 0:1], col(12), col(9), col(13), OP.mult, OP.add))

    indT = AR1b[:, 0:16384]
    Ebuf = AR1b[:, 16384:32768]
    simblk = AR5[:, 0:2048]
    simsp = scr2[:, 0:2048]
    pairA = AR6[:, 0:2048]
    pairB = AR6[:, 2048:4096]

    paccum = xsq[:].bitcast(F32)[:, 0:256]
    pa_state = {"s": [0, 0], "idx": 0}
    passA_start_sv = E.c("sv")
    passA_start_ss = E.c("ss")
    pa_chunks = []
    for srcq in (da["wsynT"], da["woutfT"]):
        for pch in range(srcq.shape[0] // 128):
            for cc2 in range(srcq.shape[1] // 1024):
                pa_chunks.append(
                    srcq[pch * 128:(pch + 1) * 128,
                         cc2 * 1024:(cc2 + 1) * 1024])

    def emit_passA_chunk():
        qa_idx = pa_state["idx"]
        if qa_idx >= len(pa_chunks):
            return
        slot = qa_idx % 2
        wl = [("ss", pa_state["s"][slot])]
        if qa_idx == 0:
            wl += [("sv", passA_start_sv), ("ss", passA_start_ss)]
        w = dma(xraw[:, slot * 1024:(slot + 1) * 1024], pa_chunks[qa_idx],
                waits=wl, eng="gpsimd", sem="dg", pipe=True)
        pa_state["s"][slot] = SC(
            lambda e, slot=slot, qa_idx=qa_idx: e.activation(
                xraw[:, slot * 1024:(slot + 1) * 1024],
                xraw[:, slot * 1024:(slot + 1) * 1024],
                AF.Abs, accum_out=paccum[:, qa_idx:qa_idx + 1]),
            [("dg", w)], dep=False)
        pa_state["idx"] = qa_idx + 1

    V(lambda e: e.memset(p_sc[:, 0:2048], 0.0), [("st", E.c("st"))])

    mmA_hist = 0
    mmA_par = [0, 0]
    indT_ev = 0
    uev_reg = [("ss", 0), ("ss", 0)]
    se_pp = [0, 0]
    rearr_dx = 0
    rearr_ds = 0
    se_hist = 0
    rearr_hist = 0
    ind_v_hist = 0
    indT_done = 0
    for ic in range(NIC):
        for sb2 in range(2):
            sbi = ic * 2 + sb2
            qoff = ic * 256 + sb2 * 128
            grp_dma0 = E.c("dx")
            for g in range(32):
                emit_passA_chunk()
                conB_pp = [0, 0]
                for pp_ in range(2):
                    qA = qoff + 4 * g + 2 * pp_
                    qB = qA + 1
                    pA = pairA if pp_ == 0 else brow[:, 0:2048]
                    pB = pairB if pp_ == 0 else brow[:, 2048:4096]
                    cwA = [("st", mmA_par[pp_])]
                    if sbi == 0 and g == 0 and pp_ == 0:
                        cwA.append(("dx", cs_pp))
                    V(lambda e, qA=qA, pA=pA: e.tensor_scalar(
                        pA[:, 0:2048], h1n[:], h1b[:, qA:qA + 1], 0.0,
                        OP.add, OP.max), cwA, dep=False)
                    conB_pp[pp_] = V(lambda e, qB=qB, pB=pB: e.tensor_scalar(
                        pB[:, 0:2048], h1n[:], h1b[:, qB:qB + 1], 0.0,
                        OP.add, OP.max), dep=False)
                regs_pair = (p_mm, p_sc)
                utl = (utile[:, 0:2048], xbt[:, 0:2048])
                for pp_ in range(2):
                    ri = (g + pp_) % 2
                    reg = regs_pair[ri]
                    pA = pairA if pp_ == 0 else brow[:, 0:2048]
                    pB = pairB if pp_ == 0 else brow[:, 2048:4096]
                    mmAw_extra = [("sv", indT_ev)] if (g == 0 and pp_ == 0) else []
                    for nch2 in range(4):
                        wd = ([("sv", conB_pp[pp_]), uev_reg[ri]] +
                              ([("sv", se_pp[0])] if pp_ == 1 else []) +
                              mmAw_extra) if nch2 == 0 else []
                        T(lambda e, nch2=nch2, pA=pA, reg=reg: e.matmul(
                            reg[0:64, nch2 * 512:(nch2 + 1) * 512], w2t[:],
                            pA[:, nch2 * 512:(nch2 + 1) * 512],
                            start=True, stop=True, tile_position=(0, 0)), wd)
                        T(lambda e, nch2=nch2, pB=pB, reg=reg: e.matmul(
                            reg[64:128, nch2 * 512:(nch2 + 1) * 512], w2t[:],
                            pB[:, nch2 * 512:(nch2 + 1) * 512],
                            start=True, stop=True, tile_position=(0, 64)))
                    mmA_hist = E.c("st")
                    mmA_par[pp_] = mmA_hist
                creg = regs_pair[g % 2]
                for pp_ in range(2):
                    ri = (g + pp_) % 2
                    reg = regs_pair[ri]
                    uv = SC(lambda e, reg=reg, pp_=pp_: e.activation(
                        utl[pp_], reg[:, 0:2048], AF.Relu,
                        bias=b2cs[:, 0:1]), [("st", mmA_par[pp_])], dep=False)
                    uev_reg[ri] = ("ss", uv)
                    for nch2 in range(4):
                        wd = [uev_reg[ri]] if nch2 == 0 else []
                        T(lambda e, nch2=nch2, pp_=pp_, creg=creg: e.matmul(
                            creg[64 * pp_:64 * pp_ + 2,
                                 nch2 * 512:(nch2 + 1) * 512],
                            w3b[:, 0:2], utl[pp_][:, nch2 * 512:(nch2 + 1) * 512],
                            start=True, stop=True, tile_position=(0, 64 * pp_)), wd)
                mmC_hist = E.c("st")
                sew = [("st", mmC_hist), ("dx", rearr_dx)]
                se_hist = V(lambda e, creg=creg: e.tensor_scalar(
                    scr2[0:66, 0:2048], creg[0:66, 0:2048],
                    gcon[0:66, 0:1], None, OP.add), sew, dep=False)
                se_pp[0] = se_hist
                rw = [("sv", se_hist)]
                if g == 0:
                    rw.append(("sv", ind_v_hist))
                for j, row in enumerate((0, 1, 64, 65)):
                    dma(simblk[4 * g + j:4 * g + j + 1, :],
                        scr2[row:row + 1, 0:2048],
                        waits=rw if j == 0 else (), eng="sync", sem="dx",
                        pipe=True)
                rearr_dx = E.c("dx")
            dg_v = G(lambda e, qoff=qoff: e.affine_select(
                out=simblk[:, 0:2048], in_=simblk[:, 0:2048],
                compare_op=OP.not_equal, fill=0.0, base=qoff,
                pattern=[[-1, 2048]], channel_multiplier=1),
                [("dx", rearr_dx)])
            topk(simblk, 2048, KSIM, Z0_SIM, tcol1, pairA, scr2,
                 [("sg", dg_v), ("st", mmA_hist), ("st", indT_done)])
            ind_v_hist = V(lambda e: e.tensor_scalar(
                pairB[:, 0:2048], simblk[:, 0:2048], tcol1[:, 0:1], None, OP.is_ge))
            for jb in range(4):
                tw = [("sv", ind_v_hist)] if jb == 0 else [("sv", indT_ev)]
                for j2 in range(4):
                    jc = 4 * jb + j2
                    T(lambda e, jc=jc, j2=j2: e.transpose(
                        p_bf[:, 2048 + j2 * 128:2048 + (j2 + 1) * 128],
                        pairB[:, jc * 128:(jc + 1) * 128], idbf[:]),
                      tw if j2 == 0 else ())
                trp = E.c("st")
                indT_ev = V(lambda e, jb=jb, qoff=qoff: e.tensor_copy(
                    indT.rearrange("p (jc i) -> p jc i", i=1024)
                        [:, 4 * jb:4 * jb + 4, qoff:qoff + 128],
                    p_bf[:, 2048:2560].rearrange("p (a b) -> p a b", a=4)),
                  [("st", trp)])
            indT_done = E.c("st")
    while pa_state["idx"] < len(pa_chunks):
        emit_passA_chunk()
    passA_last_s = E.c("ss")
    topk_sim_done_v = E.c("sv")
    topk_sim_done_s = E.c("ss")

    exp_hist = [0, 0, 0, 0]
    ao_h = [0, 0, 0, 0]
    ao_last = 0
    ao_dg = 0
    score_hist = [[0, 0, 0, 0] for _ in range(10)]
    jitd = [0] * 10
    aoev_hist = 0
    aodma_hist = 0
    for quad in range(8):
        buf = quad % 2
        jw = [("sv", topk_sim_done_v)] if quad else [
            ("sv", topk_sim_done_v), ("ss", passA_last_s)]
        if quad >= 2:
            jw.append(("st", max(score_hist[quad - 2])))
        for hh in range(4):
            h = 4 * quad + hh
            dma(kjit[32 * hh:32 * hh + 2, buf * 2048:(buf + 1) * 2048],
                qks[64 + 2 * h:64 + 2 * h + 2, 0:2048], waits=jw if hh == 0 else (),
                eng="gpsimd", sem="dg", pipe=True)
            dma(qjit[32 * hh:32 * hh + 2, buf * 1024:(buf + 1) * 1024],
                qks[2 * h:2 * h + 2, 0:1024], eng="gpsimd", sem="dg", pipe=True)
        jitd[quad] = E.c("dg")
        for ic in range(NIC):
            for hh in range(4):
                for jcp in range(8):
                    for j2 in range(2):
                        jc = 2 * jcp + j2
                        wd = []
                        if j2 == 0:
                            wd = [("ss", exp_hist[hh])]
                            if jcp == 0:
                                wd.append(("dg", jitd[quad]))
                        T(lambda e, hh=hh, jc=jc, j2=j2, buf=buf, ic=ic: e.matmul(
                            p_sc[:, hh * 512 + j2 * 256: hh * 512 + (j2 + 1) * 256],
                            kjit[32 * hh:32 * hh + 2,
                                 buf * 2048 + jc * 128: buf * 2048 + (jc + 1) * 128],
                            qjit[32 * hh:32 * hh + 2,
                                 buf * 1024 + ic * 256: buf * 1024 + (ic + 1) * 256],
                            start=True, stop=True, tile_position=(32 * hh, 0)), wd)
                    scd = E.c("st")
                    hcs = 4 * quad + hh
                    ew = [("st", scd)]
                    if jcp == 0:
                        ew.append(("st", ao_h[hh]))
                    exp_hist[hh] = SC(
                        lambda e, hh=hh, jcp=jcp, hcs=hcs: e.activation(
                            Ebuf[:, hh * 4096 + jcp * 512: hh * 4096 + (jcp + 1) * 512],
                            p_sc[:, hh * 512: hh * 512 + 512], AF.Exp,
                            bias=gcon[:, 16 + hcs:17 + hcs]),
                        ew, dep=False)
                score_hist[quad][hh] = E.c("st")
            mk0 = E.c("sv")
            for hh in range(4):
                V(lambda e, hh=hh, ic=ic: e.tensor_tensor(
                    Ebuf[:, hh * 4096:(hh + 1) * 4096],
                    Ebuf[:, hh * 4096:(hh + 1) * 4096],
                    indT.rearrange("p (jc i) -> p jc i", i=1024)
                        [:, :, ic * 256:(ic + 1) * 256],
                    OP.mult), [("ss", exp_hist[hh])], dep=False)
            mk = E.c("sv")
            for hh in range(4):
                for jc in range(DC):
                    wd = []
                    if jc == 0:
                        wd = [("sv", mk), ("sv", aoev_hist)] if hh == 0 else [("sv", mk)]
                    habs = 4 * quad + hh
                    T(lambda e, hh=hh, jc=jc, habs=habs: e.matmul(
                        p_mm[32 * hh:32 * hh + 3, 0:256],
                        vaug[:, jc * 128 + 4 * habs: jc * 128 + 4 * habs + 3],
                        Ebuf[:, hh * 4096 + jc * 256: hh * 4096 + (jc + 1) * 256],
                        start=(jc == 0), stop=(jc == DC - 1),
                        tile_position=(0, 32 * hh)), wd)
                ao_h[hh] = E.c("st")
            aod = E.c("st")
            ao_last = aod
            for hh in range(4):
                V(lambda e, hh=hh, ic=ic: e.tensor_copy(
                    scr2[32 * hh:32 * hh + 3, ic * 256:(ic + 1) * 256],
                    p_mm[32 * hh:32 * hh + 3, 0:256]),
                  [("st", aod), ("dg", ao_dg)] if (hh == 0 and ic == 0)
                  else ([("st", aod)] if hh == 0 else ()), dep=False)
            aoev_hist = E.c("sv")
        for hh in range(4):
            h = 4 * quad + hh
            dma(aot[h:h + 1, 0:1024], scr2[32 * hh:32 * hh + 1, 0:1024],
                waits=[("sv", aoev_hist)] if hh == 0 else (),
                eng="gpsimd", sem="dg", pipe=True)
            dma(aot[32 + h:32 + h + 1, 0:1024],
                scr2[32 * hh + 1:32 * hh + 2, 0:1024],
                eng="gpsimd", sem="dg", pipe=True)
            dma(aden[h:h + 1, 0:1024], scr2[32 * hh + 2:32 * hh + 3, 0:1024],
                eng="gpsimd", sem="dg", pipe=True)
        ao_dg = E.c("dg")
        aodma_hist = ao_dg
    rc = V(lambda e: e.reciprocal(aden[0:32, :], aden[0:32, :]),
           [("dg", aodma_hist)])
    w1_ = dma(denr[0:32, 0:1024], aden[0:32, :],
              waits=[("sv", rc), ("ss", passA_last_s)])
    w2_ = dma(denr[32:64, 0:1024], aden[0:32, :], waits=[("dx", w1_)])
    mulv = V(lambda e: e.tensor_tensor(aot[:, :], aot[:, :], denr[0:64, 0:1024],
                                       OP.mult), [("dx", w2_)])
    aotb_v = V(lambda e: e.tensor_copy(aotb[:], aot[:]))
    w = dma(scr2[0:64, 0:2048], da["woutaT"], waits=[("sv", aotb_v)])
    woat_v = V(lambda e: e.tensor_copy(woat, scr2[0:64, 0:2048]), [("dx", w)])
    projw = woat_v
    proj_ss = E.c("ss")
    x2_hist = [0, 0]
    for dch in range(DC):
        for n2 in range(2):
            bank = (2 * dch + n2) % 2
            wd = [("sv", projw), ("ss", proj_ss)] if dch == 0 and n2 == 0 else []
            if dch > 0 or n2 > 0:
                wd.append(("sv", x2_hist[bank]))
            T(lambda e, dch=dch, n2=n2, bank=bank: e.matmul(
                p_sc[:, bank * 512:(bank + 1) * 512],
                woat[:, dch * 128:(dch + 1) * 128],
                aotb[:, n2 * 512:(n2 + 1) * 512],
                start=True, stop=True), wd + [("sv", woat_v)] if (dch == 0 and n2 == 0) else wd)
            pj = E.c("st")
            x2_hist[bank] = V(lambda e, dch=dch, n2=n2, bank=bank:
                              e.scalar_tensor_tensor(
                AR1[:, dch * 1024 + n2 * 512: dch * 1024 + (n2 + 1) * 512],
                p_sc[:, bank * 512:(bank + 1) * 512],
                gaT[:, dch:dch + 1],
                AR2[:, dch * 1024 + n2 * 512: dch * 1024 + (n2 + 1) * 512],
                OP.mult, OP.add), [("st", pj)])
    x2_v = E.c("sv")
    for dch in range(DC):
        dma(d_x2[dch * 128:(dch + 1) * 128, :], AR1[:, dch * 1024:(dch + 1) * 1024],
            waits=[("sv", x2_v)] if dch == 0 else ())
    x2_spill = E.c("dx")

    for qi in range(2):
        rv_ = V(lambda e, qi=qi: e.reduce_sum(
            tmpk[:, 20 + qi:21 + qi], paccum[:, qi * 16:(qi + 1) * 16],
            axis=AXX), [("ss", passA_last_s)] if qi == 0 else ())
        tm = T(lambda e, qi=qi: e.matmul(
            p_sc[0:1, 1536 + qi:1537 + qi], onef[:],
            tmpk[:, 20 + qi:21 + qi], start=True, stop=True), [("sv", rv_)])
        pcp = V(lambda e, qi=qi: e.tensor_copy(sclr[0:1, 8 + qi:9 + qi],
                                               p_sc[0:1, 1536 + qi:1537 + qi]),
                [("st", tm)])
    w = dma(d_spart[:, 0:2], sclr[0:1, 8:10], waits=[("sv", E.c("sv"))])
    E.op("gpsimd", lambda e: e.collective_compute(
        "AllReduce", OP.add, replica_groups=[list(range(8))],
        ins=[d_spart.opt()], outs=[d_ssum.opt()]), inc=("scc", 1),
        waits=[("dx", w)])
    ssum_cc = E.c("scc")
    w = dma(sclr[0:1, 12:14], d_ssum[:, 0:2], waits=[("scc", ssum_cc)])
    sload = w

    def quant_scale(qi, s_col):
        V(lambda e, qi=qi: e.tensor_scalar(sclr[0:1, 4 * qi:4 * qi + 1],
                                           sclr[0:1, 12 + qi:13 + qi],
                                           1.0 / (2048.0 * 8192.0), None,
                                           OP.mult),
          [("dx", sload)] if qi == 0 else ())
        V(lambda e, qi=qi: e.tensor_scalar(sclr[0:1, 4 * qi + 1:4 * qi + 2],
                                           sclr[0:1, 4 * qi:4 * qi + 1],
                                           1e-8, None, OP.add))
        sc_v = V(lambda e, qi=qi: e.reciprocal(sclr[0:1, 4 * qi + 1:4 * qi + 2],
                                               sclr[0:1, 4 * qi + 1:4 * qi + 2]))
        w = dma(d_scal[:, 8 + 2 * qi:10 + 2 * qi], sclr[0:1, 4 * qi:4 * qi + 2],
                waits=[("sv", sc_v)])
        scb = dma(gcon[:, 48 + s_col:50 + s_col],
                  d_scal[:, 8 + 2 * qi:10 + 2 * qi].to_broadcast([128, 2]),
                  waits=[("dx", w)])
        return scb

    V(lambda e: e.memset(gcon[:, 50:51], MAGIC))
    V(lambda e: e.memset(gcon[:, 53:54], MAGIC))
    V(lambda e: e.memset(gcon[:, 54:55], -MAGIC))
    scb1 = quant_scale(0, 0)
    scb2 = quant_scale(1, 3)

    pb_a1 = [0, 0, 0, 0]
    pb_clamp = [0, 0, 0, 0]
    pb_sub = [0, 0, 0, 0]
    pb_odma = {"dg": 0, "ds": 0}
    scbs = {0: scb1, 1: scb2}
    browf = brow[:].bitcast(F32)
    ld_slots = (xraw[:, 0:1024], xraw[:, 1024:2048],
                browf[:, 0:1024], browf[:, 1024:2048])
    hidstf = hidst[:].bitcast(F32)
    sc_slots = (scr2[:, 0:1024], scr2[:, 1024:2048],
                hidstf[:, 0:1024], ytile[:, 0:1024])
    bi = 0
    for (srcq, dstq, qi, s_col) in ((da["wsynT"], d_tqsyn_sl, 0, 0),
                                    (da["woutfT"], d_tqout_sl, 1, 3)):
        for pch in range(srcq.shape[0] // 128):
            for cc2 in range(srcq.shape[1] // 1024):
                slot = bi % 4
                oslot = bi % 4
                leng, lsem = ("sync", "dx") if slot % 2 == 0 else ("scalar", "ds")
                oeng, osem = ("gpsimd", "dg")
                wl = [("ss", pb_a1[slot])]
                if bi < 4:
                    wl.append(("ss", passA_last_s))
                w = dma(ld_slots[slot],
                        srcq[pch * 128:(pch + 1) * 128,
                             cc2 * 1024:(cc2 + 1) * 1024],
                        waits=wl, eng=leng, sem=lsem, pipe=True)
                a1w = [(lsem, w), ("dx", scbs[qi]), ("sv", pb_clamp[slot])]
                pb_a1[slot] = SC(lambda e, slot=slot, s_col=s_col: e.activation(
                    sc_slots[slot], ld_slots[slot],
                    AF.Identity, bias=gcon[:, 50 + s_col:51 + s_col],
                    scale=gcon[:, 49 + s_col:50 + s_col]), a1w, dep=False)
                pb_clamp[slot] = V(lambda e, slot=slot: e.tensor_scalar(
                    sc_slots[slot], sc_slots[slot],
                    MAGIC - 1.0, MAGIC + 1.0, OP.max, OP.min),
                    [("ss", pb_a1[slot])], dep=False)
                pb_sub[oslot] = SC(lambda e, slot=slot, oslot=oslot: e.activation(
                    AR6[:, oslot * 1024:(oslot + 1) * 1024],
                    sc_slots[slot],
                    AF.Identity, bias=gcon[:, 54:55]),
                    [("sv", pb_clamp[slot]), (osem, pb_odma[osem])], dep=False)
                pb_odma[osem] = dma(
                    dstq[pch * 128:(pch + 1) * 128,
                         cc2 * 1024:(cc2 + 1) * 1024],
                    AR6[:, oslot * 1024:(oslot + 1) * 1024],
                    waits=[("ss", pb_sub[oslot])], eng=oeng, sem=osem,
                    pipe=True)
                bi += 1
    E.op("gpsimd", lambda e: e.collective_compute(
        "AllGather", OP.bypass, replica_groups=[list(range(8))],
        ins=[d_tqsyn_sl.opt()], outs=[d_tqsyn.opt()]), inc=("scc", 1),
        waits=[("dg", E.c("dg"))])
    tqsyn_cc = E.c("scc")
    E.op("gpsimd", lambda e: e.collective_compute(
        "AllGather", OP.bypass, replica_groups=[list(range(8))],
        ins=[d_tqout_sl.opt()], outs=[d_tqout.opt()]), inc=("scc", 1))
    tqout_cc = E.c("scc")
    qb_done = [("scc", tqsyn_cc)]
    bsyn_rows = [h1n[0:1, 0:2048], qks[0:1, 0:2048], vaug[0:1, 0:2048],
                 scr2b[0:1, 0:2048]]
    s1r_v = 0
    for a in range(4):
        w = dma(xraw[0:1, 0:2048], da["bsyn_row"][:, a * 2048:(a + 1) * 2048],
                waits=[("sv", E.c("sv")), ("dg", E.c("dg")), ("ss", E.c("ss"))])
        s1r_v = V(lambda e, a=a: e.tensor_copy(bsyn_rows[a], xraw[0:1, 0:2048]),
                  [("dx", w)])
    s1r_v = V(lambda e: e.tensor_copy(s1row[0:1, 0:128],
                                      gcon[0:1, 49:50].to_broadcast([1, 128])))

    mmh = [0] * DC
    for dch in range(DC):
        x2c = AR1[:, dch * 1024:(dch + 1) * 1024]
        cw = [("dx", x2_spill)] if dch == 0 else [("st", mmh[dch - 1])]
        cb_v = V(lambda e, x2c=x2c: e.tensor_copy(xbt[:, 0:1024], x2c), cw,
                 dep=False)
        sq_s = SC(lambda e: e.activation(xsq[:, 0:1024], xbt[:, 0:1024],
                                         AF.Square),
                  [("sv", cb_v)] + ([("st", mmh[dch - 1])] if dch else []))
        for n2 in range(2):
            T(lambda e, dch=dch, n2=n2: e.matmul(
                p_sc[0:1, 1024 + n2 * 512:1024 + (n2 + 1) * 512],
                onebc[:, 0:1],
                xbt[:, n2 * 512:(n2 + 1) * 512],
                start=(dch == 0), stop=(dch == DC - 1)),
              [("sv", cb_v)] if n2 == 0 else ())
            T(lambda e, dch=dch, n2=n2: e.matmul(
                p_mm[0:1, 1024 + n2 * 512:1024 + (n2 + 1) * 512],
                onebc[:, 0:1],
                xsq[:, n2 * 512:(n2 + 1) * 512],
                start=(dch == 0), stop=(dch == DC - 1)),
              [("ss", sq_s)] if n2 == 0 else ())
        mmh[dch] = E.c("st")
    stA = scr2[0:1, 1024:2048]
    stB = AR5[0:1, 0:1024]
    stC = AR5[0:1, 1024:2048]
    stMb = hidst[0:1, 0:1024]
    stRb = hidst[0:1, 1024:2048]
    ev2 = V(lambda e: e.tensor_scalar(stA, p_sc[0:1, 1024:2048], 1.0 / D, None,
                                      OP.mult), [("st", E.c("st"))])
    V(lambda e: e.tensor_scalar(stB, p_mm[0:1, 1024:2048], 1.0 / D, None,
                                OP.mult))
    V(lambda e: e.tensor_tensor(stC, stA, stA, OP.mult))
    V(lambda e: e.tensor_tensor(stB, stB, stC, OP.subtract))
    V(lambda e: e.tensor_scalar(stB, stB, LN_EPS, None, OP.add))
    rv = V(lambda e: e.reciprocal(stB, stB))
    rs_s = SC(lambda e: e.activation(stB, stB, AF.Sqrt), [("sv", rv)])
    cbb = V(lambda e: e.tensor_copy(stMb, stA), [("ss", rs_s)])
    cbb = V(lambda e: e.tensor_copy(stRb, stB))
    w = dma(d_rowsb[0:1, 0:1024], stMb, waits=[("sv", cbb)])
    w = dma(d_rowsb[1:2, 0:1024], stRb, waits=[("sv", cbb)])
    brfb = brf[:].bitcast(BF16)
    bw1 = dma(brfb[:, 0:1024], d_rowsb[0:1, 0:1024].to_broadcast([128, 1024]),
              waits=[("dx", w)])
    bw2 = dma(brfb[:, 1024:2048], d_rowsb[1:2, 0:1024].to_broadcast([128, 1024]),
              waits=[("dx", w)])
    mu2b = brfb[:, 0:1024]; rs2b = brfb[:, 1024:2048]
    xn2_hist = [0] * DC
    for dch in range(DC):
        x2c = AR1[:, dch * 1024:(dch + 1) * 1024]
        wd = [("dx", bw2)] if dch == 0 else [("ss", xn2_hist[dch - 1])]
        V(lambda e, x2c=x2c: e.tensor_tensor(xraw[:, 0:1024], x2c, mu2b,
                                             OP.subtract), wd)
        s2v = V(lambda e: e.tensor_tensor(xraw[:, 0:1024], xraw[:, 0:1024], rs2b,
                                          OP.mult))
        xn2_hist[dch] = SC(lambda e, dch=dch: e.activation(
            AR2b[:, dch * 1024:(dch + 1) * 1024], xraw[:, 0:1024], AF.Identity,
            bias=sb_t[:, dch:dch + 1], scale=A2[:, dch:dch + 1]), [("sv", s2v)])
    xn2_s = E.c("ss")
    xn2T = lambda dch: AR2b[:, dch * 1024:(dch + 1) * 1024]

    uP6 = AR2[:, 8192:16384]
    hidtm = AR2b[:, 16384:24576]
    xn2_all_v = E.c("sv")
    zmm_hist = [0, 0]
    zev_s = 0
    hidT_d = 0
    for tb in range(NTOKB):
        for nch in range(DC):
            for hs in range(2):
                buf = (2 * nch + hs) % 2
                wd = [("st", zmm_hist[buf])]
                if nch == 0 and hs == 0:
                    wd.append(("sv", E.c("sv")))
                if tb == 0 and nch == 0:
                    wd += [("ss", xn2_s), ("sv", s1r_v)] + qb_done
                sbuf_ = AR6[:, 0:4096] if buf == 0 else brow[:, 0:4096]
                ssem = "dx" if buf == 0 else "ds"
                w = dma(sbuf_.rearrange("p (c n) -> p c n", c=8),
                        d_tqsyn[hs * 1024:(hs + 1) * 1024,
                                nch * 512:(nch + 1) * 512]
                        .rearrange("(c p) n -> p c n", p=128), waits=wd,
                        sem=ssem, eng="sync" if buf == 0 else "scalar",
                        pipe=True)
                for mg in range(2):
                    for dc8 in range(8):
                        dch = 8 * hs + dc8
                        mw = []
                        if dc8 == 0:
                            mw = [(ssem, w)]
                            if hs == 0 and mg == 0:
                                mw.append(("ss", zev_s))
                        sb_ = AR6 if buf == 0 else brow
                        T(lambda e, sb_=sb_, mg=mg, dch=dch, dc8=dc8, tb=tb:
                          e.matmul(
                            p_sc[:, mg * 512:(mg + 1) * 512],
                            xn2T(dch)[:, tb * 256 + mg * 128:
                                       tb * 256 + (mg + 1) * 128],
                            sb_[:, dc8 * 512:(dc8 + 1) * 512],
                            start=(dch == 0), stop=False), mw)
                zmm_hist[buf] = E.c("st")
            for mg in range(2):
                T(lambda e, mg=mg, nch=nch: e.matmul(
                    p_sc[:, mg * 512:(mg + 1) * 512],
                    s1row[:],
                    bsyn_rows[nch // 4][:, (nch % 4) * 512:(nch % 4 + 1) * 512],
                    start=False, stop=True))
            zmm_hist[(2 * nch + 1) % 2] = E.c("st")
            zw = [("st", E.c("st"))]
            zev_s = SC(lambda e, nch=nch: e.activation(
                AR1[:, 0 * 8192 + nch * 512: 0 * 8192 + (nch + 1) * 512],
                p_sc[:, 0:512], AF.Identity, scale=gcon[:, 48:49]), zw,
                dep=False)
            zev_s = SC(lambda e, nch=nch: e.activation(
                AR1[:, 1 * 8192 + nch * 512: 1 * 8192 + (nch + 1) * 512],
                p_sc[:, 512:1024], AF.Identity, scale=gcon[:, 48:49]),
                dep=False)
        z_done_s = E.c("ss")
        for r2 in range(2):
            zt = AR1[:, r2 * 8192:(r2 + 1) * 8192]
            tk = topk(zt, NSYN, KFFN, Z0_FFN, tcol2, AR5b[:, 0:4096], uP6,
                      [("ss", z_done_s), ("st", E.c("st"))], sw=4096)
            V(lambda e: e.tensor_scalar(tcol2[:], tcol2[:], 1e-30, None, OP.max))
            hid_v = V(lambda e, zt=zt: e.scalar_tensor_tensor(
                hidtm[:, 0:8192], zt, tcol2[:, 0:1], zt, OP.is_ge, OP.mult))
            for sb4 in range(16):
                for j4 in range(4):
                    sc_ = 4 * sb4 + j4
                    T(lambda e, sc_=sc_, j4=j4: e.transpose(
                        p_bf[:, 2048 + j4 * 128:2048 + (j4 + 1) * 128],
                        hidtm[:, sc_ * 128:(sc_ + 1) * 128], idbf[:]),
                      [("sv", hid_v), ("dg", hidT_d)] if j4 == 0 else ())
                trp = E.c("st")
                hcp = V(lambda e, sb4=sb4: e.tensor_copy(
                    hidst[:, (sb4 % 2) * 1024:(sb4 % 2) * 1024 + 512],
                    p_bf[:, 2048:2560]), [("st", trp)], dep=False)
                hidT_d = dma(
                    d_hidT[:, tb * 256 + r2 * 128: tb * 256 + (r2 + 1) * 128]
                    .rearrange("(a b) t -> b a t", b=128)[:, 4 * sb4:4 * sb4 + 4, :],
                    hidst[:, (sb4 % 2) * 1024:(sb4 % 2) * 1024 + 512]
                    .rearrange("p (a t) -> p a t", a=4),
                    waits=[("sv", hcp)], eng="gpsimd", sem="dg", pipe=True)
        ffn_fwd_done = E.c("dg")

    ybuf = AR2b[:, 16384:32768]
    yv_hist = 0
    pd_hist = [0] * DC
    spill_dg = [0, 0]
    for half in range(2):
        hw = [("dg", ffn_fwd_done)] if half == 0 else [("st", E.c("st"))]
        hld = dma(AR1b[:, 0:32768].rearrange("p (c t) -> p c t", c=32),
                  d_hidT[half * 4096:(half + 1) * 4096, :]
                  .rearrange("(c p) t -> p c t", p=128), waits=hw, pipe=True)
        tqo_hist2 = [0, 0]
        wtq_hist = [0, 0]
        for ddch in range(DC):
            tbuf = ddch % 2
            tqb = AR6 if tbuf == 0 else brow
            tsem = "dx" if tbuf == 0 else "ds"
            wq = [("dx", hld)] if ddch < 2 else []
            if half == 0 and ddch == 0:
                wq.append(("scc", tqout_cc))
            if tqo_hist2[tbuf]:
                wq.append(("st", tqo_hist2[tbuf]))
            wtq = dma(tqb[:, 0:4096].rearrange("p (c m) -> p c m", c=32),
                      d_tqout[half * 4096:(half + 1) * 4096,
                              ddch * 128:(ddch + 1) * 128]
                      .rearrange("(c p) m -> p c m", p=128), waits=wq,
                      sem=tsem, eng="sync" if tbuf == 0 else "scalar",
                      pipe=True)
            wtq_hist[tbuf] = wtq
            for n2 in range(2):
                for kch in range(32):
                    mw = []
                    if kch == 0:
                        mw = [(tsem, wtq), ("sv", yv_hist)]
                    T(lambda e, ddch=ddch, n2=n2, kch=kch, tqb=tqb: e.matmul(
                        p_mm[:, n2 * 512:(n2 + 1) * 512],
                        tqb[:, kch * 128:(kch + 1) * 128],
                        AR1b[:, kch * 1024 + n2 * 512: kch * 1024 + (n2 + 1) * 512],
                        start=(kch == 0), stop=(kch == 31)), mw)
                mmd = E.c("st")
                if half == 0:
                    pv = V(lambda e, n2=n2: e.tensor_copy(
                        scr2[:, n2 * 512:(n2 + 1) * 512],
                        p_mm[:, n2 * 512:(n2 + 1) * 512]),
                        [("st", mmd), ("dg", spill_dg[n2])], dep=False)
                    yv_hist = pv
                    pd_hist[ddch] = dma(
                        d_partial[ddch * 128:(ddch + 1) * 128,
                                  n2 * 512:(n2 + 1) * 512],
                        scr2[:, n2 * 512:(n2 + 1) * 512], waits=[("sv", pv)],
                        eng="gpsimd", sem="dg", pipe=True)
                    spill_dg[n2] = pd_hist[ddch]
                else:
                    wpl = dma(ytile[:, n2 * 512:(n2 + 1) * 512],
                              d_partial[ddch * 128:(ddch + 1) * 128,
                                        n2 * 512:(n2 + 1) * 512],
                              waits=[("dg", pd_hist[ddch]), ("sv", yv_hist)],
                              pipe=True)
                    a1 = V(lambda e, n2=n2: e.tensor_tensor(
                        scr2[:, n2 * 512:(n2 + 1) * 512],
                        p_mm[:, n2 * 512:(n2 + 1) * 512],
                        ytile[:, n2 * 512:(n2 + 1) * 512], OP.add),
                        [("st", mmd), ("dx", wpl), ("dg", spill_dg[n2])],
                        dep=False)
                    a2 = SC(lambda e, ddch=ddch, n2=n2: e.activation(
                        scr2[:, n2 * 512:(n2 + 1) * 512],
                        scr2[:, n2 * 512:(n2 + 1) * 512], AF.Identity,
                        scale=gcon[:, 51:52], bias=bfbpp[:, ddch:ddch + 1]),
                        [("sv", a1)])
                    yv_hist = V(lambda e, ddch=ddch, n2=n2: e.tensor_tensor(
                        ybuf[:, ddch * 1024 + n2 * 512: ddch * 1024 + (n2 + 1) * 512],
                        scr2[:, n2 * 512:(n2 + 1) * 512],
                        xn2T(ddch)[:, n2 * 512:(n2 + 1) * 512], OP.add),
                        [("ss", a2)], dep=False)
            tqo_hist2[tbuf] = E.c("st")
    y_done_v = E.c("sv")

    ymm = [0] * DC
    for dch in range(DC):
        for n2 in range(2):
            T(lambda e, dch=dch, n2=n2: e.matmul(
                p_sc[0:1, 1024 + n2 * 512:1024 + (n2 + 1) * 512],
                onebc[:, 0:1],
                ybuf[:, dch * 1024 + n2 * 512: dch * 1024 + (n2 + 1) * 512],
                start=(dch == 0), stop=(dch == DC - 1)),
              [("sv", y_done_v)] if (dch == 0 and n2 == 0) else ())
        sqv = SC(lambda e, dch=dch: e.activation(
            xsq[:, 0:1024], ybuf[:, dch * 1024:(dch + 1) * 1024], AF.Square),
            [("sv", y_done_v)] if dch == 0 else [("st", ymm[dch - 1])])
        for n2 in range(2):
            T(lambda e, dch=dch, n2=n2: e.matmul(
                p_mm[0:1, 1024 + n2 * 512:1024 + (n2 + 1) * 512],
                onebc[:, 0:1], xsq[:, n2 * 512:(n2 + 1) * 512],
                start=(dch == 0), stop=(dch == DC - 1)),
              [("ss", sqv)] if n2 == 0 else ())
        ymm[dch] = E.c("st")
    stA2 = scr2[0:1, 0:1024]
    stB2 = scr2[0:1, 1024:2048]
    stC2 = AR5[0:1, 0:1024]
    stMb2 = hidst[0:1, 0:1024]
    stRb2 = hidst[0:1, 1024:2048]
    ev3 = V(lambda e: e.tensor_scalar(stA2, p_sc[0:1, 1024:2048], 1.0 / D, None,
                                      OP.mult), [("st", E.c("st"))])
    V(lambda e: e.tensor_scalar(stB2, p_mm[0:1, 1024:2048], 1.0 / D, None,
                                OP.mult))
    V(lambda e: e.tensor_tensor(stC2, stA2, stA2, OP.mult))
    V(lambda e: e.tensor_tensor(stB2, stB2, stC2, OP.subtract))
    V(lambda e: e.tensor_scalar(stB2, stB2, LN_EPS, None, OP.add))
    rv = V(lambda e: e.reciprocal(stB2, stB2))
    rsy_s = SC(lambda e: e.activation(stB2, stB2, AF.Sqrt), [("sv", rv)])
    cbb = V(lambda e: e.tensor_copy(stMb2, stA2), [("ss", rsy_s)])
    cbb = V(lambda e: e.tensor_copy(stRb2, stB2))
    w = dma(d_rowsb[0:1, 0:1024], stMb2, waits=[("sv", cbb)])
    w = dma(d_rowsb[1:2, 0:1024], stRb2, waits=[("sv", cbb)])
    brfb = brf[:].bitcast(BF16)
    bw1 = dma(brfb[:, 0:1024], d_rowsb[0:1, 0:1024].to_broadcast([128, 1024]),
              waits=[("dx", w)])
    bw2 = dma(brfb[:, 1024:2048], d_rowsb[1:2, 0:1024].to_broadcast([128, 1024]),
              waits=[("dx", w)])
    muyb = brfb[:, 0:1024]; rsyb = brfb[:, 1024:2048]
    x3v = 0
    outd = 0
    for dch in range(DC):
        wd = [("dx", bw2), ("sv", x3v)]
        wx2 = dma(x2l[:, 0:1024], d_x2[dch * 128:(dch + 1) * 128, :], waits=wd)
        V(lambda e, dch=dch: e.tensor_tensor(scr2[:, 0:1024],
                                             ybuf[:, dch * 1024:(dch + 1) * 1024],
                                             muyb, OP.subtract),
          [("dx", bw2), ("st", E.c("st"))])
        s3 = V(lambda e: e.tensor_tensor(scr2[:, 0:1024], scr2[:, 0:1024], rsyb,
                                         OP.mult))
        fs = SC(lambda e, dch=dch: e.activation(
            scr2[:, 1024:2048], scr2[:, 0:1024], AF.Identity,
            bias=bfpp[:, dch:dch + 1], scale=gfpp[:, dch:dch + 1]), [("sv", s3)])
        x3v = V(lambda e, dch=dch: e.scalar_tensor_tensor(
            scr2[:, 0:1024], scr2[:, 1024:2048], gbT[:, dch:dch + 1],
            x2l[:, 0:1024],
            OP.mult, OP.add), [("ss", fs), ("dx", wx2)])
        for t4 in range(2):
            tev = [("sv", x3v)] if t4 == 0 else [("sv", E.c("sv"))]
            if dch > 0 and t4 == 0:
                tev.append(("dout", outd))
            for j4 in range(4):
                tc = 4 * t4 + j4
                T(lambda e, tc=tc, j4=j4: e.transpose(
                    p_sc[:, 1024 + j4 * 128:1024 + (j4 + 1) * 128],
                    scr2[:, tc * 128:(tc + 1) * 128], idf2[:]),
                  tev if j4 == 0 else ())
            trp = E.c("st")
            V(lambda e, t4=t4: e.tensor_copy(otile[:, t4 * 512:(t4 + 1) * 512],
                                             p_sc[:, 1024:1536]), [("st", trp)],
              dep=False)
        oev = E.c("sv")
        outd = dma(out_tm[:, dch * 128:(dch + 1) * 128]
                   .rearrange("(a p) m -> p a m", p=128),
                   otile[:, 0:1024].rearrange("p (a m) -> p a m", m=128),
                   waits=[("sv", oev)], sem="dout")
    E.wait("sync", "dout", E.c("dout"))

    @block.tensor
    def _(e):
        for f in E.q["tensor"]:
            f(e)

    @block.sync
    def _(e):
        for f in E.q["sync"]:
            f(e)

    @block.vector
    def _(e):
        for f in E.q["vector"]:
            f(e)

    @block.scalar
    def _(e):
        for f in E.q["scalar"]:
            f(e)

    @block.gpsimd
    def _(e):
        for f in E.q["gpsimd"]:
            f(e)

    stack.close()
    return nc


def _host_prep(inputs, core):
    b, half = core // 2, core % 2
    qi0 = half * TI
    f32 = lambda a: np.ascontiguousarray(a, dtype=np.float32)
    bf = lambda a: np.ascontiguousarray(a).astype(ml_dtypes.bfloat16)
    x = np.asarray(inputs["L0_attn_x"], np.float32)[b]
    if half == 0:
        xp = x
    else:
        xp = np.concatenate([x[TI:], x[:TI]], axis=0)
    Wqkv = np.asarray(inputs["L0_attn_Wqkv"], np.float32)
    bqkv = np.asarray(inputs["L0_attn_bqkv"], np.float32)
    W3 = np.asarray(inputs["L0_attn_W3"], np.float32)
    w3blk = np.zeros((128, 2), np.float32)
    w3blk[0:64, 0] = W3[0]
    w3blk[64:128, 1] = W3[0]
    b2 = np.asarray(inputs["L0_attn_b2"], np.float32)
    iota16 = np.broadcast_to(np.arange(16, dtype=np.float32)[None, :], (128, 16))
    vauginit = np.zeros((128, 128), np.float32)
    for h in range(32):
        vauginit[:, 4 * h + 2] = 1.0
    ident = np.eye(128, dtype=np.float32)
    sc12 = np.ones((128, 1), np.float32)
    sc12[0:64, 0] = ISQ2
    pp = lambda v: np.ascontiguousarray(
        np.asarray(v, np.float32).reshape(DC, 128).T)
    d = {
        "x_fm": f32(xp.T),
        "w1T": f32(np.asarray(inputs["L0_attn_W1"], np.float32).T),
        "b1c": f32(np.asarray(inputs["L0_attn_b1"], np.float32)[:, None]),
        "w2T": f32(np.asarray(inputs["L0_attn_W2"], np.float32).T),
        "b2st": f32(np.concatenate([b2, b2])[:, None]),
        "w3blk": f32(w3blk),
        "b3s": f32(np.asarray(inputs["L0_attn_b3"], np.float32).reshape(1, 1)),
        "wqkT": f32(Wqkv[0:128].T),
        "bqk_row": f32(bqkv[0:128][None, :]),
        "wvT": f32(Wqkv[128:192].T),
        "bv_row": f32(bqkv[128:192][None, :]),
        "woutaT": f32(np.asarray(inputs["L0_attn_Wout"], np.float32).T[
            np.array([2 * h + c for c in (0, 1) for h in range(32)])]),
        "bout_pp": pp(inputs["L0_attn_bout"]),
        "wsynT": f32(np.asarray(inputs["L0_ffn_W_syn"], np.float32)
                     .T[core * (D // 8):(core + 1) * (D // 8)]),
        "bsyn_row": f32(np.asarray(inputs["L0_ffn_b_syn"], np.float32)[None, :]),
        "woutfT": f32(np.asarray(inputs["L0_ffn_W_out"], np.float32)
                      .T[core * (NSYN // 8):(core + 1) * (NSYN // 8)]),
        "boutf_pp": pp(inputs["L0_ffn_b_out"]),
        "gammaA_pp": pp(inputs["L0_attn_gamma"]),
        "betaA_pp": pp(inputs["L0_attn_beta"]),
        "gammaF_pp": pp(inputs["L0_ffn_gamma"]),
        "betaF_pp": pp(inputs["L0_ffn_beta"]),
        "c_pp": np.ascontiguousarray(np.asarray(inputs["L0_c"], np.float32)
            [b, half * TI:(half + 1) * TI].reshape(8, 128).T),
        "wmod": f32(np.asarray(inputs["L0_wmod"], np.float32)
            [half * TI:(half + 1) * TI]),
        "vauginit": bf(vauginit),
        "idt_bf": bf(ident),
        "idt_f32": f32(ident),
        "iota16_in": f32(iota16),
        "ones_bf_row": bf(np.ones((1, 512), np.float32)),
        "ones_f32_col": f32(np.ones((128, 1), np.float32)),
        "sc12": sc12,
    }
    return d


_NC_CACHE = None


def kernel(**inputs):
    global _NC_CACHE
    if _NC_CACHE is None:
        _NC_CACHE = build_nc()
    nc = _NC_CACHE
    in_maps = [_host_prep(inputs, c) for c in range(8)]
    res = run_bass_kernel_spmd(nc, in_maps, core_ids=list(range(8)))
    out = np.zeros((4, S, D), np.float32)
    for c in range(8):
        b, half = c // 2, c % 2
        out[b, half * TI:(half + 1) * TI, :] = res.results[c]["out_tm"]
    return out


# version 18 (final)
# speedup vs baseline: 1.2185x; 1.0903x over previous
import math
import numpy as np
import ml_dtypes
import concourse.bass as bass
import concourse.mybir as mybir
from concourse.bass_utils import run_bass_kernel_spmd
from concourse.alu_op_type import AluOpType as OP

F32 = mybir.dt.float32
F32R = mybir.dt.float32r
BF16 = mybir.dt.bfloat16
AF = mybir.ActivationFunctionType
AXX = mybir.AxisListType.X

D = 2048
S = 2048
TI = 1024
DC = 16
NSYN = 8192
KSIM = 204
KFFN = 819
LN_EPS = 1e-5
MAGIC = 12582912.0
NSEC = 6
IC = 256
NIC = TI // IC
TOKB = 256
NTOKB = TI // TOKB
Z0_SIM = 1.2872
Z0_FFN = 1.2816
ISQ2 = 1.0 / math.sqrt(2.0)
BIGM = 1.0e7


class Emit:
    ENGS = ("sync", "vector", "scalar", "tensor", "gpsimd")

    def __init__(self):
        self.q = {e: [] for e in self.ENGS}
        self.cnt = {}
        self.handles = {}

    def sem(self, name):
        self.cnt.setdefault(name, 0)

    def op(self, eng, fn, inc=None, waits=()):
        waits = tuple((s, v) for (s, v) in waits if v > 0)

        def cl(e, fn=fn, inc=inc, waits=waits):
            for (s, v) in waits:
                e.wait_ge(self.handles[s], v)
            inst = fn(e)
            if inc is not None and inst is not None:
                inst.then_inc(self.handles[inc[0]], inc[1])

        self.q[eng].append(cl)
        if inc is not None:
            self.cnt[inc[0]] += inc[1]

    def wait(self, eng, sem, val):
        if val > 0:
            def cl(e, sem=sem, val=val):
                e.wait_ge(self.handles[sem], val)
            self.q[eng].append(cl)

    def c(self, name):
        return self.cnt.get(name, 0)


def build_nc():
    nc = bass.Bass("TRN2", target_bir_lowering=False,
                   detect_race_conditions=False)
    E = Emit()
    SEMS = ("dx", "dg", "sv", "ss", "st", "sg", "dout", "dt", "ds", "scc")
    for s_ in SEMS:
        E.sem(s_)

    da = {}
    def din(name, shape, dt=F32):
        da[name] = nc.dram_tensor(name, list(shape), dt, kind="ExternalInput").ap()

    din("x_fm", (D, S))
    din("w1T", (D, 128)); din("b1c", (128, 1))
    din("w2T", (128, 64)); din("b2st", (128, 1)); din("w3blk", (128, 2))
    din("b3s", (1, 1))
    din("wqkT", (D, 128)); din("bqk_row", (1, 128))
    din("wvT", (D, 64)); din("bv_row", (1, 64))
    din("woutaT", (64, D)); din("bout_pp", (128, DC))
    din("wsynT", (D // 8, NSYN)); din("bsyn_row", (1, NSYN))
    din("woutfT", (NSYN // 8, D)); din("boutf_pp", (128, DC))
    din("gammaA_pp", (128, DC)); din("betaA_pp", (128, DC))
    din("gammaF_pp", (128, DC)); din("betaF_pp", (128, DC))
    din("c_pp", (128, 8))
    din("wmod", (TI, 6 * D))
    din("vauginit", (128, 128), BF16)
    din("idt_bf", (128, 128), BF16)
    din("idt_f32", (128, 128))
    din("iota16_in", (128, 16))
    din("ones_bf_row", (1, 512), BF16)
    din("ones_f32_col", (128, 1))
    din("sc12", (128, 1))
    out_tm = nc.dram_tensor("out_tm", [TI, D], F32, kind="ExternalOutput").ap()

    d_rows = nc.dram_tensor("d_rows", [16, S], F32, kind="Internal").ap()
    d_mod = nc.dram_tensor("d_mod", [1, 6 * D], F32, kind="Internal").ap()
    d_modp = nc.dram_tensor("d_modp", [1, 6 * D], F32, kind="Internal").ap()
    d_scal = nc.dram_tensor("d_scal", [1, 64], F32, kind="Internal").ap()
    d_cs = nc.dram_tensor("d_cs", [1, 32], F32, kind="Internal").ap()
    d_col = nc.dram_tensor("d_col", [1, 128], F32, kind="Internal").ap()
    d_tqsyn = nc.dram_tensor("d_tqsyn", [D, NSYN], BF16, kind="Internal",
                             addr_space="Shared").ap()
    d_tqout = nc.dram_tensor("d_tqout", [NSYN, D], BF16, kind="Internal",
                             addr_space="Shared").ap()
    d_tqsyn_sl = nc.dram_tensor("d_tqsyn_sl", [D // 8, NSYN], BF16,
                                kind="Internal").ap()
    d_tqout_sl = nc.dram_tensor("d_tqout_sl", [NSYN // 8, D], BF16,
                                kind="Internal").ap()
    d_spart = nc.dram_tensor("d_spart", [1, 2], F32, kind="Internal").ap()
    d_ssum = nc.dram_tensor("d_ssum", [1, 2], F32, kind="Internal",
                            addr_space="Shared").ap()
    d_hidT = nc.dram_tensor("d_hidT", [NSYN, TI], BF16, kind="Internal").ap()
    d_x2 = nc.dram_tensor("d_x2", [D, TI], F32, kind="Internal").ap()
    d_partial = nc.dram_tensor("d_partial", [D, TI], F32, kind="Internal").ap()
    d_rowsb = nc.dram_tensor("d_rowsb", [4, S], BF16, kind="Internal").ap()

    import contextlib
    stack = contextlib.ExitStack()
    def sbt(name, p, f, dt):
        return stack.enter_context(nc.sbuf_tensor(name, [p, f], dt))

    AR1 = sbt("AR1", 128, 16384, F32)
    AR2 = sbt("AR2", 128, 16384, F32)
    AR5 = sbt("AR5", 128, 2048, F32)
    AR6 = sbt("AR6", 128, 4096, BF16)
    xraw = sbt("xraw", 128, 2048, F32)
    xbt = sbt("xbt", 128, 2048, BF16)
    xsq = sbt("xsq", 128, 2048, BF16)
    brow = sbt("brow", 128, 4096, BF16)
    brf = sbt("brf", 128, 1024, F32)
    h1n = sbt("h1n", 128, 2048, BF16)
    qks = sbt("qks", 128, 2048, BF16)
    vaug = sbt("vaug", 128, DC * 128, BF16)
    w2t = sbt("w2t", 128, 64, BF16)
    w3b = sbt("w3b", 128, 2, BF16)
    hidst = sbt("hidst", 128, 2048, BF16)
    scr2 = sbt("scr2", 128, 2048, F32)
    stw = sbt("stw", 128, 4 * DC, BF16)
    stw2 = sbt("stw2", 128, 2 * DC, BF16)
    modp = sbt("modp", 128, 96, F32)
    mcon = sbt("mcon", 128, 112, F32)
    gcon = sbt("gcon", 128, 64, F32)
    ppst = sbt("ppst", 128, 12 * DC, F32)
    ppbf = sbt("ppbf", 128, 64, BF16)
    tmpk = sbt("tmpk", 128, 32, F32)
    iota16 = sbt("iota16", 128, 16, F32)
    sel16 = sbt("sel16", 128, 16, F32)
    cand16 = sbt("cand16", 128, 16, F32)
    tcol1 = sbt("tcol1", 128, 1, F32)
    tcol2 = sbt("tcol2", 128, 1, F32)
    idbf = sbt("idbf", 128, 128, BF16)
    idf2 = sbt("idf2", 128, 128, F32)
    onebf = sbt("onebf", 1, 512, BF16)
    onef = sbt("onef", 128, 1, F32)
    onebc = sbt("onebc", 128, 1, BF16)
    srow = sbt("srow", 1, 256, F32)
    scpp = sbt("scpp", 128, DC, F32)
    scrr = sbt("scrr", 128, DC, F32R)
    b1cs = sbt("b1cs", 128, 1, F32)
    b2cs = sbt("b2cs", 128, 1, F32)
    sc12s = sbt("sc12s", 128, 1, F32)
    bobpp = sbt("bobpp", 128, DC, F32)
    bfbpp = sbt("bfbpp", 128, DC, F32)
    gapp = sbt("gapp", 128, DC, F32)
    bapp = sbt("bapp", 128, DC, F32)
    gfpp = sbt("gfpp", 128, DC, F32)
    bfpp = sbt("bfpp", 128, DC, F32)
    bqkr = sbt("bqkr", 1, 128, BF16)
    bvr = sbt("bvr", 1, 64, BF16)
    s1row = sbt("s1row", 1, 128, BF16)
    sclr = sbt("sclr", 1, 16, F32)
    wmodr = sbt("wmodr", 128, 1024, F32R)
    psum_t = stack.enter_context(nc.psum_tensor("p_all", [128, 4096], F32))
    for s_ in SEMS:
        E.handles[s_] = stack.enter_context(nc.semaphore(s_))
    stack.enter_context(nc.allow_non_contiguous_dma(
        reason="small pp-layout rearrange DMAs (KB-scale)"))
    block = stack.enter_context(nc.Block())

    p_all = psum_t[:]
    p_bf = psum_t[:].bitcast(BF16)
    p_mm = psum_t[:, 0:2048]
    p_sc = psum_t[:, 2048:4096]
    AR1b = AR1[:].bitcast(BF16)
    AR2b = AR2[:].bitcast(BF16)
    AR5b = AR5[:].bitcast(BF16)
    wtr = AR5[:]
    mrow = AR5[:]
    wgt = brow[:, 2048:4096]
    h1b = brf[:]
    kjit = AR6[:]
    qjit = xbt[:]
    utile = hidst[:]
    woat = AR6[0:64, 0:2048]
    aotb = hidst[0:64, 0:1024]
    aot = scr2[0:64, 1024:2048]
    aden = scr2[64:128, 1024:2048]
    rowsb = scr2[0:8, 0:2048]
    denr = xraw[0:64, 0:1024]
    x2l = xraw[:, 0:1024]
    ytile = qks[:].bitcast(F32)
    otile = xbt[:].bitcast(F32)
    scr2b = scr2[:].bitcast(BF16)
    bsyn_rows = None

    def dma(dst, src, waits=(), eng="sync", sem="dx", pipe=False):
        waits = list(waits)
        if not pipe:
            waits.append((sem, E.c(sem)))
        E.op(eng, lambda e, d=dst, s=src: e.dma_start(out=d, in_=s), inc=(sem, 16),
             waits=waits)
        return E.c(sem)

    def V(fn, waits=(), dep=True):
        if dep:
            def fn2(e, fn=fn):
                e.drain()
                return fn(e)
        else:
            fn2 = fn
        E.op("vector", fn2, inc=("sv", 1), waits=waits)
        return E.c("sv")

    def SC(fn, waits=(), dep=True):
        if dep:
            def fn2(e, fn=fn):
                e.drain()
                return fn(e)
        else:
            fn2 = fn
        E.op("scalar", fn2, inc=("ss", 1), waits=waits)
        return E.c("ss")

    def T(fn, waits=()):
        E.op("tensor", fn, inc=("st", 1), waits=waits)
        return E.c("st")

    def G(fn, waits=()):
        E.op("gpsimd", fn, inc=("sg", 1), waits=waits)
        return E.c("sg")

    dma(idbf[:], da["idt_bf"])
    dma(idf2[:], da["idt_f32"])
    dma(iota16[:], da["iota16_in"])
    dma(onebf[:], da["ones_bf_row"])
    dma(onef[:], da["ones_f32_col"])
    dma(gcon[:, 0:1], da["b3s"].to_broadcast([128, 1]))
    dma(scpp[:, 0:8], da["c_pp"])
    dma(b1cs[:], da["b1c"])
    dma(b2cs[:], da["b2st"])
    dma(sc12s[:], da["sc12"])
    dma(bobpp[:], da["bout_pp"])
    dma(bfbpp[:], da["boutf_pp"])
    dma(gapp[:], da["gammaA_pp"])
    dma(bapp[:], da["betaA_pp"])
    dma(gfpp[:], da["gammaF_pp"])
    dma(bfpp[:], da["betaF_pp"])
    dma(wtr[:, 0:64], da["w2T"])
    dma(wtr[:, 64:66], da["w3blk"])
    dma(wtr[0:1, 128:256], da["bqk_row"])
    dma(wtr[0:1, 256:320], da["bv_row"])
    ld0 = E.c("dx")
    V(lambda e: e.tensor_copy(w2t[:], wtr[:, 0:64]), [("dx", ld0)])
    V(lambda e: e.tensor_copy(w3b[:], wtr[:, 64:66]))
    V(lambda e: e.tensor_copy(bqkr[:], wtr[0:1, 128:256]))
    V(lambda e: e.tensor_copy(bvr[:], wtr[0:1, 256:320]))
    V(lambda e: e.tensor_copy(onebc[:], onef[:]))
    sm0 = E.c("sv")
    SC(lambda e: e.activation(scpp[:, 0:8], scpp[:, 0:8], AF.Silu), [("dx", ld0)])
    silu_v = V(lambda e: e.tensor_copy(scrr[:, 0:8], scpp[:, 0:8]),
               [("ss", E.c("ss"))])

    consts_v = E.c("sv")

    ev_prev = 0
    rdy_hist = [0, 0]
    last_mm = 0
    KCH = TI // 128
    for cg in range(3):
        for dch in range(KCH):
            slot = dch % 2
            wdeps = [("sv", silu_v)] if (cg == 0 and dch == 0) else []
            if rdy_hist[slot]:
                wdeps.append(("sv", rdy_hist[slot]))
            w = dma(AR1[:, slot * 4096:(slot + 1) * 4096],
                    da["wmod"][dch * 128:(dch + 1) * 128, cg * 4096:(cg + 1) * 4096],
                    waits=wdeps, pipe=True)
            for pc4 in range(4):
                rdeps = [("dx", w)] if pc4 == 0 else []
                if last_mm:
                    rdeps.append(("st", last_mm))
                rdy = V(lambda e, slot=slot, pc4=pc4: e.tensor_copy(
                    wmodr[:, 0:1024],
                    AR1[:, slot * 4096 + pc4 * 1024:
                           slot * 4096 + (pc4 + 1) * 1024]), rdeps)
                for n2 in range(2):
                    nch = 2 * pc4 + n2
                    mmw = [("sv", rdy)] if n2 == 0 else []
                    if cg > 0 and dch == 0 and nch == 0:
                        mmw.append(("sv", ev_prev))
                    T(lambda e, nch=nch, n2=n2, dch=dch: e.matmul(
                        p_all[0:1, nch * 512:(nch + 1) * 512],
                        scrr[:, dch:dch + 1],
                        wmodr[:, n2 * 512:(n2 + 1) * 512],
                        start=(dch == 0), stop=(dch == KCH - 1)), mmw)
                last_mm = E.c("st")
            rdy_hist[slot] = rdy
        V(lambda e: e.tensor_copy(mrow[0:1, 0:2048], p_all[0:1, 0:2048]),
          [("st", E.c("st"))])
        ev_prev = V(lambda e: e.tensor_copy(scr2[0:1, 0:2048], p_all[0:1, 2048:4096]))
        dma(d_modp[:, cg * 4096:cg * 4096 + 2048], mrow[0:1, 0:2048],
            waits=[("sv", ev_prev)])
        dma(d_modp[:, cg * 4096 + 2048:(cg + 1) * 4096], scr2[0:1, 0:2048],
            waits=[("sv", ev_prev)])
    mod_d = E.c("dx")
    E.op("gpsimd", lambda e: e.collective_compute(
        "AllReduce", OP.add, replica_groups=[[0, 1], [2, 3], [4, 5], [6, 7]],
        ins=[d_modp.opt()], outs=[d_mod.opt()]), inc=("scc", 1),
        waits=[("dx", mod_d)])
    mod_cc = E.c("scc")
    for m_ in range(6):
        w = dma(modp[:, m_ * 16:(m_ + 1) * 16],
                d_mod[:, m_ * 2048:(m_ + 1) * 2048]
                .rearrange("o (k p) -> (o p) k", k=16),
                waits=[("scc", mod_cc)] if m_ == 0 else ())
    mod_pp = w

    sa_t = modp[:, 0:16]; ca_t = modp[:, 16:32]; gaw = modp[:, 32:48]
    sb_t = modp[:, 48:64]; cb_t = modp[:, 64:80]; gbw = modp[:, 80:96]
    A1 = mcon[:, 0:16]; A2 = mcon[:, 16:32]
    gaT = mcon[:, 32:48]; gbT = mcon[:, 48:64]
    A1sq = mcon[:, 64:80]; A1B1 = mcon[:, 80:96]; B1sq = mcon[:, 96:112]
    V(lambda e: e.tensor_scalar(A1, ca_t, 1.0, None, OP.add), [("dx", mod_pp)])
    V(lambda e: e.tensor_scalar(A2, cb_t, 1.0, None, OP.add))
    for (gsrc, gdst) in ((gaw, gaT), (gbw, gbT)):
        V(lambda e, gs=gsrc, gd=gdst: e.tensor_scalar(gd, gs, 0.0, 2.0, OP.is_ge,
                                                      OP.mult))
        V(lambda e, gd=gdst: e.tensor_scalar(gd, gd, 1.0, None, OP.subtract))
        V(lambda e, gs=gsrc: e.tensor_scalar(tmpk[:, 0:16], gs, -1.0, None, OP.mult))
        V(lambda e, gs=gsrc: e.tensor_tensor(tmpk[:, 0:16], tmpk[:, 0:16], gs, OP.max))
        V(lambda e: e.tensor_scalar(tmpk[:, 0:16], tmpk[:, 0:16], 0.25, None, OP.max))
        V(lambda e, gd=gdst: e.tensor_tensor(gd, gd, tmpk[:, 0:16], OP.mult))
    V(lambda e: e.tensor_tensor(A1sq, A1, A1, OP.mult))
    V(lambda e: e.tensor_tensor(A1B1, A1, sa_t, OP.mult))
    V(lambda e: e.tensor_tensor(B1sq, sa_t, sa_t, OP.mult))
    stwv = stw[:].rearrange("p (k f) -> p k f", f=4)
    stw2v = stw2[:].rearrange("p (k f) -> p k f", f=2)
    V(lambda e: e.memset(stwv[:, :, 0:1], 1.0))
    V(lambda e: e.tensor_copy(stwv[:, :, 1:2], A1.unsqueeze(2)))
    V(lambda e: e.tensor_copy(stwv[:, :, 2:3], A1sq.unsqueeze(2)))
    V(lambda e: e.tensor_copy(stwv[:, :, 3:4], A1B1.unsqueeze(2)))
    V(lambda e: e.memset(stw2v[:, :, 0:1], 1.0))
    drv_v = V(lambda e: e.tensor_copy(stw2v[:, :, 1:2], A1sq.unsqueeze(2)))
    m1 = T(lambda e: e.matmul(p_sc[0:1, 0:112], onef[:], mcon[:, 0:112],
                              start=True, stop=True), [("sv", drv_v)])
    m2 = T(lambda e: e.matmul(p_sc[0:1, 112:128], onef[:], sa_t,
                              start=True, stop=True))
    V(lambda e: e.tensor_copy(srow[0:1, 0:128], p_sc[0:1, 0:128]), [("st", m2)])
    for i, off in enumerate((0, 64, 80, 96, 112)):
        V(lambda e, off=off, i=i: e.reduce_sum(srow[0:1, 140 + i:141 + i],
                                               srow[0:1, off:off + 16], axis=AXX))
    cscal_v = V(lambda e: e.tensor_scalar(srow[0:1, 145:146], srow[0:1, 144:145],
                                          1.0 / D, None, OP.mult))
    w = dma(d_scal[:, 0:6], srow[0:1, 140:146], waits=[("sv", cscal_v)])
    gcon_scal = dma(gcon[:, 8:14], d_scal[:, 0:6].to_broadcast([128, 6]),
                    waits=[("dx", w)])

    cast_hist = [0] * DC
    sq_hist = [0] * DC
    mm_hist = [0] * DC
    for dch in range(DC):
        wd = [("sv", E.c("sv")), ("ss", E.c("ss"))]
        if dch == 0:
            wd.append(("dx", gcon_scal))
        w = dma(xraw[:, 0:2048], da["x_fm"][dch * 128:(dch + 1) * 128, :], waits=wd,
                pipe=True)
        cd = [("dx", w)]
        if dch >= 1:
            cd.append(("st", mm_hist[dch - 1]))
        cast_hist[dch] = V(lambda e: e.tensor_copy(xbt[:, 0:2048], xraw[:, 0:2048]),
                           cd)
        sd = [("sv", cast_hist[dch])]
        if dch >= 1:
            sd.append(("st", mm_hist[dch - 1]))
        sq_hist[dch] = SC(lambda e: e.activation(xsq[:, 0:2048], xbt[:, 0:2048],
                                                 AF.Square), sd)
        for nch in range(4):
            T(lambda e, dch=dch, nch=nch: e.matmul(
                p_sc[0:4, nch * 512:(nch + 1) * 512],
                stw[:, 4 * dch:4 * dch + 4],
                xbt[:, nch * 512:(nch + 1) * 512],
                start=(dch == 0), stop=(dch == DC - 1)),
              [("sv", cast_hist[dch])] if nch == 0 else ())
            T(lambda e, dch=dch, nch=nch: e.matmul(
                p_mm[0:2, nch * 512:(nch + 1) * 512],
                stw2[:, 2 * dch:2 * dch + 2],
                xsq[:, nch * 512:(nch + 1) * 512],
                start=(dch == 0), stop=(dch == DC - 1)),
              [("ss", sq_hist[dch])] if nch == 0 else ())
        mm_hist[dch] = E.c("st")
    V(lambda e: e.tensor_copy(rowsb[0:4, :], p_sc[0:4, 0:2048]), [("st", E.c("st"))])
    ev_rows = V(lambda e: e.tensor_copy(AR5[0:2, 0:2048], p_mm[0:2, 0:2048]))
    w = dma(d_rows[0:4, :], rowsb[0:4, :], waits=[("sv", ev_rows)])
    w = dma(d_rows[4:6, :], AR5[0:2, 0:2048], waits=[("sv", ev_rows)])
    for r_ in range(6):
        pp_d = dma(ppst[:, r_ * 16:(r_ + 1) * 16],
                   d_rows[r_:r_ + 1, :].rearrange("r (k p) -> (r p) k", k=16),
                   waits=[("dx", w)] if r_ == 0 else ())

    B_ = lambda i: ppst[:, i * 16:(i + 1) * 16]
    Sx, SAx, SA2x, SABx, Sxx, SA2xx = B_(0), B_(1), B_(2), B_(3), B_(4), B_(5)
    mu1, rs1, muh, rsh, t1_, t2_ = B_(6), B_(7), B_(8), B_(9), B_(10), B_(11)
    gsc = lambda i: gcon[:, 8 + i:9 + i]
    V(lambda e: e.tensor_scalar(mu1, Sx, 1.0 / D, None, OP.mult), [("dx", pp_d)])
    V(lambda e: e.tensor_tensor(t1_, mu1, mu1, OP.mult))
    V(lambda e: e.tensor_scalar(t2_, Sxx, 1.0 / D, None, OP.mult))
    V(lambda e: e.tensor_tensor(t2_, t2_, t1_, OP.subtract))
    V(lambda e: e.tensor_scalar(t2_, t2_, LN_EPS, None, OP.add))
    rv = V(lambda e: e.reciprocal(t2_, t2_))
    rs1_s = SC(lambda e: e.activation(rs1, t2_, AF.Sqrt), [("sv", rv)])
    V(lambda e: e.scalar_tensor_tensor(t1_, mu1, gsc(0), SAx, OP.mult, OP.subtract),
      [("ss", rs1_s)])
    V(lambda e: e.tensor_tensor(t1_, t1_, rs1, OP.mult))
    V(lambda e: e.tensor_scalar(t1_, t1_, -1.0 / D, None, OP.mult))
    V(lambda e: e.tensor_scalar(muh, t1_, gsc(5), None, OP.add))
    V(lambda e: e.tensor_tensor(t1_, mu1, SA2x, OP.mult))
    V(lambda e: e.scalar_tensor_tensor(t1_, t1_, -2.0, SA2xx, OP.mult, OP.add))
    V(lambda e: e.tensor_tensor(t2_, mu1, mu1, OP.mult))
    V(lambda e: e.tensor_scalar(t2_, t2_, gsc(1), None, OP.mult))
    V(lambda e: e.tensor_tensor(t1_, t1_, t2_, OP.add))
    V(lambda e: e.tensor_tensor(t2_, rs1, rs1, OP.mult))
    V(lambda e: e.tensor_tensor(t1_, t1_, t2_, OP.mult))
    V(lambda e: e.scalar_tensor_tensor(t2_, mu1, gsc(2), SABx, OP.mult, OP.subtract))
    V(lambda e: e.scalar_tensor_tensor(t2_, t2_, -2.0, rs1, OP.mult, OP.mult))
    V(lambda e: e.tensor_tensor(t1_, t1_, t2_, OP.add))
    V(lambda e: e.tensor_scalar(t1_, t1_, gsc(3), None, OP.add))
    V(lambda e: e.tensor_scalar(t1_, t1_, 1.0 / D, None, OP.mult))
    V(lambda e: e.tensor_tensor(t2_, muh, muh, OP.mult))
    V(lambda e: e.tensor_tensor(t1_, t1_, t2_, OP.subtract))
    V(lambda e: e.tensor_scalar(t1_, t1_, LN_EPS, None, OP.add))
    rv = V(lambda e: e.reciprocal(t1_, t1_))
    rsh_s = SC(lambda e: e.activation(rsh, t1_, AF.Sqrt), [("sv", rv)])
    ppc = V(lambda e: e.tensor_copy(ppbf[:].rearrange("p (r k) -> p r k", r=4),
                                    ppst[:].rearrange("p (r k) -> p r k", r=12)
                                    [:, 6:10, :]), [("ss", rsh_s)])
    for r_ in range(4):
        w = dma(d_rowsb[r_:r_ + 1, :].rearrange("r (k p) -> (r p) k", k=16),
                ppbf[:, r_ * 16:(r_ + 1) * 16], waits=[("sv", ppc)] if r_ == 0 else ())
    mu1b = brow[:, 0:2048]; rs1b = brow[:, 2048:4096]
    muhb = vaug[:, 0:2048]; rshb = qks[:, 0:2048]
    brow_v = 0
    for i, dst in enumerate((mu1b, rs1b, muhb, rshb)):
        brow_v = dma(dst, d_rowsb[i:i + 1, :].to_broadcast([128, 2048]),
                     waits=[("dx", w)])
    E.wait("vector", "dx", brow_v)
    brow_v_sem = ("dx", brow_v)

    gA1 = mcon[:, 64:80]
    gB1o = mcon[:, 80:96]
    V(lambda e: e.tensor_tensor(gA1, gaT, A1, OP.mult),
      [("st", E.c("st")), ("dx", E.c("dx"))])
    V(lambda e: e.tensor_tensor(gB1o, sa_t, bobpp[:], OP.add))
    V(lambda e: e.tensor_tensor(gB1o, gaT, gB1o, OP.mult))
    xn_hist = [0] * DC
    rstt_hist = 0
    for dch in range(DC):
        wd = [brow_v_sem, ("sv", E.c("sv")), ("ss", E.c("ss"))]
        w = dma(xraw[:, 0:2048], da["x_fm"][dch * 128:(dch + 1) * 128, :], waits=wd,
                pipe=True)
        cd = [("dx", w)]
        if dch >= 1:
            cd.append(("ss", xn_hist[dch - 1]))
        V(lambda e: e.tensor_copy(xbt[:, 0:2048], xraw[:, 0:2048]), cd)
        xb_ = xbt[:, 0:2048]
        xh_ = xsq[:, 0:2048]
        V(lambda e, xh_=xh_, xb_=xb_: e.tensor_tensor(xh_, xb_, mu1b, OP.subtract))
        xh_v = V(lambda e, xh_=xh_: e.tensor_tensor(xh_, xh_, rs1b, OP.mult))
        h_s = SC(lambda e, xh_=xh_, dch=dch: e.activation(
            scr2[:, 0:1024], xh_[:, 0:1024], AF.Identity,
            bias=gB1o[:, dch:dch + 1], scale=gA1[:, dch:dch + 1]),
            [("sv", xh_v), ("sv", rstt_hist)])
        rstt_hist = V(lambda e, dch=dch: e.tensor_tensor(
            AR2[:, dch * 1024:(dch + 1) * 1024], scr2[:, 0:1024],
            xraw[:, 0:1024], OP.add), [("ss", h_s)])
        v1_s = SC(lambda e, xb_=xb_, xh_=xh_, dch=dch: e.activation(
            xb_, xh_, AF.Identity, bias=sa_t[:, dch:dch + 1],
            scale=A1[:, dch:dch + 1]), [("sv", E.c("sv"))])
        V(lambda e, xh_=xh_, xb_=xb_: e.tensor_tensor(xh_, xb_, muhb, OP.subtract),
          [("ss", v1_s)])
        xnb_v = V(lambda e, xh_=xh_: e.tensor_tensor(xh_, xh_, rshb, OP.mult))
        xn_hist[dch] = SC(lambda e, xh_=xh_, dch=dch: e.activation(
            AR1b[:, dch * 2048:(dch + 1) * 2048], xh_, AF.Identity,
            bias=bapp[:, dch:dch + 1], scale=gapp[:, dch:dch + 1]), [("sv", xnb_v)])
    xn_s = E.c("ss")
    r_v = E.c("sv")

    xn_ = lambda dch: AR1b[:, dch * 2048:(dch + 1) * 2048]
    w = dma(wtr[:, 0:2048].rearrange("p (c m) -> p c m", m=128),
            da["w1T"].rearrange("(c p) m -> p c m", p=128),
            waits=[("ss", xn_s)])
    wgt_v = V(lambda e: e.tensor_copy(wgt, wtr[:, 0:2048]), [("dx", w)])
    evac_hist = [0, 0]
    for nch in range(4):
        bank = nch % 2
        for dch in range(DC):
            wd = []
            if dch == 0:
                wd = [("ss", xn_s), ("sv", wgt_v)]
                if nch >= 2:
                    wd.append(("sv", evac_hist[bank]))
            T(lambda e, bank=bank, nch=nch, dch=dch: e.matmul(
                p_mm[:, bank * 512:bank * 512 + 512],
                wgt[:, dch * 128:(dch + 1) * 128],
                xn_(dch)[:, nch * 512:(nch + 1) * 512],
                start=(dch == 0), stop=(dch == DC - 1)), wd)
        mmv = E.c("st")
        ev = V(lambda e, bank=bank, nch=nch: e.tensor_scalar(
            h1n[:, nch * 512:(nch + 1) * 512],
            p_mm[:, bank * 512:bank * 512 + 512], -1.0, None, OP.mult),
          [("st", mmv)])
        if nch < 2:
            ev = V(lambda e, bank=bank, nch=nch: e.tensor_scalar(
                h1b[:, nch * 512:(nch + 1) * 512],
                p_mm[:, bank * 512:bank * 512 + 512], b1cs[:, 0:1], None, OP.add))
        evac_hist[bank] = ev
    h1_v = E.c("sv")
    w = dma(wtr[:, 0:2048].rearrange("p (c m) -> p c m", m=128),
            da["wqkT"].rearrange("(c p) m -> p c m", p=128),
            waits=[("st", E.c("st"))])
    wgt_v = V(lambda e: e.tensor_copy(wgt, wtr[:, 0:2048]), [("dx", w)])
    wb = dma(wtr[0:1, 0:128], da["bqk_row"], waits=[("sv", wgt_v)])
    wgt_v = V(lambda e: e.tensor_copy(bqkr[:], wtr[0:1, 0:128]), [("dx", wb)])
    evac_hist = [0, 0]
    for nch in range(4):
        bank = nch % 2
        for dch in range(DC):
            wd = []
            if dch == 0:
                wd = [("sv", h1_v), ("sv", wgt_v)] if nch < 2 else [("sv", evac_hist[bank])]
            T(lambda e, bank=bank, nch=nch, dch=dch: e.matmul(
                p_sc[:, bank * 512:bank * 512 + 512],
                wgt[:, dch * 128:(dch + 1) * 128],
                xn_(dch)[:, nch * 512:(nch + 1) * 512],
                start=(dch == 0), stop=False), wd)
        T(lambda e, bank=bank: e.matmul(
            p_sc[:, bank * 512:bank * 512 + 512],
            bqkr[:], onebf[:, 0:512], start=False, stop=True))
        mmv = E.c("st")
        evac_hist[bank] = V(lambda e, bank=bank, nch=nch: e.tensor_scalar(
            qks[:, nch * 512:(nch + 1) * 512],
            p_sc[:, bank * 512:bank * 512 + 512],
            sc12s[:, 0:1], None, OP.mult), [("st", mmv)])
    qk_v = E.c("sv")
    w = dma(wtr[:, 0:1024].rearrange("p (c m) -> p c m", m=64),
            da["wvT"].rearrange("(c p) m -> p c m", p=128),
            waits=[("st", E.c("st"))])
    wgt_v = V(lambda e: e.tensor_copy(wgt[:, 0:1024], wtr[:, 0:1024]), [("dx", w)])
    wb = dma(wtr[0:1, 0:64], da["bv_row"], waits=[("sv", wgt_v)])
    wgt_v = V(lambda e: e.tensor_copy(bvr[:], wtr[0:1, 0:64]), [("dx", wb)])
    vgi = dma(vaug[:].rearrange("p (c m) -> p c m", m=128),
              da["vauginit"].unsqueeze(1).to_broadcast([128, DC, 128]),
              waits=[("ss", xn_s), ("st", E.c("st"))])
    vaugv = vaug[:].rearrange("p (c g f) -> p c g f", c=DC, g=32)
    evac_hist = [0, 0, 0, 0]
    for jc in range(DC):
        bank = jc % 4
        for dch in range(DC):
            wd = []
            if dch == 0:
                wd = [("sv", qk_v), ("sv", wgt_v)] if jc < 4 else [("sv", evac_hist[bank])]
            T(lambda e, bank=bank, jc=jc, dch=dch: e.matmul(
                p_mm[:, bank * 512:bank * 512 + 64],
                xn_(dch)[:, jc * 128:(jc + 1) * 128],
                wgt[:, dch * 64:(dch + 1) * 64],
                start=(dch == 0), stop=False), wd)
        T(lambda e, bank=bank: e.matmul(
            p_mm[:, bank * 512:bank * 512 + 64],
            onebf[0:1, 0:128], bvr[:], start=False, stop=True))
        mmv = E.c("st")
        evac_hist[bank] = V(lambda e, bank=bank, jc=jc: e.tensor_copy(
            vaugv[:, jc, :, 0:2],
            p_mm[:, bank * 512:bank * 512 + 64].rearrange("p (g f) -> p g f", g=32)),
            [("st", mmv), ("dx", vgi)] if jc == 0 else [("st", mmv)])
    vt_v = E.c("sv")
    csm = V(lambda e: e.reduce_max(tcol1[:], qks[:], axis=AXX,
                                   apply_absolute_value=True), [("sv", vt_v)])
    w = dma(d_col[:, 0:128].rearrange("o p -> p o"), tcol1[:, 0:1],
            waits=[("sv", csm)])
    w = dma(srow[0:1, 0:128], d_col[:, 0:128], waits=[("dx", w)])
    srv = srow[0:1, 0:128].rearrange("o (a h f) -> o a h f", a=2, h=32)
    V(lambda e: e.tensor_tensor(mrow[0:1, 0:64].rearrange("o (a h) -> o a h", a=2),
                                srv[:, :, :, 0], srv[:, :, :, 1], OP.max), [("dx", w)])
    V(lambda e: e.tensor_tensor(mrow[0:1, 64:96], mrow[0:1, 0:32], mrow[0:1, 32:64],
                                OP.mult))
    csv = V(lambda e: e.tensor_scalar(mrow[0:1, 64:96], mrow[0:1, 64:96], 60.0, -1.0,
                                      OP.min, OP.mult))
    w = dma(d_cs[:, 0:32], mrow[0:1, 64:96], waits=[("sv", csv)])
    cs_pp = dma(gcon[:, 16:48], d_cs[:, 0:32].to_broadcast([128, 32]),
                waits=[("dx", w)])
    E.wait("vector", "dx", cs_pp)

    def topk(zblk, W, k, z0, tcol, scr_bf, u_f32, pre, sw=None):
        sw = sw or W
        np_ = W // sw
        col = lambda i: tmpk[:, i:i + 1]
        pw = list(pre)
        for pc in range(np_):
            SC(lambda e, pc=pc: e.activation(
                scr_bf[:, 0:sw], zblk[:, pc * sw:(pc + 1) * sw], AF.Square,
                accum_out=col(12 + pc)), pw)
        sq_s = E.c("ss")
        V(lambda e: e.reduce_sum(col(0), zblk[:, 0:W], axis=AXX), pw)
        if np_ == 2:
            V(lambda e: e.tensor_tensor(col(1), col(12), col(13), OP.add),
              [("ss", sq_s)])
        else:
            V(lambda e: e.tensor_copy(col(1), col(12)), [("ss", sq_s)])
        V(lambda e: e.tensor_scalar(col(0), col(0), 1.0 / W, None, OP.mult))
        V(lambda e: e.tensor_scalar(col(1), col(1), 1.0 / W, None, OP.mult))
        V(lambda e: e.scalar_tensor_tensor(col(2), col(0), col(0), col(1), OP.mult,
                                           OP.subtract))
        vv = V(lambda e: e.tensor_scalar(col(2), col(2), -1.0, 1e-12, OP.mult, OP.max))
        sg_s = SC(lambda e: e.activation(col(3), col(2), AF.Sqrt), [("sv", vv)])
        V(lambda e: e.scalar_tensor_tensor(col(4), col(3), z0, col(0), OP.mult, OP.add),
          [("ss", sg_s)])
        V(lambda e: e.scalar_tensor_tensor(col(5), col(3), -8.0, col(0), OP.mult,
                                           OP.add))
        V(lambda e: e.scalar_tensor_tensor(col(6), col(3), 8.0, col(0), OP.mult,
                                           OP.add))
        V(lambda e: e.memset(col(7), float(W)))
        V(lambda e: e.memset(col(8), 0.0))

        def count_scan(tgt, op):
            for pc in range(np_):
                V(lambda e, pc=pc, op=op: e.tensor_scalar(
                    scr_bf[:, 0:sw], zblk[:, pc * sw:(pc + 1) * sw],
                    col(4) if op == OP.is_ge else col(6), None,
                    op, OP.add, accum_out=col(12 + pc)))
            if np_ == 2:
                V(lambda e, tgt=tgt: e.tensor_tensor(tgt, col(12), col(13), OP.add))
            else:
                V(lambda e, tgt=tgt: e.tensor_copy(tgt, col(12)))

        for it in range(NSEC):
            count_scan(col(11), OP.is_ge)
            V(lambda e: e.tensor_scalar(col(9), col(11), k + 0.5, None, OP.is_ge))
            V(lambda e: e.tensor_scalar(col(10), col(9), -1.0, 1.0, OP.mult, OP.add))
            V(lambda e: e.tensor_tensor(col(12), col(4), col(5), OP.subtract))
            V(lambda e: e.scalar_tensor_tensor(col(5), col(12), col(9), col(5),
                                               OP.mult, OP.add))
            V(lambda e: e.tensor_tensor(col(12), col(11), col(7), OP.subtract))
            V(lambda e: e.scalar_tensor_tensor(col(7), col(12), col(9), col(7),
                                               OP.mult, OP.add))
            V(lambda e: e.tensor_tensor(col(12), col(4), col(6), OP.subtract))
            V(lambda e: e.scalar_tensor_tensor(col(6), col(12), col(10), col(6),
                                               OP.mult, OP.add))
            V(lambda e: e.tensor_tensor(col(12), col(11), col(8), OP.subtract))
            V(lambda e: e.scalar_tensor_tensor(col(8), col(12), col(10), col(8),
                                               OP.mult, OP.add))
            if it < NSEC - 1:
                V(lambda e: e.tensor_tensor(col(12), col(7), col(8), OP.subtract))
                V(lambda e: e.reciprocal(col(12), col(12)))
                V(lambda e: e.tensor_scalar(col(13), col(7), float(-(k - 3)), None,
                                            OP.add))
                V(lambda e: e.tensor_tensor(col(13), col(13), col(12), OP.mult))
                V(lambda e: e.tensor_scalar(col(13), col(13), 0.02, 0.98, OP.max,
                                            OP.min))
                V(lambda e: e.tensor_tensor(col(12), col(6), col(5), OP.subtract))
                V(lambda e: e.scalar_tensor_tensor(col(4), col(12), col(13), col(5),
                                                   OP.mult, OP.add))
        V(lambda e: e.memset(col(14), 0.0))
        for pc in range(np_):
            V(lambda e, pc=pc: e.tensor_scalar(
                scr_bf[:, 0:sw], zblk[:, pc * sw:(pc + 1) * sw], col(6), None,
                OP.is_lt, OP.add, accum_out=col(12)))
            V(lambda e: e.tensor_tensor(col(14), col(14), col(12), OP.add))
            V(lambda e, pc=pc: e.scalar_tensor_tensor(
                u_f32[:, pc * sw:(pc + 1) * sw], scr_bf[:, 0:sw], BIGM,
                zblk[:, pc * sw:(pc + 1) * sw], OP.mult, OP.add))
        V(lambda e: e.max(out=tmpk[:, 16:24], in_=u_f32[:, 0:W]))
        V(lambda e: e.match_replace(out=u_f32[:, 0:W], in_to_replace=tmpk[:, 16:24],
                                    in_values=u_f32[:, 0:W], imm_value=-1.0e30))
        V(lambda e: e.max(out=tmpk[:, 24:32], in_=u_f32[:, 0:W]))
        V(lambda e: e.tensor_scalar(col(15), col(14), float(k - W - 1), None, OP.add))
        V(lambda e: e.tensor_scalar(col(12), col(15), 0.0, 15.0, OP.max, OP.min))
        V(lambda e: e.tensor_scalar(sel16[:], iota16[:], col(12), None, OP.is_equal))
        V(lambda e: e.tensor_scalar(cand16[:], tmpk[:, 16:32], -BIGM, None, OP.add))
        V(lambda e: e.scalar_tensor_tensor(cand16[:], cand16[:], 1.0, sel16[:],
                                           OP.mult, OP.mult, accum_out=col(13)))
        V(lambda e: e.tensor_scalar(col(9), col(15), 0.0, None, OP.is_lt))
        V(lambda e: e.tensor_tensor(col(12), col(6), col(13), OP.subtract))
        return V(lambda e, tcol=tcol: e.scalar_tensor_tensor(
            tcol[:, 0:1], col(12), col(9), col(13), OP.mult, OP.add))

    indT = AR1b[:, 0:16384]
    Ebuf = AR1b[:, 16384:32768]
    simblk = AR5[:, 0:2048]
    simsp = scr2[:, 0:2048]
    pairA = AR6[:, 0:2048]
    pairB = AR6[:, 2048:4096]

    paccum = xsq[:].bitcast(F32)[:, 0:256]
    pa_state = {"s": [0, 0], "idx": 0}
    passA_start_sv = E.c("sv")
    passA_start_ss = E.c("ss")
    pa_chunks = []
    for srcq in (da["wsynT"], da["woutfT"]):
        for pch in range(srcq.shape[0] // 128):
            for cc2 in range(srcq.shape[1] // 1024):
                pa_chunks.append(
                    srcq[pch * 128:(pch + 1) * 128,
                         cc2 * 1024:(cc2 + 1) * 1024])

    def emit_passA_chunk():
        qa_idx = pa_state["idx"]
        if qa_idx >= len(pa_chunks):
            return
        slot = qa_idx % 2
        wl = [("ss", pa_state["s"][slot])]
        if qa_idx == 0:
            wl += [("sv", passA_start_sv), ("ss", passA_start_ss)]
        w = dma(xraw[:, slot * 1024:(slot + 1) * 1024], pa_chunks[qa_idx],
                waits=wl, eng="gpsimd", sem="dg", pipe=True)
        pa_state["s"][slot] = SC(
            lambda e, slot=slot, qa_idx=qa_idx: e.activation(
                xraw[:, slot * 1024:(slot + 1) * 1024],
                xraw[:, slot * 1024:(slot + 1) * 1024],
                AF.Abs, accum_out=paccum[:, qa_idx:qa_idx + 1]),
            [("dg", w)], dep=False)
        pa_state["idx"] = qa_idx + 1

    V(lambda e: e.memset(p_sc[:, 0:2048], 0.0), [("st", E.c("st"))])

    mmA_hist = 0
    mmA_par = [0, 0]
    indT_ev = 0
    uev_reg = [("ss", 0), ("ss", 0)]
    se_pp = [0, 0]
    rearr_dx = 0
    rearr_ds = 0
    se_hist = 0
    rearr_hist = 0
    ind_v_hist = 0
    indT_done = 0
    for ic in range(NIC):
        for sb2 in range(2):
            sbi = ic * 2 + sb2
            qoff = ic * 256 + sb2 * 128
            grp_dma0 = E.c("dx")
            for g in range(32):
                emit_passA_chunk()
                conB_pp = [0, 0]
                for pp_ in range(2):
                    qA = qoff + 4 * g + 2 * pp_
                    qB = qA + 1
                    pA = pairA if pp_ == 0 else brow[:, 0:2048]
                    pB = pairB if pp_ == 0 else brow[:, 2048:4096]
                    cwA = [("st", mmA_par[pp_])]
                    if sbi == 0 and g == 0 and pp_ == 0:
                        cwA.append(("dx", cs_pp))
                    V(lambda e, qA=qA, pA=pA: e.tensor_scalar(
                        pA[:, 0:2048], h1n[:], h1b[:, qA:qA + 1], 0.0,
                        OP.add, OP.max), cwA, dep=False)
                    conB_pp[pp_] = V(lambda e, qB=qB, pB=pB: e.tensor_scalar(
                        pB[:, 0:2048], h1n[:], h1b[:, qB:qB + 1], 0.0,
                        OP.add, OP.max), dep=False)
                regs_pair = (p_mm, p_sc)
                utl = (utile[:, 0:2048], xbt[:, 0:2048])
                for pp_ in range(2):
                    ri = (g + pp_) % 2
                    reg = regs_pair[ri]
                    pA = pairA if pp_ == 0 else brow[:, 0:2048]
                    pB = pairB if pp_ == 0 else brow[:, 2048:4096]
                    mmAw_extra = [("sv", indT_ev)] if (g == 0 and pp_ == 0) else []
                    for nch2 in range(4):
                        wd = ([("sv", conB_pp[pp_]), uev_reg[ri]] +
                              ([("ss", se_pp[0])] if pp_ == 1 else []) +
                              mmAw_extra) if nch2 == 0 else []
                        T(lambda e, nch2=nch2, pA=pA, reg=reg: e.matmul(
                            reg[0:64, nch2 * 512:(nch2 + 1) * 512], w2t[:],
                            pA[:, nch2 * 512:(nch2 + 1) * 512],
                            start=True, stop=True, tile_position=(0, 0)), wd)
                        T(lambda e, nch2=nch2, pB=pB, reg=reg: e.matmul(
                            reg[64:128, nch2 * 512:(nch2 + 1) * 512], w2t[:],
                            pB[:, nch2 * 512:(nch2 + 1) * 512],
                            start=True, stop=True, tile_position=(0, 64)))
                    mmA_hist = E.c("st")
                    mmA_par[pp_] = mmA_hist
                creg = regs_pair[g % 2]
                for pp_ in range(2):
                    ri = (g + pp_) % 2
                    reg = regs_pair[ri]
                    uv = SC(lambda e, reg=reg, pp_=pp_: e.activation(
                        utl[pp_], reg[:, 0:2048], AF.Relu,
                        bias=b2cs[:, 0:1]), [("st", mmA_par[pp_])], dep=False)
                    uev_reg[ri] = ("ss", uv)
                    for nch2 in range(4):
                        wd = [uev_reg[ri]] if nch2 == 0 else []
                        T(lambda e, nch2=nch2, pp_=pp_, creg=creg: e.matmul(
                            creg[64 * pp_:64 * pp_ + 2,
                                 nch2 * 512:(nch2 + 1) * 512],
                            w3b[:, 0:2], utl[pp_][:, nch2 * 512:(nch2 + 1) * 512],
                            start=True, stop=True, tile_position=(0, 64 * pp_)), wd)
                mmC_hist = E.c("st")
                sew = [("st", mmC_hist), ("dx", rearr_dx)]
                se_hist = SC(lambda e, creg=creg: e.activation(
                    scr2[0:66, 0:2048], creg[0:66, 0:2048],
                    AF.Identity, bias=gcon[0:66, 0:1]), sew, dep=False)
                se_pp[0] = se_hist
                rw = [("ss", se_hist)]
                if g == 0:
                    rw.append(("sv", ind_v_hist))
                for j, row in enumerate((0, 1, 64, 65)):
                    dma(simblk[4 * g + j:4 * g + j + 1, :],
                        scr2[row:row + 1, 0:2048],
                        waits=rw if j == 0 else (), eng="sync", sem="dx",
                        pipe=True)
                rearr_dx = E.c("dx")
            dg_v = G(lambda e, qoff=qoff: e.affine_select(
                out=simblk[:, 0:2048], in_=simblk[:, 0:2048],
                compare_op=OP.not_equal, fill=0.0, base=qoff,
                pattern=[[-1, 2048]], channel_multiplier=1),
                [("dx", rearr_dx)])
            topk(simblk, 2048, KSIM, Z0_SIM, tcol1, pairA, scr2,
                 [("sg", dg_v), ("st", mmA_hist), ("st", indT_done)])
            ind_v_hist = V(lambda e: e.tensor_scalar(
                pairB[:, 0:2048], simblk[:, 0:2048], tcol1[:, 0:1], None, OP.is_ge))
            for jb in range(4):
                tw = [("sv", ind_v_hist)] if jb == 0 else [("sv", indT_ev)]
                for j2 in range(4):
                    jc = 4 * jb + j2
                    T(lambda e, jc=jc, j2=j2: e.transpose(
                        p_bf[:, 2048 + j2 * 128:2048 + (j2 + 1) * 128],
                        pairB[:, jc * 128:(jc + 1) * 128], idbf[:]),
                      tw if j2 == 0 else ())
                trp = E.c("st")
                indT_ev = V(lambda e, jb=jb, qoff=qoff: e.tensor_copy(
                    indT.rearrange("p (jc i) -> p jc i", i=1024)
                        [:, 4 * jb:4 * jb + 4, qoff:qoff + 128],
                    p_bf[:, 2048:2560].rearrange("p (a b) -> p a b", a=4)),
                  [("st", trp)])
            indT_done = E.c("st")
    while pa_state["idx"] < len(pa_chunks):
        emit_passA_chunk()
    passA_last_s = E.c("ss")
    topk_sim_done_v = E.c("sv")
    topk_sim_done_s = E.c("ss")

    exp_hist = [0, 0, 0, 0]
    ao_h = [0, 0, 0, 0]
    ao_last = 0
    ao_dg = 0
    score_hist = [[0, 0, 0, 0] for _ in range(10)]
    jitd = [0] * 10
    aoev_hist = 0
    aodma_hist = 0
    for quad in range(8):
        buf = quad % 2
        jw = [("sv", topk_sim_done_v)] if quad else [
            ("sv", topk_sim_done_v), ("ss", passA_last_s)]
        if quad >= 2:
            jw.append(("st", max(score_hist[quad - 2])))
        for hh in range(4):
            h = 4 * quad + hh
            dma(kjit[32 * hh:32 * hh + 2, buf * 2048:(buf + 1) * 2048],
                qks[64 + 2 * h:64 + 2 * h + 2, 0:2048], waits=jw if hh == 0 else (),
                eng="gpsimd", sem="dg", pipe=True)
            dma(qjit[32 * hh:32 * hh + 2, buf * 1024:(buf + 1) * 1024],
                qks[2 * h:2 * h + 2, 0:1024], eng="gpsimd", sem="dg", pipe=True)
        jitd[quad] = E.c("dg")
        for ic in range(NIC):
            for hh in range(4):
                for jcp in range(8):
                    for j2 in range(2):
                        jc = 2 * jcp + j2
                        wd = []
                        if j2 == 0:
                            wd = [("ss", exp_hist[hh])]
                            if jcp == 0:
                                wd.append(("dg", jitd[quad]))
                        T(lambda e, hh=hh, jc=jc, j2=j2, buf=buf, ic=ic: e.matmul(
                            p_sc[:, hh * 512 + j2 * 256: hh * 512 + (j2 + 1) * 256],
                            kjit[32 * hh:32 * hh + 2,
                                 buf * 2048 + jc * 128: buf * 2048 + (jc + 1) * 128],
                            qjit[32 * hh:32 * hh + 2,
                                 buf * 1024 + ic * 256: buf * 1024 + (ic + 1) * 256],
                            start=True, stop=True, tile_position=(32 * hh, 0)), wd)
                    scd = E.c("st")
                    hcs = 4 * quad + hh
                    ew = [("st", scd)]
                    if jcp == 0:
                        ew.append(("st", ao_h[hh]))
                    exp_hist[hh] = SC(
                        lambda e, hh=hh, jcp=jcp, hcs=hcs: e.activation(
                            Ebuf[:, hh * 4096 + jcp * 512: hh * 4096 + (jcp + 1) * 512],
                            p_sc[:, hh * 512: hh * 512 + 512], AF.Exp,
                            bias=gcon[:, 16 + hcs:17 + hcs]),
                        ew, dep=False)
                score_hist[quad][hh] = E.c("st")
            mk0 = E.c("sv")
            for hh in range(4):
                V(lambda e, hh=hh, ic=ic: e.tensor_tensor(
                    Ebuf[:, hh * 4096:(hh + 1) * 4096],
                    Ebuf[:, hh * 4096:(hh + 1) * 4096],
                    indT.rearrange("p (jc i) -> p jc i", i=1024)
                        [:, :, ic * 256:(ic + 1) * 256],
                    OP.mult), [("ss", exp_hist[hh])], dep=False)
            mk = E.c("sv")
            for hh in range(4):
                for jc in range(DC):
                    wd = []
                    if jc == 0:
                        wd = [("sv", mk), ("sv", aoev_hist)] if hh == 0 else [("sv", mk)]
                    habs = 4 * quad + hh
                    T(lambda e, hh=hh, jc=jc, habs=habs: e.matmul(
                        p_mm[32 * hh:32 * hh + 3, 0:256],
                        vaug[:, jc * 128 + 4 * habs: jc * 128 + 4 * habs + 3],
                        Ebuf[:, hh * 4096 + jc * 256: hh * 4096 + (jc + 1) * 256],
                        start=(jc == 0), stop=(jc == DC - 1),
                        tile_position=(0, 32 * hh)), wd)
                ao_h[hh] = E.c("st")
            aod = E.c("st")
            ao_last = aod
            for hh in range(4):
                V(lambda e, hh=hh, ic=ic: e.tensor_copy(
                    scr2[32 * hh:32 * hh + 3, ic * 256:(ic + 1) * 256],
                    p_mm[32 * hh:32 * hh + 3, 0:256]),
                  [("st", aod), ("dg", ao_dg)] if (hh == 0 and ic == 0)
                  else ([("st", aod)] if hh == 0 else ()), dep=False)
            aoev_hist = E.c("sv")
        for hh in range(4):
            h = 4 * quad + hh
            dma(aot[h:h + 1, 0:1024], scr2[32 * hh:32 * hh + 1, 0:1024],
                waits=[("sv", aoev_hist)] if hh == 0 else (),
                eng="gpsimd", sem="dg", pipe=True)
            dma(aot[32 + h:32 + h + 1, 0:1024],
                scr2[32 * hh + 1:32 * hh + 2, 0:1024],
                eng="gpsimd", sem="dg", pipe=True)
            dma(aden[h:h + 1, 0:1024], scr2[32 * hh + 2:32 * hh + 3, 0:1024],
                eng="gpsimd", sem="dg", pipe=True)
        ao_dg = E.c("dg")
        aodma_hist = ao_dg
    rc = V(lambda e: e.reciprocal(aden[0:32, :], aden[0:32, :]),
           [("dg", aodma_hist)])
    w1_ = dma(denr[0:32, 0:1024], aden[0:32, :],
              waits=[("sv", rc), ("ss", passA_last_s)])
    w2_ = dma(denr[32:64, 0:1024], aden[0:32, :], waits=[("dx", w1_)])
    mulv = V(lambda e: e.tensor_tensor(aot[:, :], aot[:, :], denr[0:64, 0:1024],
                                       OP.mult), [("dx", w2_)])
    aotb_v = V(lambda e: e.tensor_copy(aotb[:], aot[:]))
    w = dma(scr2[0:64, 0:2048], da["woutaT"], waits=[("sv", aotb_v)])
    woat_v = V(lambda e: e.tensor_copy(woat, scr2[0:64, 0:2048]), [("dx", w)])
    projw = woat_v
    proj_ss = E.c("ss")
    x2_hist = [0, 0]
    for dch in range(DC):
        for n2 in range(2):
            bank = (2 * dch + n2) % 2
            wd = [("sv", projw), ("ss", proj_ss)] if dch == 0 and n2 == 0 else []
            if dch > 0 or n2 > 0:
                wd.append(("sv", x2_hist[bank]))
            T(lambda e, dch=dch, n2=n2, bank=bank: e.matmul(
                p_sc[:, bank * 512:(bank + 1) * 512],
                woat[:, dch * 128:(dch + 1) * 128],
                aotb[:, n2 * 512:(n2 + 1) * 512],
                start=True, stop=True), wd + [("sv", woat_v)] if (dch == 0 and n2 == 0) else wd)
            pj = E.c("st")
            x2_hist[bank] = V(lambda e, dch=dch, n2=n2, bank=bank:
                              e.scalar_tensor_tensor(
                AR1[:, dch * 1024 + n2 * 512: dch * 1024 + (n2 + 1) * 512],
                p_sc[:, bank * 512:(bank + 1) * 512],
                gaT[:, dch:dch + 1],
                AR2[:, dch * 1024 + n2 * 512: dch * 1024 + (n2 + 1) * 512],
                OP.mult, OP.add), [("st", pj)])
    x2_v = E.c("sv")
    for dch in range(DC):
        dma(d_x2[dch * 128:(dch + 1) * 128, :], AR1[:, dch * 1024:(dch + 1) * 1024],
            waits=[("sv", x2_v)] if dch == 0 else ())
    x2_spill = E.c("dx")

    for qi in range(2):
        rv_ = V(lambda e, qi=qi: e.reduce_sum(
            tmpk[:, 20 + qi:21 + qi], paccum[:, qi * 16:(qi + 1) * 16],
            axis=AXX), [("ss", passA_last_s)] if qi == 0 else ())
        tm = T(lambda e, qi=qi: e.matmul(
            p_sc[0:1, 1536 + qi:1537 + qi], onef[:],
            tmpk[:, 20 + qi:21 + qi], start=True, stop=True), [("sv", rv_)])
        pcp = V(lambda e, qi=qi: e.tensor_copy(sclr[0:1, 8 + qi:9 + qi],
                                               p_sc[0:1, 1536 + qi:1537 + qi]),
                [("st", tm)])
    w = dma(d_spart[:, 0:2], sclr[0:1, 8:10], waits=[("sv", E.c("sv"))])
    E.op("gpsimd", lambda e: e.collective_compute(
        "AllReduce", OP.add, replica_groups=[list(range(8))],
        ins=[d_spart.opt()], outs=[d_ssum.opt()]), inc=("scc", 1),
        waits=[("dx", w)])
    ssum_cc = E.c("scc")
    w = dma(sclr[0:1, 12:14], d_ssum[:, 0:2], waits=[("scc", ssum_cc)])
    sload = w

    def quant_scale(qi, s_col):
        V(lambda e, qi=qi: e.tensor_scalar(sclr[0:1, 4 * qi:4 * qi + 1],
                                           sclr[0:1, 12 + qi:13 + qi],
                                           1.0 / (2048.0 * 8192.0), None,
                                           OP.mult),
          [("dx", sload)] if qi == 0 else ())
        V(lambda e, qi=qi: e.tensor_scalar(sclr[0:1, 4 * qi + 1:4 * qi + 2],
                                           sclr[0:1, 4 * qi:4 * qi + 1],
                                           1e-8, None, OP.add))
        sc_v = V(lambda e, qi=qi: e.reciprocal(sclr[0:1, 4 * qi + 1:4 * qi + 2],
                                               sclr[0:1, 4 * qi + 1:4 * qi + 2]))
        w = dma(d_scal[:, 8 + 2 * qi:10 + 2 * qi], sclr[0:1, 4 * qi:4 * qi + 2],
                waits=[("sv", sc_v)])
        scb = dma(gcon[:, 48 + s_col:50 + s_col],
                  d_scal[:, 8 + 2 * qi:10 + 2 * qi].to_broadcast([128, 2]),
                  waits=[("dx", w)])
        return scb

    V(lambda e: e.memset(gcon[:, 50:51], MAGIC))
    V(lambda e: e.memset(gcon[:, 53:54], MAGIC))
    V(lambda e: e.memset(gcon[:, 54:55], -MAGIC))
    scb1 = quant_scale(0, 0)
    scb2 = quant_scale(1, 3)

    pb_a1 = [0, 0, 0, 0]
    pb_clamp = [0, 0, 0, 0]
    pb_sub = [0, 0, 0, 0]
    pb_odma = {"dg": 0, "ds": 0}
    scbs = {0: scb1, 1: scb2}
    browf = brow[:].bitcast(F32)
    ld_slots = (xraw[:, 0:1024], xraw[:, 1024:2048],
                browf[:, 0:1024], browf[:, 1024:2048])
    hidstf = hidst[:].bitcast(F32)
    sc_slots = (scr2[:, 0:1024], scr2[:, 1024:2048],
                hidstf[:, 0:1024], ytile[:, 0:1024])
    bi = 0
    for (srcq, dstq, qi, s_col) in ((da["wsynT"], d_tqsyn_sl, 0, 0),
                                    (da["woutfT"], d_tqout_sl, 1, 3)):
        for pch in range(srcq.shape[0] // 128):
            for cc2 in range(srcq.shape[1] // 1024):
                slot = bi % 4
                oslot = bi % 4
                leng, lsem = ("sync", "dx") if slot % 2 == 0 else ("scalar", "ds")
                oeng, osem = ("gpsimd", "dg")
                wl = [("ss", pb_a1[slot])]
                if bi < 4:
                    wl.append(("ss", passA_last_s))
                w = dma(ld_slots[slot],
                        srcq[pch * 128:(pch + 1) * 128,
                             cc2 * 1024:(cc2 + 1) * 1024],
                        waits=wl, eng=leng, sem=lsem, pipe=True)
                a1w = [(lsem, w), ("dx", scbs[qi]), ("sv", pb_clamp[slot])]
                pb_a1[slot] = SC(lambda e, slot=slot, s_col=s_col: e.activation(
                    sc_slots[slot], ld_slots[slot],
                    AF.Identity, bias=gcon[:, 50 + s_col:51 + s_col],
                    scale=gcon[:, 49 + s_col:50 + s_col]), a1w, dep=False)
                pb_clamp[slot] = V(lambda e, slot=slot: e.tensor_scalar(
                    sc_slots[slot], sc_slots[slot],
                    MAGIC - 1.0, MAGIC + 1.0, OP.max, OP.min),
                    [("ss", pb_a1[slot])], dep=False)
                pb_sub[oslot] = SC(lambda e, slot=slot, oslot=oslot: e.activation(
                    AR6[:, oslot * 1024:(oslot + 1) * 1024],
                    sc_slots[slot],
                    AF.Identity, bias=gcon[:, 54:55]),
                    [("sv", pb_clamp[slot]), (osem, pb_odma[osem])], dep=False)
                pb_odma[osem] = dma(
                    dstq[pch * 128:(pch + 1) * 128,
                         cc2 * 1024:(cc2 + 1) * 1024],
                    AR6[:, oslot * 1024:(oslot + 1) * 1024],
                    waits=[("ss", pb_sub[oslot])], eng=oeng, sem=osem,
                    pipe=True)
                bi += 1
    E.op("gpsimd", lambda e: e.collective_compute(
        "AllGather", OP.bypass, replica_groups=[list(range(8))],
        ins=[d_tqsyn_sl.opt()], outs=[d_tqsyn.opt()]), inc=("scc", 1),
        waits=[("dg", E.c("dg"))])
    tqsyn_cc = E.c("scc")
    E.op("gpsimd", lambda e: e.collective_compute(
        "AllGather", OP.bypass, replica_groups=[list(range(8))],
        ins=[d_tqout_sl.opt()], outs=[d_tqout.opt()]), inc=("scc", 1))
    tqout_cc = E.c("scc")
    qb_done = [("scc", tqsyn_cc)]
    bsyn_rows = [h1n[0:1, 0:2048], qks[0:1, 0:2048], vaug[0:1, 0:2048],
                 scr2b[0:1, 0:2048]]
    s1r_v = 0
    for a in range(4):
        w = dma(xraw[0:1, 0:2048], da["bsyn_row"][:, a * 2048:(a + 1) * 2048],
                waits=[("sv", E.c("sv")), ("dg", E.c("dg")), ("ss", E.c("ss"))])
        s1r_v = V(lambda e, a=a: e.tensor_copy(bsyn_rows[a], xraw[0:1, 0:2048]),
                  [("dx", w)])
    s1r_v = V(lambda e: e.tensor_copy(s1row[0:1, 0:128],
                                      gcon[0:1, 49:50].to_broadcast([1, 128])))

    mmh = [0] * DC
    for dch in range(DC):
        x2c = AR1[:, dch * 1024:(dch + 1) * 1024]
        cw = [("dx", x2_spill)] if dch == 0 else [("st", mmh[dch - 1])]
        cb_v = V(lambda e, x2c=x2c: e.tensor_copy(xbt[:, 0:1024], x2c), cw,
                 dep=False)
        sq_s = SC(lambda e: e.activation(xsq[:, 0:1024], xbt[:, 0:1024],
                                         AF.Square),
                  [("sv", cb_v)] + ([("st", mmh[dch - 1])] if dch else []))
        for n2 in range(2):
            T(lambda e, dch=dch, n2=n2: e.matmul(
                p_sc[0:1, 1024 + n2 * 512:1024 + (n2 + 1) * 512],
                onebc[:, 0:1],
                xbt[:, n2 * 512:(n2 + 1) * 512],
                start=(dch == 0), stop=(dch == DC - 1)),
              [("sv", cb_v)] if n2 == 0 else ())
            T(lambda e, dch=dch, n2=n2: e.matmul(
                p_mm[0:1, 1024 + n2 * 512:1024 + (n2 + 1) * 512],
                onebc[:, 0:1],
                xsq[:, n2 * 512:(n2 + 1) * 512],
                start=(dch == 0), stop=(dch == DC - 1)),
              [("ss", sq_s)] if n2 == 0 else ())
        mmh[dch] = E.c("st")
    stA = scr2[0:1, 1024:2048]
    stB = AR5[0:1, 0:1024]
    stC = AR5[0:1, 1024:2048]
    stMb = hidst[0:1, 0:1024]
    stRb = hidst[0:1, 1024:2048]
    ev2 = V(lambda e: e.tensor_scalar(stA, p_sc[0:1, 1024:2048], 1.0 / D, None,
                                      OP.mult), [("st", E.c("st"))])
    V(lambda e: e.tensor_scalar(stB, p_mm[0:1, 1024:2048], 1.0 / D, None,
                                OP.mult))
    V(lambda e: e.tensor_tensor(stC, stA, stA, OP.mult))
    V(lambda e: e.tensor_tensor(stB, stB, stC, OP.subtract))
    V(lambda e: e.tensor_scalar(stB, stB, LN_EPS, None, OP.add))
    rv = V(lambda e: e.reciprocal(stB, stB))
    rs_s = SC(lambda e: e.activation(stB, stB, AF.Sqrt), [("sv", rv)])
    cbb = V(lambda e: e.tensor_copy(stMb, stA), [("ss", rs_s)])
    cbb = V(lambda e: e.tensor_copy(stRb, stB))
    w = dma(d_rowsb[0:1, 0:1024], stMb, waits=[("sv", cbb)])
    w = dma(d_rowsb[1:2, 0:1024], stRb, waits=[("sv", cbb)])
    brfb = brf[:].bitcast(BF16)
    bw1 = dma(brfb[:, 0:1024], d_rowsb[0:1, 0:1024].to_broadcast([128, 1024]),
              waits=[("dx", w)])
    bw2 = dma(brfb[:, 1024:2048], d_rowsb[1:2, 0:1024].to_broadcast([128, 1024]),
              waits=[("dx", w)])
    mu2b = brfb[:, 0:1024]; rs2b = brfb[:, 1024:2048]
    xn2_hist = [0] * DC
    for dch in range(DC):
        x2c = AR1[:, dch * 1024:(dch + 1) * 1024]
        wd = [("dx", bw2)] if dch == 0 else [("ss", xn2_hist[dch - 1])]
        V(lambda e, x2c=x2c: e.tensor_tensor(xraw[:, 0:1024], x2c, mu2b,
                                             OP.subtract), wd)
        s2v = V(lambda e: e.tensor_tensor(xraw[:, 0:1024], xraw[:, 0:1024], rs2b,
                                          OP.mult))
        xn2_hist[dch] = SC(lambda e, dch=dch: e.activation(
            AR2b[:, dch * 1024:(dch + 1) * 1024], xraw[:, 0:1024], AF.Identity,
            bias=sb_t[:, dch:dch + 1], scale=A2[:, dch:dch + 1]), [("sv", s2v)])
    xn2_s = E.c("ss")
    xn2T = lambda dch: AR2b[:, dch * 1024:(dch + 1) * 1024]

    uP6 = AR2[:, 8192:16384]
    hidtm = AR2b[:, 16384:24576]
    xn2_all_v = E.c("sv")
    zmm_hist = [0, 0]
    zev_s = 0
    hidT_d = 0
    for tb in range(NTOKB):
        for nch in range(DC):
            for hs in range(2):
                buf = (2 * nch + hs) % 2
                wd = [("st", zmm_hist[buf])]
                if nch == 0 and hs == 0:
                    wd.append(("sv", E.c("sv")))
                if tb == 0 and nch == 0:
                    wd += [("ss", xn2_s), ("sv", s1r_v)] + qb_done
                sbuf_ = AR6[:, 0:4096] if buf == 0 else brow[:, 0:4096]
                ssem = "dx" if buf == 0 else "ds"
                w = dma(sbuf_.rearrange("p (c n) -> p c n", c=8),
                        d_tqsyn[hs * 1024:(hs + 1) * 1024,
                                nch * 512:(nch + 1) * 512]
                        .rearrange("(c p) n -> p c n", p=128), waits=wd,
                        sem=ssem, eng="sync" if buf == 0 else "scalar",
                        pipe=True)
                for mg in range(2):
                    for dc8 in range(8):
                        dch = 8 * hs + dc8
                        mw = []
                        if dc8 == 0:
                            mw = [(ssem, w)]
                            if hs == 0 and mg == 0:
                                mw.append(("ss", zev_s))
                        sb_ = AR6 if buf == 0 else brow
                        T(lambda e, sb_=sb_, mg=mg, dch=dch, dc8=dc8, tb=tb:
                          e.matmul(
                            p_sc[:, mg * 512:(mg + 1) * 512],
                            xn2T(dch)[:, tb * 256 + mg * 128:
                                       tb * 256 + (mg + 1) * 128],
                            sb_[:, dc8 * 512:(dc8 + 1) * 512],
                            start=(dch == 0), stop=False), mw)
                zmm_hist[buf] = E.c("st")
            for mg in range(2):
                T(lambda e, mg=mg, nch=nch: e.matmul(
                    p_sc[:, mg * 512:(mg + 1) * 512],
                    s1row[:],
                    bsyn_rows[nch // 4][:, (nch % 4) * 512:(nch % 4 + 1) * 512],
                    start=False, stop=True))
            zmm_hist[(2 * nch + 1) % 2] = E.c("st")
            zw = [("st", E.c("st"))]
            zev_s = SC(lambda e, nch=nch: e.activation(
                AR1[:, 0 * 8192 + nch * 512: 0 * 8192 + (nch + 1) * 512],
                p_sc[:, 0:512], AF.Identity, scale=gcon[:, 48:49]), zw,
                dep=False)
            zev_s = SC(lambda e, nch=nch: e.activation(
                AR1[:, 1 * 8192 + nch * 512: 1 * 8192 + (nch + 1) * 512],
                p_sc[:, 512:1024], AF.Identity, scale=gcon[:, 48:49]),
                dep=False)
        z_done_s = E.c("ss")
        for r2 in range(2):
            zt = AR1[:, r2 * 8192:(r2 + 1) * 8192]
            tk = topk(zt, NSYN, KFFN, Z0_FFN, tcol2, AR5b[:, 0:4096], uP6,
                      [("ss", z_done_s), ("st", E.c("st"))], sw=4096)
            V(lambda e: e.tensor_scalar(tcol2[:], tcol2[:], 1e-30, None, OP.max))
            hid_v = V(lambda e, zt=zt: e.scalar_tensor_tensor(
                hidtm[:, 0:8192], zt, tcol2[:, 0:1], zt, OP.is_ge, OP.mult))
            for sb4 in range(16):
                for j4 in range(4):
                    sc_ = 4 * sb4 + j4
                    T(lambda e, sc_=sc_, j4=j4: e.transpose(
                        p_bf[:, 2048 + j4 * 128:2048 + (j4 + 1) * 128],
                        hidtm[:, sc_ * 128:(sc_ + 1) * 128], idbf[:]),
                      [("sv", hid_v), ("dg", hidT_d)] if j4 == 0 else ())
                trp = E.c("st")
                hcp = V(lambda e, sb4=sb4: e.tensor_copy(
                    hidst[:, (sb4 % 2) * 1024:(sb4 % 2) * 1024 + 512],
                    p_bf[:, 2048:2560]), [("st", trp)], dep=False)
                hidT_d = dma(
                    d_hidT[:, tb * 256 + r2 * 128: tb * 256 + (r2 + 1) * 128]
                    .rearrange("(a b) t -> b a t", b=128)[:, 4 * sb4:4 * sb4 + 4, :],
                    hidst[:, (sb4 % 2) * 1024:(sb4 % 2) * 1024 + 512]
                    .rearrange("p (a t) -> p a t", a=4),
                    waits=[("sv", hcp)], eng="gpsimd", sem="dg", pipe=True)
        ffn_fwd_done = E.c("dg")

    ybuf = AR2b[:, 16384:32768]
    yv_hist = 0
    pd_hist = [0] * DC
    spill_dg = [0, 0]
    for half in range(2):
        hw = [("dg", ffn_fwd_done)] if half == 0 else [("st", E.c("st"))]
        hld = dma(AR1b[:, 0:32768].rearrange("p (c t) -> p c t", c=32),
                  d_hidT[half * 4096:(half + 1) * 4096, :]
                  .rearrange("(c p) t -> p c t", p=128), waits=hw, pipe=True)
        tqo_hist2 = [0, 0]
        wtq_hist = [0, 0]
        for ddch in range(DC):
            tbuf = ddch % 2
            tqb = AR6 if tbuf == 0 else brow
            tsem = "dx" if tbuf == 0 else "ds"
            wq = [("dx", hld)] if ddch < 2 else []
            if half == 0 and ddch == 0:
                wq.append(("scc", tqout_cc))
            if tqo_hist2[tbuf]:
                wq.append(("st", tqo_hist2[tbuf]))
            wtq = dma(tqb[:, 0:4096].rearrange("p (c m) -> p c m", c=32),
                      d_tqout[half * 4096:(half + 1) * 4096,
                              ddch * 128:(ddch + 1) * 128]
                      .rearrange("(c p) m -> p c m", p=128), waits=wq,
                      sem=tsem, eng="sync" if tbuf == 0 else "scalar",
                      pipe=True)
            wtq_hist[tbuf] = wtq
            for n2 in range(2):
                for kch in range(32):
                    mw = []
                    if kch == 0:
                        mw = [(tsem, wtq), ("sv", yv_hist)]
                    T(lambda e, ddch=ddch, n2=n2, kch=kch, tqb=tqb: e.matmul(
                        p_mm[:, n2 * 512:(n2 + 1) * 512],
                        tqb[:, kch * 128:(kch + 1) * 128],
                        AR1b[:, kch * 1024 + n2 * 512: kch * 1024 + (n2 + 1) * 512],
                        start=(kch == 0), stop=(kch == 31)), mw)
                mmd = E.c("st")
                if half == 0:
                    pv = V(lambda e, n2=n2: e.tensor_copy(
                        scr2[:, n2 * 512:(n2 + 1) * 512],
                        p_mm[:, n2 * 512:(n2 + 1) * 512]),
                        [("st", mmd), ("dg", spill_dg[n2])], dep=False)
                    yv_hist = pv
                    pd_hist[ddch] = dma(
                        d_partial[ddch * 128:(ddch + 1) * 128,
                                  n2 * 512:(n2 + 1) * 512],
                        scr2[:, n2 * 512:(n2 + 1) * 512], waits=[("sv", pv)],
                        eng="gpsimd", sem="dg", pipe=True)
                    spill_dg[n2] = pd_hist[ddch]
                else:
                    wpl = dma(ytile[:, n2 * 512:(n2 + 1) * 512],
                              d_partial[ddch * 128:(ddch + 1) * 128,
                                        n2 * 512:(n2 + 1) * 512],
                              waits=[("dg", pd_hist[ddch]), ("sv", yv_hist)],
                              pipe=True)
                    a1 = V(lambda e, n2=n2: e.tensor_tensor(
                        scr2[:, n2 * 512:(n2 + 1) * 512],
                        p_mm[:, n2 * 512:(n2 + 1) * 512],
                        ytile[:, n2 * 512:(n2 + 1) * 512], OP.add),
                        [("st", mmd), ("dx", wpl), ("dg", spill_dg[n2])],
                        dep=False)
                    a2 = SC(lambda e, ddch=ddch, n2=n2: e.activation(
                        scr2[:, n2 * 512:(n2 + 1) * 512],
                        scr2[:, n2 * 512:(n2 + 1) * 512], AF.Identity,
                        scale=gcon[:, 51:52], bias=bfbpp[:, ddch:ddch + 1]),
                        [("sv", a1)])
                    yv_hist = V(lambda e, ddch=ddch, n2=n2: e.tensor_tensor(
                        ybuf[:, ddch * 1024 + n2 * 512: ddch * 1024 + (n2 + 1) * 512],
                        scr2[:, n2 * 512:(n2 + 1) * 512],
                        xn2T(ddch)[:, n2 * 512:(n2 + 1) * 512], OP.add),
                        [("ss", a2)], dep=False)
            tqo_hist2[tbuf] = E.c("st")
    y_done_v = E.c("sv")

    ymm = [0] * DC
    for dch in range(DC):
        for n2 in range(2):
            T(lambda e, dch=dch, n2=n2: e.matmul(
                p_sc[0:1, 1024 + n2 * 512:1024 + (n2 + 1) * 512],
                onebc[:, 0:1],
                ybuf[:, dch * 1024 + n2 * 512: dch * 1024 + (n2 + 1) * 512],
                start=(dch == 0), stop=(dch == DC - 1)),
              [("sv", y_done_v)] if (dch == 0 and n2 == 0) else ())
        sqv = SC(lambda e, dch=dch: e.activation(
            xsq[:, 0:1024], ybuf[:, dch * 1024:(dch + 1) * 1024], AF.Square),
            [("sv", y_done_v)] if dch == 0 else [("st", ymm[dch - 1])])
        for n2 in range(2):
            T(lambda e, dch=dch, n2=n2: e.matmul(
                p_mm[0:1, 1024 + n2 * 512:1024 + (n2 + 1) * 512],
                onebc[:, 0:1], xsq[:, n2 * 512:(n2 + 1) * 512],
                start=(dch == 0), stop=(dch == DC - 1)),
              [("ss", sqv)] if n2 == 0 else ())
        ymm[dch] = E.c("st")
    stA2 = scr2[0:1, 0:1024]
    stB2 = scr2[0:1, 1024:2048]
    stC2 = AR5[0:1, 0:1024]
    stMb2 = hidst[0:1, 0:1024]
    stRb2 = hidst[0:1, 1024:2048]
    ev3 = V(lambda e: e.tensor_scalar(stA2, p_sc[0:1, 1024:2048], 1.0 / D, None,
                                      OP.mult), [("st", E.c("st"))])
    V(lambda e: e.tensor_scalar(stB2, p_mm[0:1, 1024:2048], 1.0 / D, None,
                                OP.mult))
    V(lambda e: e.tensor_tensor(stC2, stA2, stA2, OP.mult))
    V(lambda e: e.tensor_tensor(stB2, stB2, stC2, OP.subtract))
    V(lambda e: e.tensor_scalar(stB2, stB2, LN_EPS, None, OP.add))
    rv = V(lambda e: e.reciprocal(stB2, stB2))
    rsy_s = SC(lambda e: e.activation(stB2, stB2, AF.Sqrt), [("sv", rv)])
    cbb = V(lambda e: e.tensor_copy(stMb2, stA2), [("ss", rsy_s)])
    cbb = V(lambda e: e.tensor_copy(stRb2, stB2))
    w = dma(d_rowsb[0:1, 0:1024], stMb2, waits=[("sv", cbb)])
    w = dma(d_rowsb[1:2, 0:1024], stRb2, waits=[("sv", cbb)])
    brfb = brf[:].bitcast(BF16)
    bw1 = dma(brfb[:, 0:1024], d_rowsb[0:1, 0:1024].to_broadcast([128, 1024]),
              waits=[("dx", w)])
    bw2 = dma(brfb[:, 1024:2048], d_rowsb[1:2, 0:1024].to_broadcast([128, 1024]),
              waits=[("dx", w)])
    muyb = brfb[:, 0:1024]; rsyb = brfb[:, 1024:2048]
    x3v = 0
    outd = 0
    for dch in range(DC):
        wd = [("dx", bw2), ("sv", x3v)]
        wx2 = dma(x2l[:, 0:1024], d_x2[dch * 128:(dch + 1) * 128, :], waits=wd)
        V(lambda e, dch=dch: e.tensor_tensor(scr2[:, 0:1024],
                                             ybuf[:, dch * 1024:(dch + 1) * 1024],
                                             muyb, OP.subtract),
          [("dx", bw2), ("st", E.c("st"))])
        s3 = V(lambda e: e.tensor_tensor(scr2[:, 0:1024], scr2[:, 0:1024], rsyb,
                                         OP.mult))
        fs = SC(lambda e, dch=dch: e.activation(
            scr2[:, 1024:2048], scr2[:, 0:1024], AF.Identity,
            bias=bfpp[:, dch:dch + 1], scale=gfpp[:, dch:dch + 1]), [("sv", s3)])
        x3v = V(lambda e, dch=dch: e.scalar_tensor_tensor(
            scr2[:, 0:1024], scr2[:, 1024:2048], gbT[:, dch:dch + 1],
            x2l[:, 0:1024],
            OP.mult, OP.add), [("ss", fs), ("dx", wx2)])
        for t4 in range(2):
            tev = [("sv", x3v)] if t4 == 0 else [("sv", E.c("sv"))]
            if dch > 0 and t4 == 0:
                tev.append(("dout", outd))
            for j4 in range(4):
                tc = 4 * t4 + j4
                T(lambda e, tc=tc, j4=j4: e.transpose(
                    p_sc[:, 1024 + j4 * 128:1024 + (j4 + 1) * 128],
                    scr2[:, tc * 128:(tc + 1) * 128], idf2[:]),
                  tev if j4 == 0 else ())
            trp = E.c("st")
            V(lambda e, t4=t4: e.tensor_copy(otile[:, t4 * 512:(t4 + 1) * 512],
                                             p_sc[:, 1024:1536]), [("st", trp)],
              dep=False)
        oev = E.c("sv")
        outd = dma(out_tm[:, dch * 128:(dch + 1) * 128]
                   .rearrange("(a p) m -> p a m", p=128),
                   otile[:, 0:1024].rearrange("p (a m) -> p a m", m=128),
                   waits=[("sv", oev)], sem="dout")
    E.wait("sync", "dout", E.c("dout"))

    @block.tensor
    def _(e):
        for f in E.q["tensor"]:
            f(e)

    @block.sync
    def _(e):
        for f in E.q["sync"]:
            f(e)

    @block.vector
    def _(e):
        for f in E.q["vector"]:
            f(e)

    @block.scalar
    def _(e):
        for f in E.q["scalar"]:
            f(e)

    @block.gpsimd
    def _(e):
        for f in E.q["gpsimd"]:
            f(e)

    stack.close()
    return nc


def _host_prep(inputs, core):
    b, half = core // 2, core % 2
    qi0 = half * TI
    f32 = lambda a: np.ascontiguousarray(a, dtype=np.float32)
    bf = lambda a: np.ascontiguousarray(a).astype(ml_dtypes.bfloat16)
    x = np.asarray(inputs["L0_attn_x"], np.float32)[b]
    if half == 0:
        xp = x
    else:
        xp = np.concatenate([x[TI:], x[:TI]], axis=0)
    Wqkv = np.asarray(inputs["L0_attn_Wqkv"], np.float32)
    bqkv = np.asarray(inputs["L0_attn_bqkv"], np.float32)
    W3 = np.asarray(inputs["L0_attn_W3"], np.float32)
    w3blk = np.zeros((128, 2), np.float32)
    w3blk[0:64, 0] = W3[0]
    w3blk[64:128, 1] = W3[0]
    b2 = np.asarray(inputs["L0_attn_b2"], np.float32)
    iota16 = np.broadcast_to(np.arange(16, dtype=np.float32)[None, :], (128, 16))
    vauginit = np.zeros((128, 128), np.float32)
    for h in range(32):
        vauginit[:, 4 * h + 2] = 1.0
    ident = np.eye(128, dtype=np.float32)
    sc12 = np.ones((128, 1), np.float32)
    sc12[0:64, 0] = ISQ2
    pp = lambda v: np.ascontiguousarray(
        np.asarray(v, np.float32).reshape(DC, 128).T)
    d = {
        "x_fm": f32(xp.T),
        "w1T": f32(np.asarray(inputs["L0_attn_W1"], np.float32).T),
        "b1c": f32(np.asarray(inputs["L0_attn_b1"], np.float32)[:, None]),
        "w2T": f32(np.asarray(inputs["L0_attn_W2"], np.float32).T),
        "b2st": f32(np.concatenate([b2, b2])[:, None]),
        "w3blk": f32(w3blk),
        "b3s": f32(np.asarray(inputs["L0_attn_b3"], np.float32).reshape(1, 1)),
        "wqkT": f32(Wqkv[0:128].T),
        "bqk_row": f32(bqkv[0:128][None, :]),
        "wvT": f32(Wqkv[128:192].T),
        "bv_row": f32(bqkv[128:192][None, :]),
        "woutaT": f32(np.asarray(inputs["L0_attn_Wout"], np.float32).T[
            np.array([2 * h + c for c in (0, 1) for h in range(32)])]),
        "bout_pp": pp(inputs["L0_attn_bout"]),
        "wsynT": f32(np.asarray(inputs["L0_ffn_W_syn"], np.float32)
                     .T[core * (D // 8):(core + 1) * (D // 8)]),
        "bsyn_row": f32(np.asarray(inputs["L0_ffn_b_syn"], np.float32)[None, :]),
        "woutfT": f32(np.asarray(inputs["L0_ffn_W_out"], np.float32)
                      .T[core * (NSYN // 8):(core + 1) * (NSYN // 8)]),
        "boutf_pp": pp(inputs["L0_ffn_b_out"]),
        "gammaA_pp": pp(inputs["L0_attn_gamma"]),
        "betaA_pp": pp(inputs["L0_attn_beta"]),
        "gammaF_pp": pp(inputs["L0_ffn_gamma"]),
        "betaF_pp": pp(inputs["L0_ffn_beta"]),
        "c_pp": np.ascontiguousarray(np.asarray(inputs["L0_c"], np.float32)
            [b, half * TI:(half + 1) * TI].reshape(8, 128).T),
        "wmod": f32(np.asarray(inputs["L0_wmod"], np.float32)
            [half * TI:(half + 1) * TI]),
        "vauginit": bf(vauginit),
        "idt_bf": bf(ident),
        "idt_f32": f32(ident),
        "iota16_in": f32(iota16),
        "ones_bf_row": bf(np.ones((1, 512), np.float32)),
        "ones_f32_col": f32(np.ones((128, 1), np.float32)),
        "sc12": sc12,
    }
    return d


_NC_CACHE = None


def kernel(**inputs):
    global _NC_CACHE
    if _NC_CACHE is None:
        _NC_CACHE = build_nc()
    nc = _NC_CACHE
    in_maps = [_host_prep(inputs, c) for c in range(8)]
    res = run_bass_kernel_spmd(nc, in_maps, core_ids=list(range(8)))
    out = np.zeros((4, S, D), np.float32)
    for c in range(8):
        b, half = c // 2, c % 2
        out[b, half * TI:(half + 1) * TI, :] = res.results[c]["out_tm"]
    return out
